# version 08
# baseline (speedup 1.0000x reference)
import numpy as np
import ml_dtypes
from contextlib import ExitStack

import concourse.bass as bass
import concourse.mybir as mybir
from concourse.bass_utils import run_bass_kernel_spmd

F32 = mybir.dt.float32
BF16 = mybir.dt.bfloat16
F16 = mybir.dt.float16
I8 = mybir.dt.int8
AF = mybir.ActivationFunctionType
ALU = mybir.AluOpType
AX = mybir.AxisListType
BF = ml_dtypes.bfloat16

N, D, H, DH, KC = 4096, 2048, 8, 256, 16
C0, NCH0, TS0 = 128, 32, 1
C1, NCH1 = 512, 8
C3, NCH3 = 256, 16


def _host_constants():
    d = {}
    xs = np.linspace(0.0, 1.0, DH // 2)
    thetas = np.repeat(1.0 / (10000.0 ** xs), 2)
    t = np.arange(N, dtype=np.float64)
    ang = t[None, :] * thetas[:, None]
    cosT = np.cos(ang).reshape(2, 128, N)
    sinT = np.sin(ang).reshape(2, 128, N)
    d["cosT"] = cosT.astype(np.float32)
    d["sinT"] = sinT.astype(np.float32)
    d["cosK"] = (cosT / 16.0).astype(np.float32)
    d["sinK"] = (sinT / 16.0).astype(np.float32)
    g = 1.0 - np.exp(np.linspace(np.log(1 / 32), np.log(1 / 512), H))
    p = np.arange(128)[:, None]
    m = np.arange(C3)[None, :]
    M = np.zeros((128, H * C3))
    for h in range(H):
        M[:, h * C3:(h + 1) * C3] = np.where(m >= p, g[h] ** (m - p), 0.0)
    d["Mfull"] = M.astype(np.float32)
    MC = np.zeros((128, H * 384))
    for h in range(H):
        MC[:, h * 384:h * 384 + C3] = M[:, h * C3:(h + 1) * C3]
        MC[:, h * 384 + C3:h * 384 + 384] = M[:, h * C3:h * C3 + 128]
    d["Mcomb"] = MC.astype(np.float32)
    GF = np.zeros((128, KC * C3))
    for h in range(H):
        for dk in range(2):
            GF[:, (2 * h + dk) * C3:(2 * h + dk + 1) * C3] = (g[h] ** (np.arange(C3) + 1.0))[None, :]
    d["Gfull"] = GF.astype(np.float32)
    GM = np.zeros((128, 2 * D))
    for ts in range(2):
        e = C3 - 1 - (ts * 128 + np.arange(128))
        for h in range(H):
            GM[:, ts * D + h * DH:ts * D + (h + 1) * DH] = (g[h] ** e)[:, None]
    d["Gam"] = GM.astype(np.float32)
    d["gC"] = [float(g[h] ** C3) for h in range(H)]
    RT = np.zeros((128, 128))
    i = np.arange(64)
    RT[2 * i + 1, 2 * i] = -1.0
    RT[2 * i, 2 * i + 1] = 1.0
    d["RT"] = RT.astype(BF)
    d["idf32"] = np.eye(128, dtype=np.float32)
    d["idbf"] = np.eye(128, dtype=BF)
    d["epsln"] = np.full((128, 1), 1e-5, np.float32)
    return d


class SemC:

    def __init__(self, nc, stack, name):
        self.h = stack.enter_context(nc.semaphore(name))
        self.n = 0

    def inc(self, instr, k=1):
        instr.then_inc(self.h, k)
        self.n += k
        return self.n

    def wait(self, eng, val=None):
        eng.wait_ge(self.h, self.n if val is None else val)


def build_nc(debug=False):
    nc = bass.Bass("TRN2", target_bir_lowering=False)

    x_d = nc.dram_tensor("x", [N, D], F32, kind="ExternalInput").ap()
    w_q = nc.dram_tensor("w_q", [D, D], F32, kind="ExternalInput").ap()
    w_k = nc.dram_tensor("w_k", [D, D], F32, kind="ExternalInput").ap()
    w_v = nc.dram_tensor("w_v", [D, D], F32, kind="ExternalInput").ap()
    w_g = nc.dram_tensor("w_g", [D, D], F32, kind="ExternalInput").ap()
    w_o = nc.dram_tensor("w_o", [D, D], F32, kind="ExternalInput").ap()
    ffw = nc.dram_tensor("ffw", [D, D], I8, kind="ExternalInput").ap()
    bq_t = nc.dram_tensor("bq_t", [128, KC], F32, kind="ExternalInput").ap()
    bk_t = nc.dram_tensor("bk_t", [128, KC], F32, kind="ExternalInput").ap()
    bg_t = nc.dram_tensor("bg_t", [128, KC], F32, kind="ExternalInput").ap()
    gnw_t = nc.dram_tensor("gnw_t", [128, KC], F32, kind="ExternalInput").ap()
    gnb_t = nc.dram_tensor("gnb_t", [128, KC], F32, kind="ExternalInput").ap()
    bv_bc = nc.dram_tensor("bv_bc", [128, D], F32, kind="ExternalInput").ap()
    bo_bc = nc.dram_tensor("bo_bc", [128, D], F32, kind="ExternalInput").ap()
    ws_bc = nc.dram_tensor("ws_bc", [128, D], F32, kind="ExternalInput").ap()
    fb_bc = nc.dram_tensor("fb_bc", [128, D], F32, kind="ExternalInput").ap()
    cosT = nc.dram_tensor("cosT", [2, 128, N], F32, kind="ExternalInput").ap()
    sinT = nc.dram_tensor("sinT", [2, 128, N], F32, kind="ExternalInput").ap()
    cosK = nc.dram_tensor("cosK", [2, 128, N], F32, kind="ExternalInput").ap()
    sinK = nc.dram_tensor("sinK", [2, 128, N], F32, kind="ExternalInput").ap()
    Mcomb = nc.dram_tensor("Mcomb", [128, H * 384], F32, kind="ExternalInput").ap()
    Gfull = nc.dram_tensor("Gfull", [128, KC * C3], F32, kind="ExternalInput").ap()
    Gam = nc.dram_tensor("Gam", [128, 2 * D], F32, kind="ExternalInput").ap()
    RT_d = nc.dram_tensor("RT", [128, 128], BF16, kind="ExternalInput").ap()
    idf_d = nc.dram_tensor("idf32", [128, 128], F32, kind="ExternalInput").ap()
    idb_d = nc.dram_tensor("idbf", [128, 128], BF16, kind="ExternalInput").ap()
    eps_d = nc.dram_tensor("epsln", [128, 1], F32, kind="ExternalInput").ap()
    gC = None

    out_d = nc.dram_tensor("out", [N, D], F32, kind="ExternalOutput").ap()

    xn_d = nc.dram_tensor("xn_d", [N, D], BF16, kind="Internal").ap()
    xq_d = nc.dram_tensor("xq_d", [N, D], BF16, kind="Internal").ap()
    ffn_d = nc.dram_tensor("ffn_d", [N, D], F16, kind="Internal").ap()
    wq16_d = nc.dram_tensor("wq16_d", [D, D], BF16, kind="Internal").ap()
    wk16_d = nc.dram_tensor("wk16_d", [D, D], BF16, kind="Internal").ap()
    wv16_d = nc.dram_tensor("wv16_d", [D, D], BF16, kind="Internal").ap()
    wg16_d = nc.dram_tensor("wg16_d", [D, D], BF16, kind="Internal").ap()
    wo16_d = nc.dram_tensor("wo16_d", [D, D], BF16, kind="Internal").ap()
    q_d = nc.dram_tensor("q_d", [D, N], BF16, kind="Internal").ap()
    k_d = nc.dram_tensor("k_d", [D, N], BF16, kind="Internal").ap()
    v_d = nc.dram_tensor("v_d", [N, D], BF16, kind="Internal").ap()
    ret_d = nc.dram_tensor("ret_d", [N, D], BF16, kind="Internal").ap()
    gate_d = nc.dram_tensor("gate_d", [D, N], BF16, kind="Internal").ap()

    dbg = {}
    if debug:
        for nm, shape, dt in [("xn", [N, D], BF16), ("xq", [N, D], BF16),
                              ("ffn", [N, D], F16),
                              ("q", [D, N], BF16), ("k", [D, N], BF16),
                              ("v", [N, D], BF16), ("ret", [N, D], BF16),
                              ("gate", [D, N], BF16)]:
            dbg[nm] = nc.dram_tensor("dbg_" + nm, shape, dt, kind="ExternalOutput").ap()

    gC_list = _host_constants()["gC"]

    with ExitStack() as PS:
        sbuf = lambda name, shape, dtype: PS.enter_context(nc.sbuf_tensor(name, shape, dtype))
        psum = lambda name, shape, dtype: nc.psum_tensor(name, shape, dtype)

        idf = sbuf("idf", [128, 128], F32)
        idb = sbuf("idb", [128, 128], BF16)
        RTt = sbuf("RTt", [128, 128], BF16)
        epst = sbuf("epst", [128, 1], F32)

        block = PS.enter_context(nc.Block())
        sems = []

        def sem(name):
            s = SemC(nc, PS, name)
            sems.append(s)
            return s

        L = sem("L")
        SS = sem("SS")
        GQ = sem("GQ")

        def guard(eng):
            for s in sems:
                if s.n > 0:
                    eng.wait_ge(s.h, s.n)
        with ExitStack() as ph:
            sb0 = lambda name, shape, dtype: ph.enter_context(nc.sbuf_tensor(name, shape, dtype))
            wT = sb0("wT", [128, KC * D], BF16)
            ws_t = sb0("ws_t", [128, D], F32)
            fb_t = sb0("fb_t", [128, D], F32)
            wstg = sb0("wstg", [128, 2 * D], BF16)
            xt = sb0("xt", [128, 2 * D], F32)
            xcn = sb0("xcn", [128, 2 * D], F32)
            xq8 = sb0("xq8", [128, 2 * D], I8)
            xqb = sb0("xqb", [128, 2 * D], BF16)
            xnb = sb0("xnb", [128, 2 * D], BF16)
            ep1 = sb0("ep1", [128, 2 * D], F32)
            f16t = sb0("f16t", [128, 4 * D], F16)
            scrA = sb0("scrA", [128, 2 * D], F32)
            qTg = sb0("qTg", [128, KC * 512], BF16)
            sclg = sb0("sclg", [128, 4 * NCH0], F32)
            stat = sb0("stat", [128, 64], F32)

            def stc(q, pp):
                c0 = q * 8 + pp * 4
                return stat[:, c0:c0 + 1]

            T0 = sem("T0")
            A0 = sem("A0")
            V0 = sem("V0")
            P0s = sem("P0s")
            E1 = sem("E1")
            PF = sem("PF")
            SV0 = sem("SV0")
            SA0 = sem("SA0")
            WCV = sem("WCV")

            with ExitStack() as pro:
                ff_i8 = xq8[:, 0:D]
                ff_bf = xqb[:, 0:D]
                tpb = [pro.enter_context(nc.psum_tensor(f"tpb{i}", [128, 128], BF16)) for i in range(2)]

                def _sy(s):
                    s.dma_start(out=idf[:], in_=idf_d).then_inc(L.h, 16)
                    s.dma_start(out=idb[:], in_=idb_d).then_inc(L.h, 16)
                    s.dma_start(out=RTt[:], in_=RT_d).then_inc(L.h, 16)
                    s.dma_start(out=epst[:], in_=eps_d).then_inc(L.h, 16)
                    s.dma_start(out=ws_t[:], in_=ws_bc).then_inc(L.h, 16)
                    s.dma_start(out=fb_t[:], in_=fb_bc).then_inc(L.h, 16)
                    L.n += 6 * 16
                block.sync(_sy)

                ffw_r = ffw.rearrange("(nch p) k -> nch p k", p=128)
                for i in range(KC):
                    def _sy(s, i=i):
                        if i > 0:
                            s.wait_ge(P0s.h, i)
                        s.dma_start(out=ff_i8, in_=ffw_r[i]).then_inc(L.h, 16)
                        L.n += 16
                    block.sync(_sy)

                    def _gp(g, i=i):
                        g.wait_ge(L.h, L.n)
                        if i > 0:
                            g.wait_ge(T0.h, T0.n)
                        P0s.inc(g.tensor_copy(ff_bf, ff_i8))
                    block.gpsimd(_gp)

                    def _te(t, i=i):
                        t.wait_ge(P0s.h, P0s.n)
                        for kc in range(KC):
                            done_evacs = i * KC + kc - 2 + 1
                            if done_evacs > 0:
                                t.wait_ge(A0.h, done_evacs)
                            T0.inc(t.transpose(tpb[kc % 2][:], ff_bf[:, kc * 128:(kc + 1) * 128], idb[:]))
                    block.tensor(_te)

                    def _sc(s, i=i):
                        for kc in range(KC):
                            s.wait_ge(T0.h, i * KC + kc + 1)
                            A0.inc(s.copy(wT[:, kc * D + i * 128: kc * D + (i + 1) * 128], tpb[kc % 2][:]))
                    block.scalar(_sc)

            pro_T0, pro_P0 = T0.n, P0s.n
            ffp = [ph.enter_context(psum(f"ffp{i}", [128, D], F32)) for i in range(2)]

            wconv_jobs = []
            for w_src, w_dst in [(w_q, wq16_d), (w_k, wk16_d), (w_v, wv16_d),
                                 (w_g, wg16_d), (w_o, wo16_d)]:
                ws_r = w_src.rearrange("(kc p) m -> kc p m", p=128)
                wd_r = w_dst.rearrange("(kc p) m -> kc p m", p=128)
                for kc in range(KC):
                    wconv_jobs.append((ws_r[kc], wd_r[kc]))

            def emit_wconv(g, lo, hi):
                for j in range(lo, min(hi, len(wconv_jobs))):
                    sl2 = slice((j % 2) * D, (j % 2 + 1) * D)
                    if j >= 2:
                        g.wait_ge(WCV.h, 32 * (j - 1))
                    g.dma_start(out=wstg[:, sl2], in_=wconv_jobs[j][0]).then_inc(WCV.h, 16)
                    WCV.n += 16
                    g.wait_ge(WCV.h, WCV.n)
                    g.dma_start(out=wconv_jobs[j][1], in_=wstg[:, sl2]).then_inc(WCV.h, 16)
                    WCV.n += 16

            x_r = x_d.rearrange("(c p) d -> c p d", p=128)
            xn_store = xn_d.rearrange("(c p) d -> c p d", p=128)
            xq_store = xq_d.rearrange("(c p) d -> c p d", p=128)
            ffn_store = ffn_d.rearrange("(c p) d -> c p d", p=128)
            NG = NCH0 // 4
            NV = 4 * (NG - 1) + 9
            V0Ae, A0e, SSe, T0e, Lq = [], [], [], [], {}
            P0sq = []

            for c in range(NV):
                pp = c % 2
                sl = slice(pp * D, (pp + 1) * D)
                real = c < NCH0
                cm1, cm2 = c - 1, c - 2
                pm1, pm2 = (c - 1) % 2, (c - 2) % 2
                slm1 = slice(pm1 * D, (pm1 + 1) * D)
                slm2 = slice(pm2 * D, (pm2 + 1) * D)

                def _sy(s, c=c):
                    if real:
                        if c >= 2:
                            s.wait_ge(V0.h, V0Ae[c - 2])
                        s.dma_start(out=xt[:, sl], in_=x_r[c]).then_inc(L.h, 16)
                        L.n += 16
                    if c >= 6 and (c - 6) % 4 == 0 and (c - 6) // 4 < NG:
                        g = (c - 6) // 4
                        if g > 0:
                            s.wait_ge(T0.h, T0e[-1])
                        s.wait_ge(SS.h, SSe[4 * g + 4])
                        for kc in range(KC):
                            s.dma_start(out=qTg[:, kc * 512:(kc + 1) * 512],
                                        in_=xq_d[g * 512:(g + 1) * 512, kc * 128:(kc + 1) * 128],
                                        transpose=True).then_inc(L.h, 16)
                        L.n += 16 * KC
                        Lq[g] = L.n
                block.sync(_sy)

                def _ve(v, c=c):
                    if 0 <= cm1 < NCH0:
                        v.wait_ge(P0s.h, P0sq[cm1])
                        V0.inc(v.tensor_reduce(stc(2, pm1), scrA[:, slm1], AX.X, ALU.add))
                    if real:
                        v.wait_ge(L.h, L.n if c < 6 else L.n - (16 * KC if (c - 6) % 4 == 0 else 0))
                        if c >= 1:
                            v.wait_ge(A0.h, A0e[c - 1])
                        SV0.inc(v.tensor_reduce(stc(0, pp), xt[:, sl], AX.X, ALU.add))
                        v.wait_ge(SV0.h, SV0.n)
                        SV0.inc(v.tensor_scalar(stc(0, pp), stc(0, pp), -1.0 / D, None, ALU.mult))
                        v.wait_ge(SV0.h, SV0.n)
                        v.tensor_scalar(xcn[:, sl], xt[:, sl], stc(0, pp), None, ALU.add)
                        ins = v.tensor_reduce(stc(1, pp), xcn[:, sl], AX.X, ALU.max,
                                              apply_absolute_value=True)
                        V0.inc(ins)
                    V0Ae.append(V0.n)
                block.vector(_ve)

                def _sc(s, c=c):
                    if 0 <= cm1 < NCH0:
                        s.wait_ge(V0.h, V0Ae[c] - (1 if real else 0))
                        SA0.inc(s.activation(stc(3, pm1), stc(2, pm1), AF.Ln,
                                             bias=epst[:], scale=1.0 / D))
                        s.wait_ge(SA0.h, SA0.n)
                        A0.inc(s.activation(stc(4, pm1), stc(3, pm1), AF.Exp, bias=0.0, scale=-0.5))
                block.scalar(_sc)
                a0_rstd = A0.n

                def _v2(v, c=c):
                    if 0 <= cm1 < NCH0:
                        if cm1 == 0:
                            v.wait_ge(P0s.h, pro_P0)
                            v.wait_ge(T0.h, pro_T0)
                        v.wait_ge(A0.h, a0_rstd)
                        SV0.inc(v.tensor_scalar(sclg[:, 4 * cm1:4 * cm1 + 1], stc(1, pm1),
                                                stc(4, pm1), None, ALU.mult))
                        v.wait_ge(SV0.h, SV0.n)
                        SV0.inc(v.tensor_scalar(sclg[:, 4 * cm1:4 * cm1 + 1], sclg[:, 4 * cm1:4 * cm1 + 1],
                                                1e-7, 1.0 / 127.0, ALU.max, ALU.mult))
                        v.wait_ge(SV0.h, SV0.n)
                        SV0.inc(v.reciprocal(stc(6, pm1), sclg[:, 4 * cm1:4 * cm1 + 1]))
                        v.wait_ge(SV0.h, SV0.n)
                        SV0.inc(v.tensor_scalar(stc(7, pm1), stc(4, pm1), stc(6, pm1), None, ALU.mult))
                        v.wait_ge(SV0.h, SV0.n)
                        ins = v.tensor_scalar(xq8[:, slm1], xcn[:, slm1], stc(7, pm1), None, ALU.mult)
                        V0.inc(ins)
                    for g in range(NG):
                        if c == 4 * g + 7:
                            subs = [0, 1]
                        elif c == 4 * g + 8:
                            subs = [2, 3]
                        else:
                            continue
                        for j in subs:
                            cj = 4 * g + j
                            v.wait_ge(T0.h, pro_T0 + 4 * g + j + 1)
                            if 4 * g + j >= 2:
                                v.wait_ge(PF.h, 4 * g + j - 1)
                            ins = v.scalar_tensor_tensor(ep1[:, (j % 2) * D:(j % 2 + 1) * D],
                                                         ffp[j % 2][:], sclg[:, 4 * cj:4 * cj + 1],
                                                         ws_t[:], ALU.mult, ALU.mult)
                            E1.inc(ins)
                block.vector(_v2)

                def _s2(s, c=c):
                    if 0 <= cm1 < NCH0:
                        s.wait_ge(V0.h, V0.n)
                        if c >= 3:
                            s.wait_ge(SS.h, SSe[c - 2])
                        if cm1 == 0:
                            s.wait_ge(T0.h, pro_T0)
                        s.wait_ge(A0.h, A0.n)
                        A0.inc(s.activation(xnb[:, slm1], xcn[:, slm1], AF.Identity,
                                            bias=0.0, scale=stc(4, pm1)))
                        A0.inc(s.activation(xqb[:, slm1], xq8[:, slm1], AF.Identity,
                                            bias=0.0, scale=1.0))
                        s.wait_ge(A0.h, A0.n)
                        s.dma_start(out=xn_store[cm1], in_=xnb[:, slm1]).then_inc(SS.h, 16)
                        s.dma_start(out=xq_store[cm1], in_=xqb[:, slm1]).then_inc(SS.h, 16)
                        SS.n += 32
                    for g in range(NG):
                        if c == 4 * g + 8:
                            for j in range(4):
                                s.wait_ge(PF.h, 4 * g + j + 1)
                                s.dma_start(out=ffn_store[4 * g + j],
                                            in_=f16t[:, j * D:(j + 1) * D]).then_inc(SS.h, 16)
                                SS.n += 16
                    A0e.append(A0.n)
                    SSe.append(SS.n)
                block.scalar(_s2)

                def _gp(g2, c=c):
                    if real:
                        g2.wait_ge(V0.h, V0Ae[c])
                        if c >= 2:
                            g2.wait_ge(V0.h, V0.n)
                        ins = g2.tensor_tensor(scrA[:, sl], xcn[:, sl], xcn[:, sl], ALU.mult)
                        P0s.inc(ins)
                    P0sq.append(P0s.n)
                    emit_wconv(g2, 3 * c, 3 * (c + 1))
                    for g in range(NG):
                        if c == 4 * g + 8:
                            for j in range(4):
                                g2.wait_ge(E1.h, 4 * g + j + 1)
                                ins = g2.tensor_tensor(f16t[:, j * D:(j + 1) * D],
                                                       ep1[:, (j % 2) * D:(j % 2 + 1) * D],
                                                       fb_t[:], ALU.add)
                                PF.inc(ins)
                block.gpsimd(_gp)

                def _te(t, c=c):
                    for g in range(NG):
                        if c == 4 * g + 6:
                            subs = [0, 1]
                            t.wait_ge(L.h, Lq[g])
                        elif c == 4 * g + 7:
                            subs = [2, 3]
                        else:
                            continue
                        for j in subs:
                            prev_use = 4 * g + j - 2
                            if prev_use >= 0:
                                t.wait_ge(E1.h, prev_use + 1)
                            for nf in range(4):
                                for kc in range(KC):
                                    ins = t.matmul(ffp[j % 2][:, nf * 512:(nf + 1) * 512],
                                                   qTg[:, kc * 512 + j * 128: kc * 512 + (j + 1) * 128],
                                                   wT[:, kc * D + nf * 512: kc * D + (nf + 1) * 512],
                                                   start=(kc == 0), stop=(kc == KC - 1))
                            T0.inc(ins)
                    T0e.append(T0.n)
                block.tensor(_te)

        def proj_rope_phase(tag, w_ap, bias_ap, cos_ap, sin_ap, out_store):
            with ExitStack() as ph:
                sb1 = lambda name, shape, dtype: ph.enter_context(nc.sbuf_tensor(tag + name, shape, dtype))
                wqb = sb1("wqb", [128, KC * D], BF16)
                bqt = sb1("bqt", [128, KC], F32)
                xnTs = sb1("xnTs", [128, 2 * KC * C1], BF16)
                cosS = sb1("cosS", [128, 2 * 2 * C1], F32)
                sinS = sb1("sinS", [128, 2 * 2 * C1], F32)
                q1 = sb1("q1", [128, KC * C1], BF16)
                qh = sb1("qh", [128, 2 * KC * C1], BF16)
                rot = sb1("rot", [128, 2 * C1], BF16)
                t1b = sb1("t1b", [128, 2 * C1], BF16)
                qps = [ph.enter_context(psum(tag + f"qps{i}", [128, C1], F32)) for i in range(2)]
                rps = [ph.enter_context(psum(tag + f"rps{i}", [128, C1], F32)) for i in range(2)]

                Tp = sem(tag + "Tp")
                Tr = sem(tag + "Tr")
                Ap = sem(tag + "Ap")
                Vp = sem(tag + "Vp")
                Pp = sem(tag + "Pp")

                def _g1(g):
                    guard(g)
                    g.dma_start(out=wqb.rearrange("p (kc m) -> p kc m", kc=KC),
                                in_=w_ap.rearrange("(kc p) m -> p kc m", p=128)).then_inc(GQ.h, 16)
                    GQ.n += 16
                block.gpsimd(_g1)

                def _s1(s):
                    guard(s)
                    s.dma_start(out=bqt[:], in_=bias_ap).then_inc(SS.h, 16)
                    SS.n += 16
                block.scalar(_s1)

                tp_e, ap_e, vp_e, pp_e, l_e, ss_e = [], [], [], [], [], []
                KCC = KC * C1
                for c in range(NCH1):
                    pp_ = c % 2
                    xsl = slice(pp_ * KCC, (pp_ + 1) * KCC)
                    csl_base = pp_ * 2 * C1

                    def _sy(s, c=c):
                        if c == 0:
                            guard(s)
                        elif c >= 2:
                            s.wait_ge(Tp.h, tp_e[c - 2])
                            s.wait_ge(Pp.h, pp_e[c - 2])
                            s.wait_ge(Vp.h, vp_e[c - 2])
                        for kc in range(KC):
                            s.dma_start(out=xnTs[:, pp_ * KCC + kc * C1:(pp_ * KCC + (kc + 1) * C1)],
                                        in_=xn_d[c * C1:(c + 1) * C1, kc * 128:(kc + 1) * 128],
                                        transpose=True).then_inc(L.h, 16)
                        s.dma_start(out=cosS[:, csl_base:csl_base + 2 * C1].rearrange("p (dk t) -> p dk t", dk=2),
                                    in_=cos_ap[:, :, c * C1:(c + 1) * C1].rearrange("dk p t -> p dk t")).then_inc(L.h, 16)
                        s.dma_start(out=sinS[:, csl_base:csl_base + 2 * C1].rearrange("p (dk t) -> p dk t", dk=2),
                                    in_=sin_ap[:, :, c * C1:(c + 1) * C1].rearrange("dk p t -> p dk t")).then_inc(L.h, 16)
                        L.n += 16 * (KC + 2)
                        l_e.append(L.n)
                    block.sync(_sy)

                    tp_b, tr_b, ap_b, vp_b, pp_b = Tp.n, Tr.n, Ap.n, Vp.n, Pp.n

                    def _te(t, c=c):
                        t.wait_ge(L.h, l_e[c])
                        if c == 0:
                            t.wait_ge(GQ.h, GQ.n)
                            t.wait_ge(SS.h, SS.n)
                        else:
                            t.wait_ge(Ap.h, ap_b)
                            t.wait_ge(Vp.h, vp_b)
                        for mm in range(KC):
                            if mm >= 2:
                                t.wait_ge(Ap.h, ap_b + mm - 1)
                            for kc in range(KC):
                                ins = t.matmul(qps[mm % 2][:],
                                               wqb[:, kc * D + mm * 128: kc * D + (mm + 1) * 128],
                                               xnTs[:, pp_ * KCC + kc * C1: pp_ * KCC + (kc + 1) * C1],
                                               start=(kc == 0), stop=(kc == KC - 1))
                            Tp.inc(ins)
                            if mm >= 1:
                                r = mm - 1
                                t.wait_ge(Ap.h, ap_b + r + 1)
                                if r >= 2:
                                    t.wait_ge(Vp.h, vp_b + r - 1)
                                ins = t.matmul(rps[r % 2][:], RTt[:], q1[:, r * C1:(r + 1) * C1],
                                               start=True, stop=True)
                                Tr.inc(ins)
                        r = KC - 1
                        t.wait_ge(Ap.h, ap_b + r + 1)
                        t.wait_ge(Vp.h, vp_b + r - 1)
                        ins = t.matmul(rps[r % 2][:], RTt[:], q1[:, r * C1:(r + 1) * C1],
                                       start=True, stop=True)
                        Tr.inc(ins)
                    block.tensor(_te)

                    def _sc(s, c=c):
                        if c == 0:
                            s.wait_ge(SS.h, SS.n)
                        for mm in range(KC):
                            s.wait_ge(Tp.h, tp_b + mm + 1)
                            if c > 0 and mm == 0:
                                s.wait_ge(Pp.h, pp_b)
                            ins = s.activation(q1[:, mm * C1:(mm + 1) * C1], qps[mm % 2][:],
                                               AF.Identity, bias=bqt[:, mm:mm + 1], scale=1.0)
                            Ap.inc(ins)
                    block.scalar(_sc)

                    def _ve(v, c=c):
                        if c > 0:
                            v.wait_ge(Pp.h, pp_b)
                        for mm in range(KC):
                            v.wait_ge(Tr.h, tr_b + mm + 1)
                            if mm >= 2:
                                v.wait_ge(Pp.h, pp_b + mm - 1)
                            dk = mm % 2
                            v.tensor_tensor(rot[:, dk * C1:(dk + 1) * C1], rps[mm % 2][:],
                                            sinS[:, csl_base + dk * C1: csl_base + (dk + 1) * C1], ALU.mult)
                            ins = v.tensor_tensor(t1b[:, dk * C1:(dk + 1) * C1], q1[:, mm * C1:(mm + 1) * C1],
                                                  cosS[:, csl_base + dk * C1: csl_base + (dk + 1) * C1], ALU.mult)
                            Vp.inc(ins)
                    block.vector(_ve)

                    def _gp(g, c=c):
                        if c >= 2:
                            g.wait_ge(SS.h, ss_e[c - 2])
                        for mm in range(KC):
                            dk = mm % 2
                            g.wait_ge(Vp.h, vp_b + mm + 1)
                            ins = g.tensor_tensor(qh[:, pp_ * KCC + mm * C1: pp_ * KCC + (mm + 1) * C1],
                                                  t1b[:, dk * C1:(dk + 1) * C1],
                                                  rot[:, dk * C1:(dk + 1) * C1], ALU.add)
                            Pp.inc(ins)
                    block.gpsimd(_gp)

                    def _sc(s, c=c):
                        s.wait_ge(Pp.h, pp_b + KC)
                        s.dma_start(out=out_store[:, :, c * C1:(c + 1) * C1],
                                    in_=qh[:, xsl].rearrange("p (kc t) -> p kc t", kc=KC)).then_inc(SS.h, 16)
                        SS.n += 16
                        ss_e.append(SS.n)
                    block.scalar(_sc)

                    tp_e.append(Tp.n)
                    ap_e.append(Ap.n)
                    vp_e.append(Vp.n)
                    pp_e.append(Pp.n)

        q_store = q_d.rearrange("(kc p) t -> p kc t", p=128)
        k_store = k_d.rearrange("(kc p) t -> p kc t", p=128)
        proj_rope_phase("P1q", wq16_d, bq_t, cosT, sinT, q_store)
        proj_rope_phase("P1k", wk16_d, bk_t, cosK, sinK, k_store)

        with ExitStack() as ph:
            sb2 = lambda name, shape, dtype: ph.enter_context(nc.sbuf_tensor("P2" + name, shape, dtype))
            wvb = sb2("wvb", [128, KC * D], BF16)
            bvt = sb2("bvt", [128, D], F32)
            xnTs = sb2("xnTs", [128, 2 * KC * C1], BF16)
            vsb = sb2("vsb", [128, 2 * D], BF16)
            vps = [ph.enter_context(psum(f"vps{i}", [128, D], F32)) for i in range(2)]

            T2 = sem("T2")
            V2 = sem("V2")

            def _g1(g):
                guard(g)
                g.dma_start(out=wvb.rearrange("p (kc m) -> p kc m", kc=KC),
                            in_=wv16_d.rearrange("(kc p) m -> p kc m", p=128)).then_inc(GQ.h, 16)
                GQ.n += 16
            block.gpsimd(_g1)

            def _s1(s):
                guard(s)
                s.dma_start(out=bvt[:], in_=bv_bc).then_inc(SS.h, 16)
                SS.n += 16
            block.scalar(_s1)

            v_store = v_d.rearrange("(r p) d -> r p d", p=128)
            KCC = KC * C1
            t2_e, l2_e = [], []
            for c in range(NCH1):
                pp_ = c % 2
                def _sy(s, c=c):
                    if c == 0:
                        guard(s)
                    elif c >= 2:
                        s.wait_ge(T2.h, t2_e[c - 2])
                    for kc in range(KC):
                        s.dma_start(out=xnTs[:, pp_ * KCC + kc * C1:(pp_ * KCC + (kc + 1) * C1)],
                                    in_=xn_d[c * C1:(c + 1) * C1, kc * 128:(kc + 1) * 128],
                                    transpose=True).then_inc(L.h, 16)
                    L.n += 16 * KC
                    l2_e.append(L.n)
                block.sync(_sy)

                t2_b, v2_b, ss_b = T2.n, V2.n, SS.n

                def _te(t, c=c):
                    t.wait_ge(L.h, l2_e[c])
                    if c == 0:
                        t.wait_ge(GQ.h, GQ.n)
                        t.wait_ge(SS.h, SS.n)
                    else:
                        t.wait_ge(V2.h, v2_b)
                    for ts in range(4):
                        if ts >= 2:
                            t.wait_ge(V2.h, v2_b + ts - 1)
                        for nf in range(4):
                            for kc in range(KC):
                                ins = t.matmul(vps[ts % 2][:, nf * 512:(nf + 1) * 512],
                                               xnTs[:, pp_ * KCC + kc * C1 + ts * 128: pp_ * KCC + kc * C1 + (ts + 1) * 128],
                                               wvb[:, kc * D + nf * 512: kc * D + (nf + 1) * 512],
                                               start=(kc == 0), stop=(kc == KC - 1))
                        T2.inc(ins)
                    t2_e.append(T2.n)
                block.tensor(_te)

                def _ve(v, c=c):
                    v.wait_ge(SS.h, ss_b if c > 0 else SS.n)
                    for ts in range(4):
                        v.wait_ge(T2.h, t2_b + ts + 1)
                        if ts >= 2:
                            v.wait_ge(SS.h, ss_b + 16 * (ts - 1))
                        ins = v.scalar_tensor_tensor(vsb[:, (ts % 2) * D:(ts % 2 + 1) * D],
                                                     vps[ts % 2][:], 1.0, bvt[:], ALU.mult, ALU.add)
                        V2.inc(ins)
                block.vector(_ve)

                def _sc(s, c=c):
                    for ts in range(4):
                        s.wait_ge(V2.h, v2_b + ts + 1)
                        s.dma_start(out=v_store[c * 4 + ts],
                                    in_=vsb[:, (ts % 2) * D:(ts % 2 + 1) * D]).then_inc(SS.h, 16)
                        SS.n += 16
                block.scalar(_sc)

        with ExitStack() as ph:
            sb3 = lambda name, shape, dtype: ph.enter_context(nc.sbuf_tensor("P3" + name, shape, dtype))
            KCC3 = KC * C3
            qsb = sb3("qsb", [128, 2 * KCC3], BF16)
            ksb = sb3("ksb", [128, 2 * KCC3], BF16)
            vsb = sb3("vsb", [128, 2 * 2 * D], BF16)
            ktok = sb3("ktok", [128, 2 * D], BF16)
            ktg = sb3("ktg", [128, 2 * D], BF16)
            qg = sb3("qg", [128, KCC3], BF16)
            Mct = sb3("Mct", [128, H * 384], F32)
            Gt = sb3("Gt", [128, KCC3], F32)
            Gmt = sb3("Gmt", [128, 2 * D], F32)
            Ssb = sb3("Ssb", [128, KC * DH], F32)
            Sbf = sb3("Sbf", [128, KC * DH], BF16)
            simb = sb3("simb", [128, 2 * 384], BF16)
            retb = sb3("retb", [128, 2 * 2 * D], BF16)
            rtb = sb3("rtb", [128, 2 * 2 * D], BF16)
            scrB = sb3("scrB", [128, 2 * D], F32)
            gsum = sb3("gsum", [128, 16], F32)
            gmneg = sb3("gmneg", [128, 16], F32)
            gmsq = sb3("gmsq", [128, 16], F32)
            gsq = sb3("gsq", [128, 16], F32)
            gvar = sb3("gvar", [128, 16], F32)
            grs = sb3("grs", [128, 16], F32)

            simc = [ph.enter_context(psum(f"simc{i}", [128, 384], F32)) for i in range(2)]
            ret_ps = [[ph.enter_context(psum(f"ret{i}{j}", [128, DH], F32)) for j in range(2)]
                      for i in range(2)]
            upd_ps = [ph.enter_context(psum(f"upd{i}", [128, 2 * DH], F32)) for i in range(2)]

            T3s = sem("T3s")
            T3r = sem("T3r")
            T3u = sem("T3u")
            V3s = sem("V3s")
            V3u = sem("V3u")
            A3r = sem("A3r")
            P3p = sem("P3p")
            V3g = sem("V3g")
            A3g = sem("A3g")
            P3q = sem("P3q")
            P3g = sem("P3g")
            SV3 = sem("SV3")
            SP3 = sem("SP3")
            V3p = sem("V3p")

            def _sy(s):
                guard(s)
                s.dma_start(out=Mct[:], in_=Mcomb).then_inc(L.h, 16)
                s.dma_start(out=Gt[:], in_=Gfull).then_inc(L.h, 16)
                s.dma_start(out=Gmt[:], in_=Gam).then_inc(L.h, 16)
                L.n += 48
            block.sync(_sy)

            q_src = q_d.rearrange("(kc p) t -> p kc t", p=128)
            k_src = k_d.rearrange("(kc p) t -> p kc t", p=128)
            v_src = v_d.rearrange("(r p) d -> r p d", p=128)
            ret_store = ret_d.rearrange("(r p) d -> r p d", p=128)

            t3s_e, t3r_e, t3u_e, v3s_e, v3u_e, a3r_e = [], [], [], [], [], []
            p3p_e, l3_e, k3_e, p3g_e, ss3_e = [], [], [], [], []

            def P3_emit_gn(c):
                pp = c % 2
                rb = pp * 2 * D

                def _sq(s, c=c):
                    s.wait_ge(V3g.h, V3g.n)
                    for i in range(2):
                        ins = s.activation(scrB[:, i * D:(i + 1) * D],
                                           retb[:, rb + i * D: rb + (i + 1) * D], AF.Square)
                        A3g.inc(ins)
                block.scalar(_sq)
                a3g_b = A3g.n - 2

                def _vg(v, c=c):
                    for i in range(2):
                        v.wait_ge(A3r.h, a3r_e[c])
                        if c >= 2:
                            v.wait_ge(P3q.h, P3q.n)
                        SV3.inc(v.tensor_reduce(gsum[:, i * 8:(i + 1) * 8].rearrange("p (h o) -> p h o", o=1),
                                                retb[:, rb + i * D: rb + (i + 1) * D].rearrange("p (h d) -> p h d", h=H),
                                                AX.X, ALU.add))
                        v.wait_ge(A3g.h, a3g_b + i + 1)
                        ins = v.tensor_reduce(gsq[:, i * 8:(i + 1) * 8].rearrange("p (h o) -> p h o", o=1),
                                              scrB[:, i * D:(i + 1) * D].rearrange("p (h d) -> p h d", h=H),
                                              AX.X, ALU.add)
                        V3g.inc(ins)
                block.vector(_vg)
                v3g_b = V3g.n - 2

                def _pm(g, c=c):
                    for i in range(2):
                        g.wait_ge(V3g.h, v3g_b + i + 1)
                        SP3.inc(g.tensor_scalar(gmneg[:, i * 8:(i + 1) * 8], gsum[:, i * 8:(i + 1) * 8],
                                                -1.0 / DH, None, ALU.mult))
                        g.wait_ge(SP3.h, SP3.n)
                        SP3.inc(g.tensor_tensor(gmsq[:, i * 8:(i + 1) * 8], gmneg[:, i * 8:(i + 1) * 8],
                                                gmneg[:, i * 8:(i + 1) * 8], ALU.mult))
                        g.wait_ge(SP3.h, SP3.n)
                        SP3.inc(g.tensor_scalar(gvar[:, i * 8:(i + 1) * 8], gsq[:, i * 8:(i + 1) * 8],
                                                1.0 / DH, None, ALU.mult))
                        g.wait_ge(SP3.h, SP3.n)
                        ins = g.tensor_tensor(gvar[:, i * 8:(i + 1) * 8], gvar[:, i * 8:(i + 1) * 8],
                                              gmsq[:, i * 8:(i + 1) * 8], ALU.subtract)
                        P3q.inc(ins)
                block.gpsimd(_pm)
                p3q_b = P3q.n - 2

                def _sd(s, c=c):
                    for i in range(2):
                        s.wait_ge(P3q.h, p3q_b + i + 1)
                        SA3g = s.activation(gvar[:, i * 8:(i + 1) * 8], gvar[:, i * 8:(i + 1) * 8],
                                            AF.Ln, bias=epst[:], scale=1.0)
                        A3g.inc(SA3g)
                        s.wait_ge(A3g.h, A3g.n)
                        ins = s.activation(grs[:, i * 8:(i + 1) * 8], gvar[:, i * 8:(i + 1) * 8],
                                           AF.Exp, bias=0.0, scale=-0.5)
                        A3g.inc(ins)
                block.scalar(_sd)
                a3g_b2 = A3g.n - 4

                def _pn(g, c=c):
                    for i in range(2):
                        g.wait_ge(A3g.h, a3g_b2 + 2 * (i + 1))
                        if c >= 2:
                            g.wait_ge(SS.h, ss3_e[c - 2])
                        for h in range(H):
                            ins = g.tensor_scalar(rtb[:, rb + i * D + h * DH: rb + i * D + (h + 1) * DH],
                                                  retb[:, rb + i * D + h * DH: rb + i * D + (h + 1) * DH],
                                                  gmneg[:, i * 8 + h: i * 8 + h + 1],
                                                  grs[:, i * 8 + h: i * 8 + h + 1],
                                                  ALU.add, ALU.mult)
                            P3g.inc(ins)
                block.gpsimd(_pn)
                p3g_e.append(P3g.n)

                def _st(s, c=c):
                    for i in range(2):
                        s.wait_ge(P3g.h, p3g_e[-1] - (1 - i) * 8)
                        s.dma_start(out=ret_store[2 * c + i],
                                    in_=rtb[:, rb + i * D: rb + (i + 1) * D]).then_inc(SS.h, 16)
                        SS.n += 16
                    ss3_e.append(SS.n)
                block.scalar(_st)

            for c in range(NCH3):
                pp = c % 2
                qb = pp * KCC3
                vb = pp * 2 * D
                rb = pp * 2 * D

                def _sy(s, c=c):
                    if c == 0:
                        guard(s)
                    elif c >= 2:
                        s.wait_ge(T3s.h, t3s_e[c - 2])
                        s.wait_ge(T3r.h, t3r_e[c - 2])
                        s.wait_ge(T3u.h, t3u_e[c - 2])
                        s.wait_ge(P3p.h, p3p_e[c - 2])
                    s.dma_start(out=qsb[:, qb:qb + KCC3].rearrange("p (kc t) -> p kc t", kc=KC),
                                in_=q_src[:, :, c * C3:(c + 1) * C3]).then_inc(L.h, 16)
                    s.dma_start(out=ksb[:, qb:qb + KCC3].rearrange("p (kc t) -> p kc t", kc=KC),
                                in_=k_src[:, :, c * C3:(c + 1) * C3]).then_inc(L.h, 16)
                    s.dma_start(out=vsb[:, vb:vb + 2 * D].rearrange("p (r d) -> p r d", r=2),
                                in_=v_src[2 * c: 2 * c + 2].rearrange("r p d -> p r d")).then_inc(L.h, 16)
                    L.n += 48
                    l3_e.append(L.n)
                block.sync(_sy)

                def _sck(s, c=c):
                    if c == 0:
                        guard(s)
                    else:
                        s.wait_ge(P3p.h, p3p_e[c - 1])
                    for ts in range(2):
                        s.dma_start(out=ktok[:, ts * D:(ts + 1) * D],
                                    in_=k_d[:, c * C3 + ts * 128: c * C3 + (ts + 1) * 128],
                                    transpose=True).then_inc(SS.h, 16)
                    SS.n += 32
                    k3_e.append(SS.n)
                block.scalar(_sck)

                p3_b = P3p.n

                def _gp(g, c=c):
                    g.wait_ge(SS.h, k3_e[c])
                    if c > 0:
                        g.wait_ge(T3u.h, t3u_e[c - 1])
                    P3p.inc(g.tensor_tensor(ktg[:], ktok[:], Gmt[:], ALU.mult))
                    p3p_e.append(P3p.n)
                block.gpsimd(_gp)

                t3s_b, t3r_b, t3u_b = T3s.n, T3r.n, T3u.n
                v3s_b, v3u_b, a3r_b = V3s.n, V3u.n, A3r.n
                v3p_b = V3p.n

                def _te(t, c=c):
                    t.wait_ge(L.h, l3_e[c])
                    for h in range(H):
                        hp = h % 2
                        if h > 1:
                            t.wait_ge(V3s.h, v3s_b + h - 1)
                        elif c > 0:
                            t.wait_ge(V3s.h, v3s_b)
                        for dk in range(2):
                            blk = 2 * h + dk
                            t.matmul(simc[hp][:, 0:C3],
                                     ksb[:, qb + blk * C3: qb + blk * C3 + 128],
                                     qsb[:, qb + blk * C3: qb + blk * C3 + C3],
                                     start=(dk == 0), stop=(dk == 1))
                        for dk in range(2):
                            blk = 2 * h + dk
                            ins = t.matmul(simc[hp][:, 256:384],
                                           ksb[:, qb + blk * C3 + 128: qb + blk * C3 + 256],
                                           qsb[:, qb + blk * C3 + 128: qb + blk * C3 + 256],
                                           start=(dk == 0), stop=(dk == 1))
                        T3s.inc(ins)
                        if h > 1:
                            t.wait_ge(A3r.h, a3r_b + 2 * (h - 1))
                        elif c > 0:
                            t.wait_ge(A3r.h, a3r_b)
                        if c > 0 and h == 0:
                            t.wait_ge(V3p.h, v3p_b + 2)
                        t.wait_ge(V3s.h, v3s_b + h + 1)
                        for i in range(2):
                            ops = [(simb[:, hp * 384 + i * 128: hp * 384 + (i + 1) * 128],
                                    vsb[:, vb + 0 * D + h * DH: vb + 0 * D + (h + 1) * DH])]
                            if i == 1:
                                ops.append((simb[:, hp * 384 + 2 * 128: hp * 384 + 3 * 128],
                                            vsb[:, vb + 1 * D + h * DH: vb + 1 * D + (h + 1) * DH]))
                            if c > 0:
                                for dk in range(2):
                                    blk = 2 * h + dk
                                    ops.append((qg[:, blk * C3 + i * 128: blk * C3 + (i + 1) * 128],
                                                Sbf[:, blk * DH:(blk + 1) * DH]))
                            for oi, (lh, rh) in enumerate(ops):
                                ins = t.matmul(ret_ps[i][hp][:], lh, rh,
                                               start=(oi == 0), stop=(oi == len(ops) - 1))
                            T3r.inc(ins)
                    t.wait_ge(P3p.h, p3_b + 1)
                    for h in range(H):
                        hp = h % 2
                        if h > 1:
                            t.wait_ge(V3u.h, v3u_b + h - 1)
                        elif c > 0:
                            t.wait_ge(V3u.h, v3u_b)
                        for dk in range(2):
                            for ts in range(2):
                                ins = t.matmul(upd_ps[hp][:, dk * DH:(dk + 1) * DH],
                                               ktg[:, ts * D + h * DH + dk * 128: ts * D + h * DH + (dk + 1) * 128],
                                               vsb[:, vb + ts * D + h * DH: vb + ts * D + (h + 1) * DH],
                                               start=(ts == 0), stop=(ts == 1))
                        T3u.inc(ins)
                    t3s_e.append(T3s.n)
                    t3r_e.append(T3r.n)
                    t3u_e.append(T3u.n)
                block.tensor(_te)

                def _ve(v, c=c):
                    v.wait_ge(L.h, l3_e[c])
                    if c > 0:
                        v.wait_ge(V3u.h, v3u_e[c - 1])
                        v.wait_ge(T3r.h, t3r_e[c - 1])
                        V3p.inc(v.tensor_copy(Sbf[:], Ssb[:]))
                        V3p.inc(v.tensor_tensor(qg[:], qsb[:, qb:qb + KCC3], Gt[:], ALU.mult))
                    for h in range(H):
                        hp = h % 2
                        v.wait_ge(T3s.h, t3s_b + h + 1)
                        if h > 1:
                            v.wait_ge(T3r.h, t3r_b + 2 * (h - 1))
                        elif c > 0:
                            v.wait_ge(T3r.h, t3r_b)
                        ins = v.tensor_tensor(simb[:, hp * 384:(hp + 1) * 384], simc[hp][:],
                                              Mct[:, h * 384:(h + 1) * 384], ALU.mult)
                        V3s.inc(ins)
                    for h in range(H):
                        hp = h % 2
                        v.wait_ge(T3u.h, t3u_b + h + 1)
                        if c == 0:
                            ins = v.tensor_copy(Ssb[:, 2 * h * DH:(2 * h + 2) * DH], upd_ps[hp][:])
                        else:
                            ins = v.scalar_tensor_tensor(Ssb[:, 2 * h * DH:(2 * h + 2) * DH],
                                                         Ssb[:, 2 * h * DH:(2 * h + 2) * DH],
                                                         gC_list[h], upd_ps[hp][:],
                                                         ALU.mult, ALU.add)
                        V3u.inc(ins)
                    v3s_e.append(V3s.n)
                    v3u_e.append(V3u.n)
                block.vector(_ve)

                def _sc(s, c=c):
                    for h in range(H):
                        hp = h % 2
                        for i in range(2):
                            s.wait_ge(T3r.h, t3r_b + 2 * h + i + 1)
                            if c >= 2 and h == 0 and i == 0:
                                s.wait_ge(P3g.h, p3g_e[c - 2])
                            A3r.inc(s.copy(retb[:, rb + i * D + h * DH: rb + i * D + (h + 1) * DH],
                                           ret_ps[i][hp][:]))
                    a3r_e.append(A3r.n)
                block.scalar(_sc)

                if c >= 1:
                    P3_emit_gn(c - 1)

            P3_emit_gn(NCH3 - 1)

        with ExitStack() as ph:
            sb4 = lambda name, shape, dtype: ph.enter_context(nc.sbuf_tensor("P4g" + name, shape, dtype))
            wgb = sb4("wgb", [128, KC * D], BF16)
            bgt = sb4("bgt", [128, KC], F32)
            bgtn = sb4("bgtn", [128, KC], F32)
            g1b = sb4("g1b", [128, 2 * C1], BF16)
            teb = sb4("teb", [128, 2 * C1], F32)
            srb = sb4("srb", [128, 2 * C1], F32)
            gnwt = sb4("gnwt", [128, KC], F32)
            gnbt = sb4("gnbt", [128, KC], F32)
            xnTs = sb4("xnTs", [128, 2 * KC * C1], BF16)
            rtf = sb4("rtf", [128, 2 * KC * C1], BF16)
            rta = sb4("rta", [128, 2 * C1], BF16)
            gateb = sb4("gateb", [128, 2 * KC * C1], BF16)
            gps_ = [ph.enter_context(psum(f"gps{i}", [128, C1], F32)) for i in range(2)]

            T4 = sem("T4")
            A4 = sem("A4")
            V4 = sem("V4")
            V4s = sem("V4s")
            P4 = sem("P4")
            SV4 = sem("SV4")

            def _g1(g):
                guard(g)
                g.dma_start(out=wgb.rearrange("p (kc m) -> p kc m", kc=KC),
                            in_=wg16_d.rearrange("(kc p) m -> p kc m", p=128)).then_inc(GQ.h, 16)
                GQ.n += 16
            block.gpsimd(_g1)

            def _s1(s):
                guard(s)
                s.dma_start(out=bgt[:], in_=bg_t).then_inc(SS.h, 16)
                s.dma_start(out=gnwt[:], in_=gnw_t).then_inc(SS.h, 16)
                s.dma_start(out=gnbt[:], in_=gnb_t).then_inc(SS.h, 16)
                SS.n += 48
            block.scalar(_s1)

            def _v1(v):
                guard(v)
                v.wait_ge(SS.h, SS.n)
                SV4.inc(v.tensor_scalar(bgtn[:], bgt[:], -1.0, None, ALU.mult))
                v.wait_ge(SV4.h, SV4.n)
            block.vector(_v1)

            gate_store = gate_d.rearrange("(kc p) t -> p kc t", p=128)
            KCC = KC * C1
            t4_e, l4_e, r4_e, v4_e, p4_e, ss4_e = [], [], [], [], [], []
            for c in range(NCH1):
                pp_ = c % 2
                xsl = slice(pp_ * KCC, (pp_ + 1) * KCC)
                def _sck(s, c=c):
                    if c == 0:
                        guard(s)
                    else:
                        if c >= 2:
                            s.wait_ge(T4.h, t4_e[c - 2])
                            s.wait_ge(V4s.h, v4_e[c - 2])
                    for kc in range(KC):
                        s.dma_start(out=xnTs[:, pp_ * KCC + kc * C1:(pp_ * KCC + (kc + 1) * C1)],
                                    in_=xn_d[c * C1:(c + 1) * C1, kc * 128:(kc + 1) * 128],
                                    transpose=True).then_inc(L.h, 16)
                    L.n += 16 * KC
                    l4_e.append(L.n)
                    for mm in range(KC):
                        s.dma_start(out=rtf[:, pp_ * KCC + mm * C1: pp_ * KCC + (mm + 1) * C1],
                                    in_=ret_d[c * C1:(c + 1) * C1, mm * 128:(mm + 1) * 128],
                                    transpose=True).then_inc(SS.h, 16)
                    SS.n += 16 * KC
                    r4_e.append(SS.n)
                block.scalar(_sck)

                t4_b, a4_b, v4_b, p4_b, v4s_b = T4.n, A4.n, V4.n, P4.n, V4s.n

                def _te(t, c=c):
                    t.wait_ge(L.h, l4_e[c])
                    if c == 0:
                        t.wait_ge(GQ.h, GQ.n)
                        t.wait_ge(SS.h, SS.n)
                    else:
                        t.wait_ge(A4.h, a4_b)
                    for mm in range(KC):
                        if mm >= 2:
                            t.wait_ge(A4.h, a4_b + mm - 1)
                        for kc in range(KC):
                            ins = t.matmul(gps_[mm % 2][:],
                                           wgb[:, kc * D + mm * 128: kc * D + (mm + 1) * 128],
                                           xnTs[:, pp_ * KCC + kc * C1: pp_ * KCC + (kc + 1) * C1],
                                           start=(kc == 0), stop=(kc == KC - 1))
                        T4.inc(ins)
                    t4_e.append(T4.n)
                block.tensor(_te)

                def _sc(s, c=c):
                    if c == 0:
                        s.wait_ge(SS.h, SS.n)
                    else:
                        s.wait_ge(P4.h, p4_b)
                    for mm in range(KC):
                        pm = mm % 2
                        s.wait_ge(T4.h, t4_b + mm + 1)
                        if mm >= 2:
                            s.wait_ge(P4.h, p4_b + mm - 1)
                            s.wait_ge(V4s.h, v4s_b + mm - 1)
                        s.activation(g1b[:, pm * C1:(pm + 1) * C1], gps_[mm % 2][:],
                                     AF.Identity, bias=bgt[:, mm:mm + 1], scale=1.0)
                        ins = s.activation(teb[:, pm * C1:(pm + 1) * C1],
                                           g1b[:, pm * C1:(pm + 1) * C1],
                                           AF.Exp, bias=0.0, scale=-1.0)
                        A4.inc(ins)
                block.scalar(_sc)

                def _ve(v, c=c):
                    v.wait_ge(SS.h, r4_e[c])
                    if c > 0:
                        v.wait_ge(P4.h, p4_b)
                    for mm in range(KC):
                        pm = mm % 2
                        if mm >= 2:
                            v.wait_ge(P4.h, p4_b + mm - 1)
                        v.tensor_scalar(rta[:, pm * C1:(pm + 1) * C1],
                                        rtf[:, pp_ * KCC + mm * C1: pp_ * KCC + (mm + 1) * C1],
                                        gnwt[:, mm:mm + 1], gnbt[:, mm:mm + 1],
                                        ALU.mult, ALU.add)
                        v.wait_ge(A4.h, a4_b + mm + 1)
                        v.tensor_scalar(teb[:, pm * C1:(pm + 1) * C1],
                                        teb[:, pm * C1:(pm + 1) * C1], 1.0, None, ALU.add)
                        ins = v.reciprocal(srb[:, pm * C1:(pm + 1) * C1],
                                           teb[:, pm * C1:(pm + 1) * C1])
                        V4s.inc(ins)
                    v4_e.append(V4s.n)
                block.vector(_ve)

                def _gp(g, c=c):
                    if c >= 2:
                        g.wait_ge(SS.h, ss4_e[c - 2])
                    for mm in range(KC):
                        pm = mm % 2
                        g.wait_ge(V4s.h, v4s_b + mm + 1)
                        g.tensor_tensor(gateb[:, pp_ * KCC + mm * C1: pp_ * KCC + (mm + 1) * C1],
                                        g1b[:, pm * C1:(pm + 1) * C1],
                                        srb[:, pm * C1:(pm + 1) * C1], ALU.mult)
                        ins = g.tensor_tensor(gateb[:, pp_ * KCC + mm * C1: pp_ * KCC + (mm + 1) * C1],
                                              gateb[:, pp_ * KCC + mm * C1: pp_ * KCC + (mm + 1) * C1],
                                              rta[:, pm * C1:(pm + 1) * C1], ALU.mult)
                        P4.inc(ins)
                block.gpsimd(_gp)


                def _sc(s, c=c):
                    s.wait_ge(P4.h, p4_b + KC)
                    s.dma_start(out=gate_store[:, :, c * C1:(c + 1) * C1],
                                in_=gateb[:, xsl].rearrange("p (kc t) -> p kc t", kc=KC)).then_inc(SS.h, 16)
                    SS.n += 16
                    ss4_e.append(SS.n)
                block.scalar(_sc)

        with ExitStack() as ph:
            sb5 = lambda name, shape, dtype: ph.enter_context(nc.sbuf_tensor("P4o" + name, shape, dtype))
            wob = sb5("wob", [128, KC * D], BF16)
            bot = sb5("bot", [128, D], F32)
            gateb = sb5("gateb", [128, 2 * KC * C1], BF16)
            xb = sb5("xb", [128, 2 * D], F32)
            fnb = sb5("fnb", [128, 2 * D], F16)
            otmp = sb5("otmp", [128, 2 * D], F32)
            otm2 = sb5("otm2", [128, 2 * D], F32)
            osb = sb5("osb", [128, 2 * D], F32)
            ops_ = [ph.enter_context(psum(f"ops{i}", [128, D], F32)) for i in range(2)]

            T5 = sem("T5")
            V5 = sem("V5")
            V5b = sem("V5b")
            P5 = sem("P5")
            L5 = sem("L5")
            SO = sem("SO")

            def _g1(g):
                guard(g)
                g.dma_start(out=wob.rearrange("p (kc m) -> p kc m", kc=KC),
                            in_=wo16_d.rearrange("(kc p) m -> p kc m", p=128)).then_inc(GQ.h, 16)
                GQ.n += 16
            block.gpsimd(_g1)

            def _s1(s):
                guard(s)
                s.dma_start(out=bot[:], in_=bo_bc).then_inc(SS.h, 16)
                SS.n += 16
            block.scalar(_s1)

            gate_src = gate_d.rearrange("(kc p) t -> p kc t", p=128)
            x_src = x_d.rearrange("(r p) d -> r p d", p=128)
            f_src = ffn_d.rearrange("(r p) d -> r p d", p=128)
            out_store = out_d.rearrange("(r p) d -> r p d", p=128)
            KCC = KC * C1
            t5_e, v5_e, l5_e = [], [], []
            for c in range(NCH1):
                pp_ = c % 2
                xsl = slice(pp_ * KCC, (pp_ + 1) * KCC)
                def _sy(s, c=c):
                    if c == 0:
                        guard(s)
                    elif c >= 2:
                        s.wait_ge(T5.h, t5_e[c - 2])
                    s.dma_start(out=gateb[:, xsl].rearrange("p (kc t) -> p kc t", kc=KC),
                                in_=gate_src[:, :, c * C1:(c + 1) * C1]).then_inc(L.h, 16)
                    L.n += 16
                    l5_e.append(L.n)
                    for ts in range(4):
                        tsp = ts % 2
                        gi = 4 * c + ts
                        if gi >= 2:
                            s.wait_ge(V5.h, max(0, gi - 1))
                            s.wait_ge(V5b.h, max(0, gi - 1))
                        s.dma_start(out=xb[:, tsp * D:(tsp + 1) * D], in_=x_src[gi]).then_inc(L5.h, 16)
                        s.dma_start(out=fnb[:, tsp * D:(tsp + 1) * D], in_=f_src[gi]).then_inc(L5.h, 16)
                        L5.n += 32
                block.sync(_sy)

                t5_b, v5_b, v5b_b, p5_b, so_b = T5.n, V5.n, V5b.n, P5.n, SO.n

                def _te(t, c=c):
                    t.wait_ge(L.h, l5_e[c])
                    if c == 0:
                        t.wait_ge(GQ.h, GQ.n)
                        t.wait_ge(SS.h, SS.n)
                    else:
                        t.wait_ge(V5.h, v5_b)
                    for ts in range(4):
                        if ts >= 2:
                            t.wait_ge(V5.h, v5_b + ts - 1)
                        for nf in range(4):
                            for kc in range(KC):
                                ins = t.matmul(ops_[ts % 2][:, nf * 512:(nf + 1) * 512],
                                               gateb[:, pp_ * KCC + kc * C1 + ts * 128: pp_ * KCC + kc * C1 + (ts + 1) * 128],
                                               wob[:, kc * D + nf * 512: kc * D + (nf + 1) * 512],
                                               start=(kc == 0), stop=(kc == KC - 1))
                        T5.inc(ins)
                    t5_e.append(T5.n)
                block.tensor(_te)

                def _ve(v, c=c):
                    if c > 0:
                        v.wait_ge(P5.h, p5_b)
                    for ts in range(4):
                        tsp = ts % 2
                        v.wait_ge(T5.h, t5_b + ts + 1)
                        v.wait_ge(L5.h, 32 * (4 * c + ts) + 32)
                        if ts >= 2:
                            v.wait_ge(P5.h, p5_b + ts - 1)
                        ins = v.scalar_tensor_tensor(otmp[:, tsp * D:(tsp + 1) * D],
                                                     ops_[ts % 2][:], 1.0, xb[:, tsp * D:(tsp + 1) * D],
                                                     ALU.mult, ALU.add)
                        V5.inc(ins)
                        ins = v.tensor_tensor(otm2[:, tsp * D:(tsp + 1) * D],
                                              otmp[:, tsp * D:(tsp + 1) * D],
                                              fnb[:, tsp * D:(tsp + 1) * D], ALU.add)
                        V5b.inc(ins)
                block.vector(_ve)

                def _gp(g, c=c):
                    if c == 0:
                        g.wait_ge(SS.h, SS.n)
                    else:
                        g.wait_ge(SO.h, so_b)
                    for ts in range(4):
                        tsp = ts % 2
                        g.wait_ge(V5b.h, v5b_b + ts + 1)
                        if ts >= 2:
                            g.wait_ge(SO.h, so_b + 16 * (ts - 1))
                        ins = g.tensor_tensor(osb[:, tsp * D:(tsp + 1) * D],
                                              otm2[:, tsp * D:(tsp + 1) * D], bot[:], ALU.add)
                        P5.inc(ins)
                block.gpsimd(_gp)

                def _sc(s, c=c):
                    for ts in range(4):
                        s.wait_ge(P5.h, p5_b + ts + 1)
                        s.dma_start(out=out_store[4 * c + ts],
                                    in_=osb[:, (ts % 2) * D:(ts % 2 + 1) * D]).then_inc(SO.h, 16)
                        SO.n += 16
                block.scalar(_sc)

            if debug:
                DBG = sem("DBG")

                def _dbg(s):
                    guard(s)
                    for nm, srcap in [("xn", xn_d), ("xq", xq_d), ("ffn", ffn_d), ("q", q_d),
                                      ("k", k_d), ("v", v_d), ("ret", ret_d), ("gate", gate_d)]:
                        s.dma_start(out=dbg[nm], in_=srcap).then_inc(DBG.h, 16)
                        DBG.n += 16
                    s.wait_ge(DBG.h, DBG.n)
                block.sync(_dbg)

            def _fin(s):
                s.wait_ge(SO.h, SO.n)
            block.scalar(_fin)

    return nc


def kernel(L0_attn_x, L0_attn_w_q, L0_attn_b_q, L0_attn_w_k, L0_attn_b_k,
           L0_attn_w_v, L0_attn_b_v, L0_attn_w_g, L0_attn_b_g,
           L0_attn_w_o, L0_attn_b_o, L0_attn_gn_weight, L0_attn_gn_bias,
           L0_ffn_weight, L0_ffn_w_scale, L0_ffn_bias):
    x = np.ascontiguousarray(np.asarray(L0_attn_x, dtype=np.float32))
    consts = _host_constants()

    def col16(v):
        return np.ascontiguousarray(np.asarray(v, np.float32).reshape(KC, 128).T)

    def bc(v):
        return np.ascontiguousarray(np.broadcast_to(np.asarray(v, np.float32).reshape(1, D), (128, D)))

    shared = {
        "w_q": np.ascontiguousarray(np.asarray(L0_attn_w_q, np.float32)),
        "w_k": np.ascontiguousarray(np.asarray(L0_attn_w_k, np.float32)),
        "w_v": np.ascontiguousarray(np.asarray(L0_attn_w_v, np.float32)),
        "w_g": np.ascontiguousarray(np.asarray(L0_attn_w_g, np.float32)),
        "w_o": np.ascontiguousarray(np.asarray(L0_attn_w_o, np.float32)),
        "ffw": np.ascontiguousarray(np.asarray(L0_ffn_weight, np.int8)),
        "bq_t": col16(L0_attn_b_q), "bk_t": col16(L0_attn_b_k), "bg_t": col16(L0_attn_b_g),
        "gnw_t": col16(L0_attn_gn_weight), "gnb_t": col16(L0_attn_gn_bias),
        "bv_bc": bc(L0_attn_b_v), "bo_bc": bc(L0_attn_b_o),
        "ws_bc": bc(np.asarray(L0_ffn_w_scale, np.float32)[:, 0]),
        "fb_bc": bc(L0_ffn_bias),
        "cosT": consts["cosT"], "sinT": consts["sinT"],
        "cosK": consts["cosK"], "sinK": consts["sinK"],
        "Mcomb": consts["Mcomb"], "Gfull": consts["Gfull"], "Gam": consts["Gam"],
        "RT": consts["RT"], "idf32": consts["idf32"], "idbf": consts["idbf"],
        "epsln": consts["epsln"],
    }
    nc = build_nc()
    in_maps = [dict(shared, x=np.ascontiguousarray(x[b])) for b in range(8)]
    res = run_bass_kernel_spmd(nc, in_maps, core_ids=list(range(8)))
    out = np.stack([np.asarray(res.results[b]["out"]) for b in range(8)], axis=0)
    return out


# version 09
# speedup vs baseline: 1.0289x; 1.0289x over previous
import numpy as np
import ml_dtypes
from contextlib import ExitStack

import concourse.bass as bass
import concourse.mybir as mybir
from concourse.bass_utils import run_bass_kernel_spmd

F32 = mybir.dt.float32
BF16 = mybir.dt.bfloat16
F16 = mybir.dt.float16
I8 = mybir.dt.int8
AF = mybir.ActivationFunctionType
ALU = mybir.AluOpType
AX = mybir.AxisListType
BF = ml_dtypes.bfloat16

N, D, H, DH, KC = 4096, 2048, 8, 256, 16
C0, NCH0, TS0 = 128, 32, 1
C1, NCH1 = 512, 8
C3, NCH3 = 256, 16


def _host_constants():
    d = {}
    xs = np.linspace(0.0, 1.0, DH // 2)
    thetas = np.repeat(1.0 / (10000.0 ** xs), 2)
    t = np.arange(N, dtype=np.float64)
    ang = t[None, :] * thetas[:, None]
    cosT = np.cos(ang).reshape(2, 128, N)
    sinT = np.sin(ang).reshape(2, 128, N)
    d["cosT"] = cosT.astype(np.float32)
    d["sinT"] = sinT.astype(np.float32)
    d["cosK"] = (cosT / 16.0).astype(np.float32)
    d["sinK"] = (sinT / 16.0).astype(np.float32)
    g = 1.0 - np.exp(np.linspace(np.log(1 / 32), np.log(1 / 512), H))
    p = np.arange(128)[:, None]
    m = np.arange(C3)[None, :]
    M = np.zeros((128, H * C3))
    for h in range(H):
        M[:, h * C3:(h + 1) * C3] = np.where(m >= p, g[h] ** (m - p), 0.0)
    d["Mfull"] = M.astype(np.float32)
    MC = np.zeros((128, H * 384))
    for h in range(H):
        MC[:, h * 384:h * 384 + C3] = M[:, h * C3:(h + 1) * C3]
        MC[:, h * 384 + C3:h * 384 + 384] = M[:, h * C3:h * C3 + 128]
    d["Mcomb"] = MC.astype(np.float32)
    GF = np.zeros((128, KC * C3))
    for h in range(H):
        for dk in range(2):
            GF[:, (2 * h + dk) * C3:(2 * h + dk + 1) * C3] = (g[h] ** (np.arange(C3) + 1.0))[None, :]
    d["Gfull"] = GF.astype(np.float32)
    GM = np.zeros((128, 2 * D))
    for ts in range(2):
        e = C3 - 1 - (ts * 128 + np.arange(128))
        for h in range(H):
            GM[:, ts * D + h * DH:ts * D + (h + 1) * DH] = (g[h] ** e)[:, None]
    d["Gam"] = GM.astype(np.float32)
    d["gC"] = [float(g[h] ** C3) for h in range(H)]
    RT = np.zeros((128, 128))
    i = np.arange(64)
    RT[2 * i + 1, 2 * i] = -1.0
    RT[2 * i, 2 * i + 1] = 1.0
    d["RT"] = RT.astype(BF)
    d["idf32"] = np.eye(128, dtype=np.float32)
    d["idbf"] = np.eye(128, dtype=BF)
    d["epsln"] = np.full((128, 1), 1e-5, np.float32)
    return d


class SemC:

    def __init__(self, nc, stack, name):
        self.h = stack.enter_context(nc.semaphore(name))
        self.n = 0

    def inc(self, instr, k=1):
        instr.then_inc(self.h, k)
        self.n += k
        return self.n

    def wait(self, eng, val=None):
        eng.wait_ge(self.h, self.n if val is None else val)


def build_nc(debug=False):
    nc = bass.Bass("TRN2", target_bir_lowering=False)

    x_d = nc.dram_tensor("x", [N, D], F32, kind="ExternalInput").ap()
    w_q = nc.dram_tensor("w_q", [D, D], F32, kind="ExternalInput").ap()
    w_k = nc.dram_tensor("w_k", [D, D], F32, kind="ExternalInput").ap()
    w_v = nc.dram_tensor("w_v", [D, D], F32, kind="ExternalInput").ap()
    w_g = nc.dram_tensor("w_g", [D, D], F32, kind="ExternalInput").ap()
    w_o = nc.dram_tensor("w_o", [D, D], F32, kind="ExternalInput").ap()
    ffw = nc.dram_tensor("ffw", [D, D], I8, kind="ExternalInput").ap()
    bq_t = nc.dram_tensor("bq_t", [128, KC], F32, kind="ExternalInput").ap()
    bk_t = nc.dram_tensor("bk_t", [128, KC], F32, kind="ExternalInput").ap()
    bg_t = nc.dram_tensor("bg_t", [128, KC], F32, kind="ExternalInput").ap()
    gnw_t = nc.dram_tensor("gnw_t", [128, KC], F32, kind="ExternalInput").ap()
    gnb_t = nc.dram_tensor("gnb_t", [128, KC], F32, kind="ExternalInput").ap()
    bv_bc = nc.dram_tensor("bv_bc", [128, D], F32, kind="ExternalInput").ap()
    bo_bc = nc.dram_tensor("bo_bc", [128, D], F32, kind="ExternalInput").ap()
    ws_bc = nc.dram_tensor("ws_bc", [128, D], F32, kind="ExternalInput").ap()
    fb_bc = nc.dram_tensor("fb_bc", [128, D], F32, kind="ExternalInput").ap()
    cosT = nc.dram_tensor("cosT", [2, 128, N], F32, kind="ExternalInput").ap()
    sinT = nc.dram_tensor("sinT", [2, 128, N], F32, kind="ExternalInput").ap()
    cosK = nc.dram_tensor("cosK", [2, 128, N], F32, kind="ExternalInput").ap()
    sinK = nc.dram_tensor("sinK", [2, 128, N], F32, kind="ExternalInput").ap()
    Mcomb = nc.dram_tensor("Mcomb", [128, H * 384], F32, kind="ExternalInput").ap()
    Gfull = nc.dram_tensor("Gfull", [128, KC * C3], F32, kind="ExternalInput").ap()
    Gam = nc.dram_tensor("Gam", [128, 2 * D], F32, kind="ExternalInput").ap()
    RT_d = nc.dram_tensor("RT", [128, 128], BF16, kind="ExternalInput").ap()
    idf_d = nc.dram_tensor("idf32", [128, 128], F32, kind="ExternalInput").ap()
    idb_d = nc.dram_tensor("idbf", [128, 128], BF16, kind="ExternalInput").ap()
    eps_d = nc.dram_tensor("epsln", [128, 1], F32, kind="ExternalInput").ap()
    gC = None

    out_d = nc.dram_tensor("out", [N, D], F32, kind="ExternalOutput").ap()

    xn_d = nc.dram_tensor("xn_d", [N, D], BF16, kind="Internal").ap()
    xnT_d = nc.dram_tensor("xnT_d", [D, N], BF16, kind="Internal").ap()
    xq_d = nc.dram_tensor("xq_d", [N, D], BF16, kind="Internal").ap()
    ffn_d = nc.dram_tensor("ffn_d", [N, D], F16, kind="Internal").ap()
    wq16_d = nc.dram_tensor("wq16_d", [D, D], BF16, kind="Internal").ap()
    wk16_d = nc.dram_tensor("wk16_d", [D, D], BF16, kind="Internal").ap()
    wv16_d = nc.dram_tensor("wv16_d", [D, D], BF16, kind="Internal").ap()
    wg16_d = nc.dram_tensor("wg16_d", [D, D], BF16, kind="Internal").ap()
    wo16_d = nc.dram_tensor("wo16_d", [D, D], BF16, kind="Internal").ap()
    q_d = nc.dram_tensor("q_d", [D, N], BF16, kind="Internal").ap()
    k_d = nc.dram_tensor("k_d", [D, N], BF16, kind="Internal").ap()
    v_d = nc.dram_tensor("v_d", [N, D], BF16, kind="Internal").ap()
    ret_d = nc.dram_tensor("ret_d", [N, D], BF16, kind="Internal").ap()
    gate_d = nc.dram_tensor("gate_d", [D, N], BF16, kind="Internal").ap()

    dbg = {}
    if debug:
        for nm, shape, dt in [("xn", [N, D], BF16), ("xq", [N, D], BF16),
                              ("ffn", [N, D], F16),
                              ("q", [D, N], BF16), ("k", [D, N], BF16),
                              ("v", [N, D], BF16), ("ret", [N, D], BF16),
                              ("gate", [D, N], BF16)]:
            dbg[nm] = nc.dram_tensor("dbg_" + nm, shape, dt, kind="ExternalOutput").ap()

    gC_list = _host_constants()["gC"]

    with ExitStack() as PS:
        sbuf = lambda name, shape, dtype: PS.enter_context(nc.sbuf_tensor(name, shape, dtype))
        psum = lambda name, shape, dtype: nc.psum_tensor(name, shape, dtype)

        idf = sbuf("idf", [128, 128], F32)
        idb = sbuf("idb", [128, 128], BF16)
        RTt = sbuf("RTt", [128, 128], BF16)
        epst = sbuf("epst", [128, 1], F32)

        block = PS.enter_context(nc.Block())
        sems = []

        def sem(name):
            s = SemC(nc, PS, name)
            sems.append(s)
            return s

        L = sem("L")
        SS = sem("SS")
        GQ = sem("GQ")

        def guard(eng):
            for s in sems:
                if s.n > 0:
                    eng.wait_ge(s.h, s.n)
        with ExitStack() as ph:
            sb0 = lambda name, shape, dtype: ph.enter_context(nc.sbuf_tensor(name, shape, dtype))
            wT = sb0("wT", [128, KC * D], BF16)
            ws_t = sb0("ws_t", [128, D], F32)
            fb_t = sb0("fb_t", [128, D], F32)
            wstg = sb0("wstg", [128, 2 * D], BF16)
            xt = sb0("xt", [128, 2 * D], F32)
            xcn = sb0("xcn", [128, 2 * D], F32)
            xq8 = sb0("xq8", [128, 2 * D], I8)
            xqb = sb0("xqb", [128, 2 * D], BF16)
            xnb = sb0("xnb", [128, 2 * D], BF16)
            ep1 = sb0("ep1", [128, 2 * D], F32)
            f16t = sb0("f16t", [128, 4 * D], F16)
            scrA = sb0("scrA", [128, 2 * D], F32)
            qTg = sb0("qTg", [128, KC * 512], BF16)
            sclg = sb0("sclg", [128, 4 * NCH0], F32)
            stat = sb0("stat", [128, 64], F32)

            def stc(q, pp):
                c0 = q * 8 + pp * 4
                return stat[:, c0:c0 + 1]

            T0 = sem("T0")
            A0 = sem("A0")
            V0 = sem("V0")
            P0s = sem("P0s")
            E1 = sem("E1")
            PF = sem("PF")
            SV0 = sem("SV0")
            SA0 = sem("SA0")
            WCV = sem("WCV")

            with ExitStack() as pro:
                ff_i8 = xq8[:, 0:D]
                ff_bf = xqb[:, 0:D]
                tpb = [pro.enter_context(nc.psum_tensor(f"tpb{i}", [128, 128], BF16)) for i in range(2)]

                def _sy(s):
                    s.dma_start(out=idf[:], in_=idf_d).then_inc(L.h, 16)
                    s.dma_start(out=idb[:], in_=idb_d).then_inc(L.h, 16)
                    s.dma_start(out=RTt[:], in_=RT_d).then_inc(L.h, 16)
                    s.dma_start(out=epst[:], in_=eps_d).then_inc(L.h, 16)
                    s.dma_start(out=ws_t[:], in_=ws_bc).then_inc(L.h, 16)
                    s.dma_start(out=fb_t[:], in_=fb_bc).then_inc(L.h, 16)
                    L.n += 6 * 16
                block.sync(_sy)

                ffw_r = ffw.rearrange("(nch p) k -> nch p k", p=128)
                for i in range(KC):
                    def _sy(s, i=i):
                        if i > 0:
                            s.wait_ge(P0s.h, i)
                        s.dma_start(out=ff_i8, in_=ffw_r[i]).then_inc(L.h, 16)
                        L.n += 16
                    block.sync(_sy)

                    def _gp(g, i=i):
                        g.wait_ge(L.h, L.n)
                        if i > 0:
                            g.wait_ge(T0.h, T0.n)
                        P0s.inc(g.tensor_copy(ff_bf, ff_i8))
                    block.gpsimd(_gp)

                    def _te(t, i=i):
                        t.wait_ge(P0s.h, P0s.n)
                        for kc in range(KC):
                            done_evacs = i * KC + kc - 2 + 1
                            if done_evacs > 0:
                                t.wait_ge(A0.h, done_evacs)
                            T0.inc(t.transpose(tpb[kc % 2][:], ff_bf[:, kc * 128:(kc + 1) * 128], idb[:]))
                    block.tensor(_te)

                    def _sc(s, i=i):
                        for kc in range(KC):
                            s.wait_ge(T0.h, i * KC + kc + 1)
                            A0.inc(s.copy(wT[:, kc * D + i * 128: kc * D + (i + 1) * 128], tpb[kc % 2][:]))
                    block.scalar(_sc)

            pro_T0, pro_P0 = T0.n, P0s.n
            ffp = [ph.enter_context(psum(f"ffp{i}", [128, D], F32)) for i in range(2)]

            wconv_jobs = []
            for w_src, w_dst in [(w_q, wq16_d), (w_k, wk16_d), (w_v, wv16_d),
                                 (w_g, wg16_d), (w_o, wo16_d)]:
                ws_r = w_src.rearrange("(kc p) m -> kc p m", p=128)
                wd_r = w_dst.rearrange("(kc p) m -> kc p m", p=128)
                for kc in range(KC):
                    wconv_jobs.append((ws_r[kc], wd_r[kc]))

            def emit_wconv(g, lo, hi):
                for j in range(lo, min(hi, len(wconv_jobs))):
                    sl2 = slice((j % 2) * D, (j % 2 + 1) * D)
                    if j >= 2:
                        g.wait_ge(WCV.h, 32 * (j - 1))
                    g.dma_start(out=wstg[:, sl2], in_=wconv_jobs[j][0]).then_inc(WCV.h, 16)
                    WCV.n += 16
                    g.wait_ge(WCV.h, WCV.n)
                    g.dma_start(out=wconv_jobs[j][1], in_=wstg[:, sl2]).then_inc(WCV.h, 16)
                    WCV.n += 16

            x_r = x_d.rearrange("(c p) d -> c p d", p=128)
            xn_store = xn_d.rearrange("(c p) d -> c p d", p=128)
            xq_store = xq_d.rearrange("(c p) d -> c p d", p=128)
            ffn_store = ffn_d.rearrange("(c p) d -> c p d", p=128)
            NG = NCH0 // 4
            NV = 4 * (NG - 1) + 9
            V0Ae, A0e, SSe, T0e, Lq = [], [], [], [], {}
            P0sq = []

            for c in range(NV):
                pp = c % 2
                sl = slice(pp * D, (pp + 1) * D)
                real = c < NCH0
                cm1, cm2 = c - 1, c - 2
                pm1, pm2 = (c - 1) % 2, (c - 2) % 2
                slm1 = slice(pm1 * D, (pm1 + 1) * D)
                slm2 = slice(pm2 * D, (pm2 + 1) * D)

                def _sy(s, c=c):
                    if real:
                        if c >= 2:
                            s.wait_ge(V0.h, V0Ae[c - 2])
                        s.dma_start(out=xt[:, sl], in_=x_r[c]).then_inc(L.h, 16)
                        L.n += 16
                    if c >= 6 and (c - 6) % 4 == 0 and (c - 6) // 4 < NG:
                        g = (c - 6) // 4
                        if g > 0:
                            s.wait_ge(T0.h, T0e[-1])
                        s.wait_ge(SS.h, SSe[4 * g + 4])
                        for kc in range(KC):
                            s.dma_start(out=qTg[:, kc * 512:(kc + 1) * 512],
                                        in_=xq_d[g * 512:(g + 1) * 512, kc * 128:(kc + 1) * 128],
                                        transpose=True).then_inc(L.h, 16)
                        L.n += 16 * KC
                        Lq[g] = L.n
                block.sync(_sy)

                def _ve(v, c=c):
                    if 0 <= cm1 < NCH0:
                        v.wait_ge(P0s.h, P0sq[cm1])
                        V0.inc(v.tensor_reduce(stc(2, pm1), scrA[:, slm1], AX.X, ALU.add))
                    if real:
                        v.wait_ge(L.h, L.n if c < 6 else L.n - (16 * KC if (c - 6) % 4 == 0 else 0))
                        if c >= 1:
                            v.wait_ge(A0.h, A0e[c - 1])
                        SV0.inc(v.tensor_reduce(stc(0, pp), xt[:, sl], AX.X, ALU.add))
                        v.wait_ge(SV0.h, SV0.n)
                        SV0.inc(v.tensor_scalar(stc(0, pp), stc(0, pp), -1.0 / D, None, ALU.mult))
                        v.wait_ge(SV0.h, SV0.n)
                        v.tensor_scalar(xcn[:, sl], xt[:, sl], stc(0, pp), None, ALU.add)
                        ins = v.tensor_reduce(stc(1, pp), xcn[:, sl], AX.X, ALU.max,
                                              apply_absolute_value=True)
                        V0.inc(ins)
                    V0Ae.append(V0.n)
                block.vector(_ve)

                def _sc(s, c=c):
                    if 0 <= cm1 < NCH0:
                        s.wait_ge(V0.h, V0Ae[c] - (1 if real else 0))
                        SA0.inc(s.activation(stc(3, pm1), stc(2, pm1), AF.Ln,
                                             bias=epst[:], scale=1.0 / D))
                        s.wait_ge(SA0.h, SA0.n)
                        A0.inc(s.activation(stc(4, pm1), stc(3, pm1), AF.Exp, bias=0.0, scale=-0.5))
                block.scalar(_sc)
                a0_rstd = A0.n

                def _v2(v, c=c):
                    if 0 <= cm1 < NCH0:
                        if cm1 == 0:
                            v.wait_ge(P0s.h, pro_P0)
                            v.wait_ge(T0.h, pro_T0)
                        v.wait_ge(A0.h, a0_rstd)
                        SV0.inc(v.tensor_scalar(sclg[:, 4 * cm1:4 * cm1 + 1], stc(1, pm1),
                                                stc(4, pm1), None, ALU.mult))
                        v.wait_ge(SV0.h, SV0.n)
                        SV0.inc(v.tensor_scalar(sclg[:, 4 * cm1:4 * cm1 + 1], sclg[:, 4 * cm1:4 * cm1 + 1],
                                                1e-7, 1.0 / 127.0, ALU.max, ALU.mult))
                        v.wait_ge(SV0.h, SV0.n)
                        SV0.inc(v.reciprocal(stc(6, pm1), sclg[:, 4 * cm1:4 * cm1 + 1]))
                        v.wait_ge(SV0.h, SV0.n)
                        SV0.inc(v.tensor_scalar(stc(7, pm1), stc(4, pm1), stc(6, pm1), None, ALU.mult))
                        v.wait_ge(SV0.h, SV0.n)
                        ins = v.tensor_scalar(xq8[:, slm1], xcn[:, slm1], stc(7, pm1), None, ALU.mult)
                        V0.inc(ins)
                    for g in range(NG):
                        if c == 4 * g + 7:
                            subs = [0, 1]
                        elif c == 4 * g + 8:
                            subs = [2, 3]
                        else:
                            continue
                        for j in subs:
                            cj = 4 * g + j
                            v.wait_ge(T0.h, pro_T0 + 4 * g + j + 1)
                            if 4 * g + j >= 2:
                                v.wait_ge(PF.h, 4 * g + j - 1)
                            ins = v.scalar_tensor_tensor(ep1[:, (j % 2) * D:(j % 2 + 1) * D],
                                                         ffp[j % 2][:], sclg[:, 4 * cj:4 * cj + 1],
                                                         ws_t[:], ALU.mult, ALU.mult)
                            E1.inc(ins)
                block.vector(_v2)

                def _s2(s, c=c):
                    if 0 <= cm1 < NCH0:
                        s.wait_ge(V0.h, V0.n)
                        if c >= 3:
                            s.wait_ge(SS.h, SSe[c - 2])
                        if cm1 == 0:
                            s.wait_ge(T0.h, pro_T0)
                        s.wait_ge(A0.h, A0.n)
                        A0.inc(s.activation(xnb[:, slm1], xcn[:, slm1], AF.Identity,
                                            bias=0.0, scale=stc(4, pm1)))
                        A0.inc(s.activation(xqb[:, slm1], xq8[:, slm1], AF.Identity,
                                            bias=0.0, scale=1.0))
                        s.wait_ge(A0.h, A0.n)
                        s.dma_start(out=xn_store[cm1], in_=xnb[:, slm1]).then_inc(SS.h, 16)
                        s.dma_start(out=xq_store[cm1], in_=xqb[:, slm1]).then_inc(SS.h, 16)
                        SS.n += 32
                    for g in range(NG):
                        if c == 4 * g + 8:
                            for j in range(4):
                                s.wait_ge(PF.h, 4 * g + j + 1)
                                s.dma_start(out=ffn_store[4 * g + j],
                                            in_=f16t[:, j * D:(j + 1) * D]).then_inc(SS.h, 16)
                                SS.n += 16
                    A0e.append(A0.n)
                    SSe.append(SS.n)
                block.scalar(_s2)

                def _gp(g2, c=c):
                    if real:
                        g2.wait_ge(V0.h, V0Ae[c])
                        if c >= 2:
                            g2.wait_ge(V0.h, V0.n)
                        ins = g2.tensor_tensor(scrA[:, sl], xcn[:, sl], xcn[:, sl], ALU.mult)
                        P0s.inc(ins)
                    P0sq.append(P0s.n)
                    emit_wconv(g2, 3 * c, 3 * (c + 1))
                    for g in range(NG):
                        if c == 4 * g + 8:
                            for j in range(4):
                                g2.wait_ge(E1.h, 4 * g + j + 1)
                                ins = g2.tensor_tensor(f16t[:, j * D:(j + 1) * D],
                                                       ep1[:, (j % 2) * D:(j % 2 + 1) * D],
                                                       fb_t[:], ALU.add)
                                PF.inc(ins)
                block.gpsimd(_gp)

                def _te(t, c=c):
                    for g in range(NG):
                        if c == 4 * g + 6:
                            subs = [0, 1]
                            t.wait_ge(L.h, Lq[g])
                        elif c == 4 * g + 7:
                            subs = [2, 3]
                        else:
                            continue
                        for j in subs:
                            prev_use = 4 * g + j - 2
                            if prev_use >= 0:
                                t.wait_ge(E1.h, prev_use + 1)
                            for nf in range(4):
                                for kc in range(KC):
                                    ins = t.matmul(ffp[j % 2][:, nf * 512:(nf + 1) * 512],
                                                   qTg[:, kc * 512 + j * 128: kc * 512 + (j + 1) * 128],
                                                   wT[:, kc * D + nf * 512: kc * D + (nf + 1) * 512],
                                                   start=(kc == 0), stop=(kc == KC - 1))
                            T0.inc(ins)
                    T0e.append(T0.n)
                block.tensor(_te)

        def proj_rope_phase(tag, w_ap, bias_ap, cos_ap, sin_ap, out_store):
            with ExitStack() as ph:
                sb1 = lambda name, shape, dtype: ph.enter_context(nc.sbuf_tensor(tag + name, shape, dtype))
                wqb = sb1("wqb", [128, KC * D], BF16)
                bqt = sb1("bqt", [128, KC], F32)
                xnTs = sb1("xnTs", [128, 2 * KC * C1], BF16)
                cosS = sb1("cosS", [128, 2 * 2 * C1], F32)
                sinS = sb1("sinS", [128, 2 * 2 * C1], F32)
                q1 = sb1("q1", [128, KC * C1], BF16)
                qh = sb1("qh", [128, 2 * KC * C1], BF16)
                rot = sb1("rot", [128, 2 * C1], BF16)
                t1b = sb1("t1b", [128, 2 * C1], BF16)
                qps = [ph.enter_context(psum(tag + f"qps{i}", [128, C1], F32)) for i in range(2)]
                rps = [ph.enter_context(psum(tag + f"rps{i}", [128, C1], F32)) for i in range(2)]

                Tp = sem(tag + "Tp")
                Tr = sem(tag + "Tr")
                Ap = sem(tag + "Ap")
                Vp = sem(tag + "Vp")
                Pp = sem(tag + "Pp")

                def _g1(g):
                    guard(g)
                    g.dma_start(out=wqb.rearrange("p (kc m) -> p kc m", kc=KC),
                                in_=w_ap.rearrange("(kc p) m -> p kc m", p=128)).then_inc(GQ.h, 16)
                    GQ.n += 16
                block.gpsimd(_g1)

                def _s1(s):
                    guard(s)
                    s.dma_start(out=bqt[:], in_=bias_ap).then_inc(SS.h, 16)
                    SS.n += 16
                block.scalar(_s1)

                tp_e, ap_e, vp_e, pp_e, l_e, ss_e = [], [], [], [], [], []
                KCC = KC * C1
                for c in range(NCH1):
                    pp_ = c % 2
                    xsl = slice(pp_ * KCC, (pp_ + 1) * KCC)
                    csl_base = pp_ * 2 * C1

                    def _sy(s, c=c):
                        if c == 0:
                            guard(s)
                        elif c >= 2:
                            s.wait_ge(Tp.h, tp_e[c - 2])
                            s.wait_ge(SS.h, ss_e[c - 2])
                            s.wait_ge(Pp.h, pp_e[c - 2])
                            s.wait_ge(Vp.h, vp_e[c - 2])
                        if tag == "P1q":
                            for kc in range(KC):
                                s.dma_start(out=xnTs[:, pp_ * KCC + kc * C1:(pp_ * KCC + (kc + 1) * C1)],
                                            in_=xn_d[c * C1:(c + 1) * C1, kc * 128:(kc + 1) * 128],
                                            transpose=True).then_inc(L.h, 16)
                        else:
                            s.dma_start(out=xnTs[:, pp_ * KCC:(pp_ + 1) * KCC].rearrange("p (kc t) -> p kc t", kc=KC),
                                        in_=xnT_d.rearrange("(kc p) t -> p kc t", p=128)[:, :, c * C1:(c + 1) * C1]).then_inc(L.h, 16)
                            L.n -= 16 * (KC - 1)
                        s.dma_start(out=cosS[:, csl_base:csl_base + 2 * C1].rearrange("p (dk t) -> p dk t", dk=2),
                                    in_=cos_ap[:, :, c * C1:(c + 1) * C1].rearrange("dk p t -> p dk t")).then_inc(L.h, 16)
                        s.dma_start(out=sinS[:, csl_base:csl_base + 2 * C1].rearrange("p (dk t) -> p dk t", dk=2),
                                    in_=sin_ap[:, :, c * C1:(c + 1) * C1].rearrange("dk p t -> p dk t")).then_inc(L.h, 16)
                        L.n += 16 * (KC + 2)
                        l_e.append(L.n)
                    block.sync(_sy)

                    tp_b, tr_b, ap_b, vp_b, pp_b = Tp.n, Tr.n, Ap.n, Vp.n, Pp.n

                    def _te(t, c=c):
                        t.wait_ge(L.h, l_e[c])
                        if c == 0:
                            t.wait_ge(GQ.h, GQ.n)
                            t.wait_ge(SS.h, SS.n)
                        else:
                            t.wait_ge(Ap.h, ap_b)
                            t.wait_ge(Vp.h, vp_b)
                        for mm in range(KC):
                            if mm >= 2:
                                t.wait_ge(Ap.h, ap_b + mm - 1)
                            for kc in range(KC):
                                ins = t.matmul(qps[mm % 2][:],
                                               wqb[:, kc * D + mm * 128: kc * D + (mm + 1) * 128],
                                               xnTs[:, pp_ * KCC + kc * C1: pp_ * KCC + (kc + 1) * C1],
                                               start=(kc == 0), stop=(kc == KC - 1))
                            Tp.inc(ins)
                            if mm >= 1:
                                r = mm - 1
                                t.wait_ge(Ap.h, ap_b + r + 1)
                                if r >= 2:
                                    t.wait_ge(Vp.h, vp_b + r - 1)
                                ins = t.matmul(rps[r % 2][:], RTt[:], q1[:, r * C1:(r + 1) * C1],
                                               start=True, stop=True)
                                Tr.inc(ins)
                        r = KC - 1
                        t.wait_ge(Ap.h, ap_b + r + 1)
                        t.wait_ge(Vp.h, vp_b + r - 1)
                        ins = t.matmul(rps[r % 2][:], RTt[:], q1[:, r * C1:(r + 1) * C1],
                                       start=True, stop=True)
                        Tr.inc(ins)
                    block.tensor(_te)

                    def _sc(s, c=c):
                        if c == 0:
                            s.wait_ge(SS.h, SS.n)
                        for mm in range(KC):
                            s.wait_ge(Tp.h, tp_b + mm + 1)
                            if c > 0 and mm == 0:
                                s.wait_ge(Pp.h, pp_b)
                            ins = s.activation(q1[:, mm * C1:(mm + 1) * C1], qps[mm % 2][:],
                                               AF.Identity, bias=bqt[:, mm:mm + 1], scale=1.0)
                            Ap.inc(ins)
                    block.scalar(_sc)

                    def _ve(v, c=c):
                        if c > 0:
                            v.wait_ge(Pp.h, pp_b)
                        for mm in range(KC):
                            v.wait_ge(Tr.h, tr_b + mm + 1)
                            if mm >= 2:
                                v.wait_ge(Pp.h, pp_b + mm - 1)
                            dk = mm % 2
                            v.tensor_tensor(rot[:, dk * C1:(dk + 1) * C1], rps[mm % 2][:],
                                            sinS[:, csl_base + dk * C1: csl_base + (dk + 1) * C1], ALU.mult)
                            ins = v.tensor_tensor(t1b[:, dk * C1:(dk + 1) * C1], q1[:, mm * C1:(mm + 1) * C1],
                                                  cosS[:, csl_base + dk * C1: csl_base + (dk + 1) * C1], ALU.mult)
                            Vp.inc(ins)
                    block.vector(_ve)

                    def _gp(g, c=c):
                        if c >= 2:
                            g.wait_ge(SS.h, ss_e[c - 2])
                        for mm in range(KC):
                            dk = mm % 2
                            g.wait_ge(Vp.h, vp_b + mm + 1)
                            ins = g.tensor_tensor(qh[:, pp_ * KCC + mm * C1: pp_ * KCC + (mm + 1) * C1],
                                                  t1b[:, dk * C1:(dk + 1) * C1],
                                                  rot[:, dk * C1:(dk + 1) * C1], ALU.add)
                            Pp.inc(ins)
                    block.gpsimd(_gp)

                    def _sc(s, c=c):
                        if tag == "P1q":
                            s.wait_ge(L.h, l_e[c])
                            s.dma_start(out=xnT_d.rearrange("(kc p) t -> p kc t", p=128)[:, :, c * C1:(c + 1) * C1],
                                        in_=xnTs[:, xsl].rearrange("p (kc t) -> p kc t", kc=KC)).then_inc(SS.h, 16)
                            SS.n += 16
                        s.wait_ge(Pp.h, pp_b + KC)
                        s.dma_start(out=out_store[:, :, c * C1:(c + 1) * C1],
                                    in_=qh[:, xsl].rearrange("p (kc t) -> p kc t", kc=KC)).then_inc(SS.h, 16)
                        SS.n += 16
                        ss_e.append(SS.n)
                    block.scalar(_sc)

                    tp_e.append(Tp.n)
                    ap_e.append(Ap.n)
                    vp_e.append(Vp.n)
                    pp_e.append(Pp.n)

        q_store = q_d.rearrange("(kc p) t -> p kc t", p=128)
        k_store = k_d.rearrange("(kc p) t -> p kc t", p=128)
        proj_rope_phase("P1q", wq16_d, bq_t, cosT, sinT, q_store)
        proj_rope_phase("P1k", wk16_d, bk_t, cosK, sinK, k_store)

        with ExitStack() as ph:
            sb2 = lambda name, shape, dtype: ph.enter_context(nc.sbuf_tensor("P2" + name, shape, dtype))
            wvb = sb2("wvb", [128, KC * D], BF16)
            bvt = sb2("bvt", [128, D], F32)
            xnTs = sb2("xnTs", [128, 2 * KC * C1], BF16)
            vsb = sb2("vsb", [128, 2 * D], BF16)
            vps = [ph.enter_context(psum(f"vps{i}", [128, D], F32)) for i in range(2)]

            T2 = sem("T2")
            V2 = sem("V2")

            def _g1(g):
                guard(g)
                g.dma_start(out=wvb.rearrange("p (kc m) -> p kc m", kc=KC),
                            in_=wv16_d.rearrange("(kc p) m -> p kc m", p=128)).then_inc(GQ.h, 16)
                GQ.n += 16
            block.gpsimd(_g1)

            def _s1(s):
                guard(s)
                s.dma_start(out=bvt[:], in_=bv_bc).then_inc(SS.h, 16)
                SS.n += 16
            block.scalar(_s1)

            v_store = v_d.rearrange("(r p) d -> r p d", p=128)
            KCC = KC * C1
            t2_e, l2_e = [], []
            for c in range(NCH1):
                pp_ = c % 2
                def _sy(s, c=c):
                    if c == 0:
                        guard(s)
                    elif c >= 2:
                        s.wait_ge(T2.h, t2_e[c - 2])
                    s.dma_start(out=xnTs[:, pp_ * KCC:(pp_ + 1) * KCC].rearrange("p (kc t) -> p kc t", kc=KC),
                                in_=xnT_d.rearrange("(kc p) t -> p kc t", p=128)[:, :, c * C1:(c + 1) * C1]).then_inc(L.h, 16)
                    L.n += 16
                    l2_e.append(L.n)
                block.sync(_sy)

                t2_b, v2_b, ss_b = T2.n, V2.n, SS.n

                def _te(t, c=c):
                    t.wait_ge(L.h, l2_e[c])
                    if c == 0:
                        t.wait_ge(GQ.h, GQ.n)
                        t.wait_ge(SS.h, SS.n)
                    else:
                        t.wait_ge(V2.h, v2_b)
                    for ts in range(4):
                        if ts >= 2:
                            t.wait_ge(V2.h, v2_b + ts - 1)
                        for nf in range(4):
                            for kc in range(KC):
                                ins = t.matmul(vps[ts % 2][:, nf * 512:(nf + 1) * 512],
                                               xnTs[:, pp_ * KCC + kc * C1 + ts * 128: pp_ * KCC + kc * C1 + (ts + 1) * 128],
                                               wvb[:, kc * D + nf * 512: kc * D + (nf + 1) * 512],
                                               start=(kc == 0), stop=(kc == KC - 1))
                        T2.inc(ins)
                    t2_e.append(T2.n)
                block.tensor(_te)

                def _ve(v, c=c):
                    v.wait_ge(SS.h, ss_b if c > 0 else SS.n)
                    for ts in range(4):
                        v.wait_ge(T2.h, t2_b + ts + 1)
                        if ts >= 2:
                            v.wait_ge(SS.h, ss_b + 16 * (ts - 1))
                        ins = v.scalar_tensor_tensor(vsb[:, (ts % 2) * D:(ts % 2 + 1) * D],
                                                     vps[ts % 2][:], 1.0, bvt[:], ALU.mult, ALU.add)
                        V2.inc(ins)
                block.vector(_ve)

                def _sc(s, c=c):
                    for ts in range(4):
                        s.wait_ge(V2.h, v2_b + ts + 1)
                        s.dma_start(out=v_store[c * 4 + ts],
                                    in_=vsb[:, (ts % 2) * D:(ts % 2 + 1) * D]).then_inc(SS.h, 16)
                        SS.n += 16
                block.scalar(_sc)

        with ExitStack() as ph:
            sb3 = lambda name, shape, dtype: ph.enter_context(nc.sbuf_tensor("P3" + name, shape, dtype))
            KCC3 = KC * C3
            qsb = sb3("qsb", [128, 2 * KCC3], BF16)
            ksb = sb3("ksb", [128, 2 * KCC3], BF16)
            vsb = sb3("vsb", [128, 2 * 2 * D], BF16)
            ktok = sb3("ktok", [128, 2 * D], BF16)
            ktg = sb3("ktg", [128, 2 * D], BF16)
            qg = sb3("qg", [128, KCC3], BF16)
            Mct = sb3("Mct", [128, H * 384], F32)
            Gt = sb3("Gt", [128, KCC3], F32)
            Gmt = sb3("Gmt", [128, 2 * D], F32)
            Ssb = sb3("Ssb", [128, KC * DH], F32)
            Sbf = sb3("Sbf", [128, KC * DH], BF16)
            simb = sb3("simb", [128, 2 * 384], BF16)
            retb = sb3("retb", [128, 2 * 2 * D], BF16)
            rtb = sb3("rtb", [128, 2 * 2 * D], BF16)
            scrB = sb3("scrB", [128, 2 * D], F32)
            gsum = sb3("gsum", [128, 16], F32)
            gmneg = sb3("gmneg", [128, 16], F32)
            gmsq = sb3("gmsq", [128, 16], F32)
            gsq = sb3("gsq", [128, 16], F32)
            gvar = sb3("gvar", [128, 16], F32)
            grs = sb3("grs", [128, 16], F32)

            simc = [ph.enter_context(psum(f"simc{i}", [128, 384], F32)) for i in range(2)]
            ret_ps = [[ph.enter_context(psum(f"ret{i}{j}", [128, DH], F32)) for j in range(2)]
                      for i in range(2)]
            upd_ps = [ph.enter_context(psum(f"upd{i}", [128, 2 * DH], F32)) for i in range(2)]

            T3s = sem("T3s")
            T3r = sem("T3r")
            T3u = sem("T3u")
            V3s = sem("V3s")
            V3u = sem("V3u")
            A3r = sem("A3r")
            P3p = sem("P3p")
            V3g = sem("V3g")
            A3g = sem("A3g")
            P3q = sem("P3q")
            P3g = sem("P3g")
            SV3 = sem("SV3")
            SP3 = sem("SP3")
            V3p = sem("V3p")

            def _sy(s):
                guard(s)
                s.dma_start(out=Mct[:], in_=Mcomb).then_inc(L.h, 16)
                s.dma_start(out=Gt[:], in_=Gfull).then_inc(L.h, 16)
                s.dma_start(out=Gmt[:], in_=Gam).then_inc(L.h, 16)
                L.n += 48
            block.sync(_sy)

            q_src = q_d.rearrange("(kc p) t -> p kc t", p=128)
            k_src = k_d.rearrange("(kc p) t -> p kc t", p=128)
            v_src = v_d.rearrange("(r p) d -> r p d", p=128)
            ret_store = ret_d.rearrange("(r p) d -> r p d", p=128)

            t3s_e, t3r_e, t3u_e, v3s_e, v3u_e, a3r_e = [], [], [], [], [], []
            p3p_e, l3_e, k3_e, p3g_e, ss3_e = [], [], [], [], []

            def P3_emit_gn(c):
                pp = c % 2
                rb = pp * 2 * D

                def _sq(s, c=c):
                    s.wait_ge(V3g.h, V3g.n)
                    for i in range(2):
                        ins = s.activation(scrB[:, i * D:(i + 1) * D],
                                           retb[:, rb + i * D: rb + (i + 1) * D], AF.Square)
                        A3g.inc(ins)
                block.scalar(_sq)
                a3g_b = A3g.n - 2

                def _vg(v, c=c):
                    for i in range(2):
                        v.wait_ge(A3r.h, a3r_e[c])
                        if c >= 2:
                            v.wait_ge(P3q.h, P3q.n)
                        SV3.inc(v.tensor_reduce(gsum[:, i * 8:(i + 1) * 8].rearrange("p (h o) -> p h o", o=1),
                                                retb[:, rb + i * D: rb + (i + 1) * D].rearrange("p (h d) -> p h d", h=H),
                                                AX.X, ALU.add))
                        v.wait_ge(A3g.h, a3g_b + i + 1)
                        ins = v.tensor_reduce(gsq[:, i * 8:(i + 1) * 8].rearrange("p (h o) -> p h o", o=1),
                                              scrB[:, i * D:(i + 1) * D].rearrange("p (h d) -> p h d", h=H),
                                              AX.X, ALU.add)
                        V3g.inc(ins)
                block.vector(_vg)
                v3g_b = V3g.n - 2

                def _pm(g, c=c):
                    for i in range(2):
                        g.wait_ge(V3g.h, v3g_b + i + 1)
                        SP3.inc(g.tensor_scalar(gmneg[:, i * 8:(i + 1) * 8], gsum[:, i * 8:(i + 1) * 8],
                                                -1.0 / DH, None, ALU.mult))
                        g.wait_ge(SP3.h, SP3.n)
                        SP3.inc(g.tensor_tensor(gmsq[:, i * 8:(i + 1) * 8], gmneg[:, i * 8:(i + 1) * 8],
                                                gmneg[:, i * 8:(i + 1) * 8], ALU.mult))
                        g.wait_ge(SP3.h, SP3.n)
                        SP3.inc(g.tensor_scalar(gvar[:, i * 8:(i + 1) * 8], gsq[:, i * 8:(i + 1) * 8],
                                                1.0 / DH, None, ALU.mult))
                        g.wait_ge(SP3.h, SP3.n)
                        ins = g.tensor_tensor(gvar[:, i * 8:(i + 1) * 8], gvar[:, i * 8:(i + 1) * 8],
                                              gmsq[:, i * 8:(i + 1) * 8], ALU.subtract)
                        P3q.inc(ins)
                block.gpsimd(_pm)
                p3q_b = P3q.n - 2

                def _sd(s, c=c):
                    for i in range(2):
                        s.wait_ge(P3q.h, p3q_b + i + 1)
                        SA3g = s.activation(gvar[:, i * 8:(i + 1) * 8], gvar[:, i * 8:(i + 1) * 8],
                                            AF.Ln, bias=epst[:], scale=1.0)
                        A3g.inc(SA3g)
                        s.wait_ge(A3g.h, A3g.n)
                        ins = s.activation(grs[:, i * 8:(i + 1) * 8], gvar[:, i * 8:(i + 1) * 8],
                                           AF.Exp, bias=0.0, scale=-0.5)
                        A3g.inc(ins)
                block.scalar(_sd)
                a3g_b2 = A3g.n - 4

                def _pn(g, c=c):
                    for i in range(2):
                        g.wait_ge(A3g.h, a3g_b2 + 2 * (i + 1))
                        if c >= 2:
                            g.wait_ge(SS.h, ss3_e[c - 2])
                        for h in range(H):
                            ins = g.tensor_scalar(rtb[:, rb + i * D + h * DH: rb + i * D + (h + 1) * DH],
                                                  retb[:, rb + i * D + h * DH: rb + i * D + (h + 1) * DH],
                                                  gmneg[:, i * 8 + h: i * 8 + h + 1],
                                                  grs[:, i * 8 + h: i * 8 + h + 1],
                                                  ALU.add, ALU.mult)
                            P3g.inc(ins)
                block.gpsimd(_pn)
                p3g_e.append(P3g.n)

                def _st(s, c=c):
                    for i in range(2):
                        s.wait_ge(P3g.h, p3g_e[-1] - (1 - i) * 8)
                        s.dma_start(out=ret_store[2 * c + i],
                                    in_=rtb[:, rb + i * D: rb + (i + 1) * D]).then_inc(SS.h, 16)
                        SS.n += 16
                    ss3_e.append(SS.n)
                block.scalar(_st)

            for c in range(NCH3):
                pp = c % 2
                qb = pp * KCC3
                vb = pp * 2 * D
                rb = pp * 2 * D

                def _sy(s, c=c):
                    if c == 0:
                        guard(s)
                    elif c >= 2:
                        s.wait_ge(T3s.h, t3s_e[c - 2])
                        s.wait_ge(T3r.h, t3r_e[c - 2])
                        s.wait_ge(T3u.h, t3u_e[c - 2])
                        s.wait_ge(P3p.h, p3p_e[c - 2])
                    s.dma_start(out=qsb[:, qb:qb + KCC3].rearrange("p (kc t) -> p kc t", kc=KC),
                                in_=q_src[:, :, c * C3:(c + 1) * C3]).then_inc(L.h, 16)
                    s.dma_start(out=ksb[:, qb:qb + KCC3].rearrange("p (kc t) -> p kc t", kc=KC),
                                in_=k_src[:, :, c * C3:(c + 1) * C3]).then_inc(L.h, 16)
                    s.dma_start(out=vsb[:, vb:vb + 2 * D].rearrange("p (r d) -> p r d", r=2),
                                in_=v_src[2 * c: 2 * c + 2].rearrange("r p d -> p r d")).then_inc(L.h, 16)
                    L.n += 48
                    l3_e.append(L.n)
                block.sync(_sy)

                def _sck(s, c=c):
                    if c == 0:
                        guard(s)
                    else:
                        s.wait_ge(P3p.h, p3p_e[c - 1])
                    for ts in range(2):
                        s.dma_start(out=ktok[:, ts * D:(ts + 1) * D],
                                    in_=k_d[:, c * C3 + ts * 128: c * C3 + (ts + 1) * 128],
                                    transpose=True).then_inc(SS.h, 16)
                    SS.n += 32
                    k3_e.append(SS.n)
                block.scalar(_sck)

                p3_b = P3p.n

                def _gp(g, c=c):
                    g.wait_ge(SS.h, k3_e[c])
                    if c > 0:
                        g.wait_ge(T3u.h, t3u_e[c - 1])
                    P3p.inc(g.tensor_tensor(ktg[:], ktok[:], Gmt[:], ALU.mult))
                    p3p_e.append(P3p.n)
                block.gpsimd(_gp)

                t3s_b, t3r_b, t3u_b = T3s.n, T3r.n, T3u.n
                v3s_b, v3u_b, a3r_b = V3s.n, V3u.n, A3r.n
                v3p_b = V3p.n

                def _te(t, c=c):
                    t.wait_ge(L.h, l3_e[c])
                    for h in range(H):
                        hp = h % 2
                        if h > 1:
                            t.wait_ge(V3s.h, v3s_b + h - 1)
                        elif c > 0:
                            t.wait_ge(V3s.h, v3s_b)
                        for dk in range(2):
                            blk = 2 * h + dk
                            t.matmul(simc[hp][:, 0:C3],
                                     ksb[:, qb + blk * C3: qb + blk * C3 + 128],
                                     qsb[:, qb + blk * C3: qb + blk * C3 + C3],
                                     start=(dk == 0), stop=(dk == 1))
                        for dk in range(2):
                            blk = 2 * h + dk
                            ins = t.matmul(simc[hp][:, 256:384],
                                           ksb[:, qb + blk * C3 + 128: qb + blk * C3 + 256],
                                           qsb[:, qb + blk * C3 + 128: qb + blk * C3 + 256],
                                           start=(dk == 0), stop=(dk == 1))
                        T3s.inc(ins)
                        if h > 1:
                            t.wait_ge(A3r.h, a3r_b + 2 * (h - 1))
                        elif c > 0:
                            t.wait_ge(A3r.h, a3r_b)
                        if c > 0 and h == 0:
                            t.wait_ge(V3p.h, v3p_b + 2)
                        t.wait_ge(V3s.h, v3s_b + h + 1)
                        for i in range(2):
                            ops = [(simb[:, hp * 384 + i * 128: hp * 384 + (i + 1) * 128],
                                    vsb[:, vb + 0 * D + h * DH: vb + 0 * D + (h + 1) * DH])]
                            if i == 1:
                                ops.append((simb[:, hp * 384 + 2 * 128: hp * 384 + 3 * 128],
                                            vsb[:, vb + 1 * D + h * DH: vb + 1 * D + (h + 1) * DH]))
                            if c > 0:
                                for dk in range(2):
                                    blk = 2 * h + dk
                                    ops.append((qg[:, blk * C3 + i * 128: blk * C3 + (i + 1) * 128],
                                                Sbf[:, blk * DH:(blk + 1) * DH]))
                            for oi, (lh, rh) in enumerate(ops):
                                ins = t.matmul(ret_ps[i][hp][:], lh, rh,
                                               start=(oi == 0), stop=(oi == len(ops) - 1))
                            T3r.inc(ins)
                    t.wait_ge(P3p.h, p3_b + 1)
                    for h in range(H):
                        hp = h % 2
                        if h > 1:
                            t.wait_ge(V3u.h, v3u_b + h - 1)
                        elif c > 0:
                            t.wait_ge(V3u.h, v3u_b)
                        for dk in range(2):
                            for ts in range(2):
                                ins = t.matmul(upd_ps[hp][:, dk * DH:(dk + 1) * DH],
                                               ktg[:, ts * D + h * DH + dk * 128: ts * D + h * DH + (dk + 1) * 128],
                                               vsb[:, vb + ts * D + h * DH: vb + ts * D + (h + 1) * DH],
                                               start=(ts == 0), stop=(ts == 1))
                        T3u.inc(ins)
                    t3s_e.append(T3s.n)
                    t3r_e.append(T3r.n)
                    t3u_e.append(T3u.n)
                block.tensor(_te)

                def _ve(v, c=c):
                    v.wait_ge(L.h, l3_e[c])
                    if c > 0:
                        v.wait_ge(V3u.h, v3u_e[c - 1])
                        v.wait_ge(T3r.h, t3r_e[c - 1])
                        V3p.inc(v.tensor_copy(Sbf[:], Ssb[:]))
                        V3p.inc(v.tensor_tensor(qg[:], qsb[:, qb:qb + KCC3], Gt[:], ALU.mult))
                    for h in range(H):
                        hp = h % 2
                        v.wait_ge(T3s.h, t3s_b + h + 1)
                        if h > 1:
                            v.wait_ge(T3r.h, t3r_b + 2 * (h - 1))
                        elif c > 0:
                            v.wait_ge(T3r.h, t3r_b)
                        ins = v.tensor_tensor(simb[:, hp * 384:(hp + 1) * 384], simc[hp][:],
                                              Mct[:, h * 384:(h + 1) * 384], ALU.mult)
                        V3s.inc(ins)
                    for h in range(H):
                        hp = h % 2
                        v.wait_ge(T3u.h, t3u_b + h + 1)
                        if c == 0:
                            ins = v.tensor_copy(Ssb[:, 2 * h * DH:(2 * h + 2) * DH], upd_ps[hp][:])
                        else:
                            ins = v.scalar_tensor_tensor(Ssb[:, 2 * h * DH:(2 * h + 2) * DH],
                                                         Ssb[:, 2 * h * DH:(2 * h + 2) * DH],
                                                         gC_list[h], upd_ps[hp][:],
                                                         ALU.mult, ALU.add)
                        V3u.inc(ins)
                    v3s_e.append(V3s.n)
                    v3u_e.append(V3u.n)
                block.vector(_ve)

                def _sc(s, c=c):
                    for h in range(H):
                        hp = h % 2
                        for i in range(2):
                            s.wait_ge(T3r.h, t3r_b + 2 * h + i + 1)
                            if c >= 2 and h == 0 and i == 0:
                                s.wait_ge(P3g.h, p3g_e[c - 2])
                            A3r.inc(s.copy(retb[:, rb + i * D + h * DH: rb + i * D + (h + 1) * DH],
                                           ret_ps[i][hp][:]))
                    a3r_e.append(A3r.n)
                block.scalar(_sc)

                if c >= 1:
                    P3_emit_gn(c - 1)

            P3_emit_gn(NCH3 - 1)

        with ExitStack() as ph:
            sb4 = lambda name, shape, dtype: ph.enter_context(nc.sbuf_tensor("P4g" + name, shape, dtype))
            wgb = sb4("wgb", [128, KC * D], BF16)
            bgt = sb4("bgt", [128, KC], F32)
            bgtn = sb4("bgtn", [128, KC], F32)
            g1b = sb4("g1b", [128, 2 * C1], BF16)
            teb = sb4("teb", [128, 2 * C1], F32)
            srb = sb4("srb", [128, 2 * C1], F32)
            gnwt = sb4("gnwt", [128, KC], F32)
            gnbt = sb4("gnbt", [128, KC], F32)
            xnTs = sb4("xnTs", [128, 2 * KC * C1], BF16)
            rtf = sb4("rtf", [128, 2 * KC * C1], BF16)
            rta = sb4("rta", [128, 2 * C1], BF16)
            gateb = sb4("gateb", [128, 2 * KC * C1], BF16)
            gps_ = [ph.enter_context(psum(f"gps{i}", [128, C1], F32)) for i in range(2)]

            T4 = sem("T4")
            A4 = sem("A4")
            V4 = sem("V4")
            V4s = sem("V4s")
            P4 = sem("P4")
            SV4 = sem("SV4")

            def _g1(g):
                guard(g)
                g.dma_start(out=wgb.rearrange("p (kc m) -> p kc m", kc=KC),
                            in_=wg16_d.rearrange("(kc p) m -> p kc m", p=128)).then_inc(GQ.h, 16)
                GQ.n += 16
            block.gpsimd(_g1)

            def _s1(s):
                guard(s)
                s.dma_start(out=bgt[:], in_=bg_t).then_inc(SS.h, 16)
                s.dma_start(out=gnwt[:], in_=gnw_t).then_inc(SS.h, 16)
                s.dma_start(out=gnbt[:], in_=gnb_t).then_inc(SS.h, 16)
                SS.n += 48
            block.scalar(_s1)

            def _v1(v):
                guard(v)
                v.wait_ge(SS.h, SS.n)
                SV4.inc(v.tensor_scalar(bgtn[:], bgt[:], -1.0, None, ALU.mult))
                v.wait_ge(SV4.h, SV4.n)
            block.vector(_v1)

            gate_store = gate_d.rearrange("(kc p) t -> p kc t", p=128)
            KCC = KC * C1
            t4_e, l4_e, r4_e, v4_e, p4_e, ss4_e = [], [], [], [], [], []
            for c in range(NCH1):
                pp_ = c % 2
                xsl = slice(pp_ * KCC, (pp_ + 1) * KCC)
                def _sck(s, c=c):
                    if c == 0:
                        guard(s)
                    else:
                        if c >= 2:
                            s.wait_ge(T4.h, t4_e[c - 2])
                            s.wait_ge(V4s.h, v4_e[c - 2])
                    s.dma_start(out=xnTs[:, pp_ * KCC:(pp_ + 1) * KCC].rearrange("p (kc t) -> p kc t", kc=KC),
                                in_=xnT_d.rearrange("(kc p) t -> p kc t", p=128)[:, :, c * C1:(c + 1) * C1]).then_inc(L.h, 16)
                    L.n += 16
                    l4_e.append(L.n)
                    for mm in range(KC):
                        s.dma_start(out=rtf[:, pp_ * KCC + mm * C1: pp_ * KCC + (mm + 1) * C1],
                                    in_=ret_d[c * C1:(c + 1) * C1, mm * 128:(mm + 1) * 128],
                                    transpose=True).then_inc(SS.h, 16)
                    SS.n += 16 * KC
                    r4_e.append(SS.n)
                block.scalar(_sck)

                t4_b, a4_b, v4_b, p4_b, v4s_b = T4.n, A4.n, V4.n, P4.n, V4s.n

                def _te(t, c=c):
                    t.wait_ge(L.h, l4_e[c])
                    if c == 0:
                        t.wait_ge(GQ.h, GQ.n)
                        t.wait_ge(SS.h, SS.n)
                    else:
                        t.wait_ge(A4.h, a4_b)
                    for mm in range(KC):
                        if mm >= 2:
                            t.wait_ge(A4.h, a4_b + mm - 1)
                        for kc in range(KC):
                            ins = t.matmul(gps_[mm % 2][:],
                                           wgb[:, kc * D + mm * 128: kc * D + (mm + 1) * 128],
                                           xnTs[:, pp_ * KCC + kc * C1: pp_ * KCC + (kc + 1) * C1],
                                           start=(kc == 0), stop=(kc == KC - 1))
                        T4.inc(ins)
                    t4_e.append(T4.n)
                block.tensor(_te)

                def _sc(s, c=c):
                    if c == 0:
                        s.wait_ge(SS.h, SS.n)
                    else:
                        s.wait_ge(P4.h, p4_b)
                    for mm in range(KC):
                        pm = mm % 2
                        s.wait_ge(T4.h, t4_b + mm + 1)
                        if mm >= 2:
                            s.wait_ge(P4.h, p4_b + mm - 1)
                            s.wait_ge(V4s.h, v4s_b + mm - 1)
                        s.activation(g1b[:, pm * C1:(pm + 1) * C1], gps_[mm % 2][:],
                                     AF.Identity, bias=bgt[:, mm:mm + 1], scale=1.0)
                        ins = s.activation(teb[:, pm * C1:(pm + 1) * C1],
                                           g1b[:, pm * C1:(pm + 1) * C1],
                                           AF.Exp, bias=0.0, scale=-1.0)
                        A4.inc(ins)
                block.scalar(_sc)

                def _ve(v, c=c):
                    v.wait_ge(SS.h, r4_e[c])
                    if c > 0:
                        v.wait_ge(P4.h, p4_b)
                    for mm in range(KC):
                        pm = mm % 2
                        if mm >= 2:
                            v.wait_ge(P4.h, p4_b + mm - 1)
                        v.tensor_scalar(rta[:, pm * C1:(pm + 1) * C1],
                                        rtf[:, pp_ * KCC + mm * C1: pp_ * KCC + (mm + 1) * C1],
                                        gnwt[:, mm:mm + 1], gnbt[:, mm:mm + 1],
                                        ALU.mult, ALU.add)
                        v.wait_ge(A4.h, a4_b + mm + 1)
                        v.tensor_scalar(teb[:, pm * C1:(pm + 1) * C1],
                                        teb[:, pm * C1:(pm + 1) * C1], 1.0, None, ALU.add)
                        ins = v.reciprocal(srb[:, pm * C1:(pm + 1) * C1],
                                           teb[:, pm * C1:(pm + 1) * C1])
                        V4s.inc(ins)
                    v4_e.append(V4s.n)
                block.vector(_ve)

                def _gp(g, c=c):
                    if c >= 2:
                        g.wait_ge(SS.h, ss4_e[c - 2])
                    for mm in range(KC):
                        pm = mm % 2
                        g.wait_ge(V4s.h, v4s_b + mm + 1)
                        g.tensor_tensor(gateb[:, pp_ * KCC + mm * C1: pp_ * KCC + (mm + 1) * C1],
                                        g1b[:, pm * C1:(pm + 1) * C1],
                                        srb[:, pm * C1:(pm + 1) * C1], ALU.mult)
                        ins = g.tensor_tensor(gateb[:, pp_ * KCC + mm * C1: pp_ * KCC + (mm + 1) * C1],
                                              gateb[:, pp_ * KCC + mm * C1: pp_ * KCC + (mm + 1) * C1],
                                              rta[:, pm * C1:(pm + 1) * C1], ALU.mult)
                        P4.inc(ins)
                block.gpsimd(_gp)


                def _sc(s, c=c):
                    s.wait_ge(P4.h, p4_b + KC)
                    s.dma_start(out=gate_store[:, :, c * C1:(c + 1) * C1],
                                in_=gateb[:, xsl].rearrange("p (kc t) -> p kc t", kc=KC)).then_inc(SS.h, 16)
                    SS.n += 16
                    ss4_e.append(SS.n)
                block.scalar(_sc)

        with ExitStack() as ph:
            sb5 = lambda name, shape, dtype: ph.enter_context(nc.sbuf_tensor("P4o" + name, shape, dtype))
            wob = sb5("wob", [128, KC * D], BF16)
            bot = sb5("bot", [128, D], F32)
            gateb = sb5("gateb", [128, 2 * KC * C1], BF16)
            xb = sb5("xb", [128, 2 * D], F32)
            fnb = sb5("fnb", [128, 2 * D], F16)
            otmp = sb5("otmp", [128, 2 * D], F32)
            otm2 = sb5("otm2", [128, 2 * D], F32)
            osb = sb5("osb", [128, 2 * D], F32)
            ops_ = [ph.enter_context(psum(f"ops{i}", [128, D], F32)) for i in range(2)]

            T5 = sem("T5")
            V5 = sem("V5")
            V5b = sem("V5b")
            P5 = sem("P5")
            L5 = sem("L5")
            SO = sem("SO")

            def _g1(g):
                guard(g)
                g.dma_start(out=wob.rearrange("p (kc m) -> p kc m", kc=KC),
                            in_=wo16_d.rearrange("(kc p) m -> p kc m", p=128)).then_inc(GQ.h, 16)
                GQ.n += 16
            block.gpsimd(_g1)

            def _s1(s):
                guard(s)
                s.dma_start(out=bot[:], in_=bo_bc).then_inc(SS.h, 16)
                SS.n += 16
            block.scalar(_s1)

            gate_src = gate_d.rearrange("(kc p) t -> p kc t", p=128)
            x_src = x_d.rearrange("(r p) d -> r p d", p=128)
            f_src = ffn_d.rearrange("(r p) d -> r p d", p=128)
            out_store = out_d.rearrange("(r p) d -> r p d", p=128)
            KCC = KC * C1
            t5_e, v5_e, l5_e = [], [], []
            for c in range(NCH1):
                pp_ = c % 2
                xsl = slice(pp_ * KCC, (pp_ + 1) * KCC)
                def _sy(s, c=c):
                    if c == 0:
                        guard(s)
                    elif c >= 2:
                        s.wait_ge(T5.h, t5_e[c - 2])
                    s.dma_start(out=gateb[:, xsl].rearrange("p (kc t) -> p kc t", kc=KC),
                                in_=gate_src[:, :, c * C1:(c + 1) * C1]).then_inc(L.h, 16)
                    L.n += 16
                    l5_e.append(L.n)
                    for ts in range(4):
                        tsp = ts % 2
                        gi = 4 * c + ts
                        if gi >= 2:
                            s.wait_ge(V5.h, max(0, gi - 1))
                            s.wait_ge(V5b.h, max(0, gi - 1))
                        s.dma_start(out=xb[:, tsp * D:(tsp + 1) * D], in_=x_src[gi]).then_inc(L5.h, 16)
                        s.dma_start(out=fnb[:, tsp * D:(tsp + 1) * D], in_=f_src[gi]).then_inc(L5.h, 16)
                        L5.n += 32
                block.sync(_sy)

                t5_b, v5_b, v5b_b, p5_b, so_b = T5.n, V5.n, V5b.n, P5.n, SO.n

                def _te(t, c=c):
                    t.wait_ge(L.h, l5_e[c])
                    if c == 0:
                        t.wait_ge(GQ.h, GQ.n)
                        t.wait_ge(SS.h, SS.n)
                    else:
                        t.wait_ge(V5.h, v5_b)
                    for ts in range(4):
                        if ts >= 2:
                            t.wait_ge(V5.h, v5_b + ts - 1)
                        for nf in range(4):
                            for kc in range(KC):
                                ins = t.matmul(ops_[ts % 2][:, nf * 512:(nf + 1) * 512],
                                               gateb[:, pp_ * KCC + kc * C1 + ts * 128: pp_ * KCC + kc * C1 + (ts + 1) * 128],
                                               wob[:, kc * D + nf * 512: kc * D + (nf + 1) * 512],
                                               start=(kc == 0), stop=(kc == KC - 1))
                        T5.inc(ins)
                    t5_e.append(T5.n)
                block.tensor(_te)

                def _ve(v, c=c):
                    if c > 0:
                        v.wait_ge(P5.h, p5_b)
                    for ts in range(4):
                        tsp = ts % 2
                        v.wait_ge(T5.h, t5_b + ts + 1)
                        v.wait_ge(L5.h, 32 * (4 * c + ts) + 32)
                        if ts >= 2:
                            v.wait_ge(P5.h, p5_b + ts - 1)
                        ins = v.scalar_tensor_tensor(otmp[:, tsp * D:(tsp + 1) * D],
                                                     ops_[ts % 2][:], 1.0, xb[:, tsp * D:(tsp + 1) * D],
                                                     ALU.mult, ALU.add)
                        V5.inc(ins)
                        ins = v.tensor_tensor(otm2[:, tsp * D:(tsp + 1) * D],
                                              otmp[:, tsp * D:(tsp + 1) * D],
                                              fnb[:, tsp * D:(tsp + 1) * D], ALU.add)
                        V5b.inc(ins)
                block.vector(_ve)

                def _gp(g, c=c):
                    if c == 0:
                        g.wait_ge(SS.h, SS.n)
                    else:
                        g.wait_ge(SO.h, so_b)
                    for ts in range(4):
                        tsp = ts % 2
                        g.wait_ge(V5b.h, v5b_b + ts + 1)
                        if ts >= 2:
                            g.wait_ge(SO.h, so_b + 16 * (ts - 1))
                        ins = g.tensor_tensor(osb[:, tsp * D:(tsp + 1) * D],
                                              otm2[:, tsp * D:(tsp + 1) * D], bot[:], ALU.add)
                        P5.inc(ins)
                block.gpsimd(_gp)

                def _sc(s, c=c):
                    for ts in range(4):
                        s.wait_ge(P5.h, p5_b + ts + 1)
                        s.dma_start(out=out_store[4 * c + ts],
                                    in_=osb[:, (ts % 2) * D:(ts % 2 + 1) * D]).then_inc(SO.h, 16)
                        SO.n += 16
                block.scalar(_sc)

            if debug:
                DBG = sem("DBG")

                def _dbg(s):
                    guard(s)
                    for nm, srcap in [("xn", xn_d), ("xq", xq_d), ("ffn", ffn_d), ("q", q_d),
                                      ("k", k_d), ("v", v_d), ("ret", ret_d), ("gate", gate_d)]:
                        s.dma_start(out=dbg[nm], in_=srcap).then_inc(DBG.h, 16)
                        DBG.n += 16
                    s.wait_ge(DBG.h, DBG.n)
                block.sync(_dbg)

            def _fin(s):
                s.wait_ge(SO.h, SO.n)
            block.scalar(_fin)

    return nc


def kernel(L0_attn_x, L0_attn_w_q, L0_attn_b_q, L0_attn_w_k, L0_attn_b_k,
           L0_attn_w_v, L0_attn_b_v, L0_attn_w_g, L0_attn_b_g,
           L0_attn_w_o, L0_attn_b_o, L0_attn_gn_weight, L0_attn_gn_bias,
           L0_ffn_weight, L0_ffn_w_scale, L0_ffn_bias):
    x = np.ascontiguousarray(np.asarray(L0_attn_x, dtype=np.float32))
    consts = _host_constants()

    def col16(v):
        return np.ascontiguousarray(np.asarray(v, np.float32).reshape(KC, 128).T)

    def bc(v):
        return np.ascontiguousarray(np.broadcast_to(np.asarray(v, np.float32).reshape(1, D), (128, D)))

    shared = {
        "w_q": np.ascontiguousarray(np.asarray(L0_attn_w_q, np.float32)),
        "w_k": np.ascontiguousarray(np.asarray(L0_attn_w_k, np.float32)),
        "w_v": np.ascontiguousarray(np.asarray(L0_attn_w_v, np.float32)),
        "w_g": np.ascontiguousarray(np.asarray(L0_attn_w_g, np.float32)),
        "w_o": np.ascontiguousarray(np.asarray(L0_attn_w_o, np.float32)),
        "ffw": np.ascontiguousarray(np.asarray(L0_ffn_weight, np.int8)),
        "bq_t": col16(L0_attn_b_q), "bk_t": col16(L0_attn_b_k), "bg_t": col16(L0_attn_b_g),
        "gnw_t": col16(L0_attn_gn_weight), "gnb_t": col16(L0_attn_gn_bias),
        "bv_bc": bc(L0_attn_b_v), "bo_bc": bc(L0_attn_b_o),
        "ws_bc": bc(np.asarray(L0_ffn_w_scale, np.float32)[:, 0]),
        "fb_bc": bc(L0_ffn_bias),
        "cosT": consts["cosT"], "sinT": consts["sinT"],
        "cosK": consts["cosK"], "sinK": consts["sinK"],
        "Mcomb": consts["Mcomb"], "Gfull": consts["Gfull"], "Gam": consts["Gam"],
        "RT": consts["RT"], "idf32": consts["idf32"], "idbf": consts["idbf"],
        "epsln": consts["epsln"],
    }
    nc = build_nc()
    in_maps = [dict(shared, x=np.ascontiguousarray(x[b])) for b in range(8)]
    res = run_bass_kernel_spmd(nc, in_maps, core_ids=list(range(8)))
    out = np.stack([np.asarray(res.results[b]["out"]) for b in range(8)], axis=0)
    return out


# version 11
# speedup vs baseline: 1.1059x; 1.0749x over previous
import numpy as np
import ml_dtypes
from contextlib import ExitStack

import concourse.bass as bass
import concourse.mybir as mybir
from concourse.bass_utils import run_bass_kernel_spmd

F32 = mybir.dt.float32
BF16 = mybir.dt.bfloat16
F16 = mybir.dt.float16
I8 = mybir.dt.int8
AF = mybir.ActivationFunctionType
ALU = mybir.AluOpType
AX = mybir.AxisListType
BF = ml_dtypes.bfloat16

N, D, H, DH, KC = 4096, 2048, 8, 256, 16
C0, NCH0, TS0 = 512, 8, 1
C1, NCH1 = 512, 8
C3, NCH3 = 256, 16


def _host_constants():
    d = {}
    xs = np.linspace(0.0, 1.0, DH // 2)
    thetas = np.repeat(1.0 / (10000.0 ** xs), 2)
    t = np.arange(N, dtype=np.float64)
    ang = t[None, :] * thetas[:, None]
    cosT = np.cos(ang).reshape(2, 128, N)
    sinT = np.sin(ang).reshape(2, 128, N)
    d["cosT"] = cosT.astype(np.float32)
    d["sinT"] = sinT.astype(np.float32)
    d["cosK"] = (cosT / 16.0).astype(np.float32)
    d["sinK"] = (sinT / 16.0).astype(np.float32)
    g = 1.0 - np.exp(np.linspace(np.log(1 / 32), np.log(1 / 512), H))
    p = np.arange(128)[:, None]
    m = np.arange(C3)[None, :]
    M = np.zeros((128, H * C3))
    for h in range(H):
        M[:, h * C3:(h + 1) * C3] = np.where(m >= p, g[h] ** (m - p), 0.0)
    d["Mfull"] = M.astype(np.float32)
    MC = np.zeros((128, H * 384))
    for h in range(H):
        MC[:, h * 384:h * 384 + C3] = M[:, h * C3:(h + 1) * C3]
        MC[:, h * 384 + C3:h * 384 + 384] = M[:, h * C3:h * C3 + 128]
    d["Mcomb"] = MC.astype(np.float32)
    GF = np.zeros((128, KC * C3))
    for h in range(H):
        for dk in range(2):
            GF[:, (2 * h + dk) * C3:(2 * h + dk + 1) * C3] = (g[h] ** (np.arange(C3) + 1.0))[None, :]
    d["Gfull"] = GF.astype(np.float32)
    GM = np.zeros((128, 2 * D))
    for ts in range(2):
        e = C3 - 1 - (ts * 128 + np.arange(128))
        for h in range(H):
            GM[:, ts * D + h * DH:ts * D + (h + 1) * DH] = (g[h] ** e)[:, None]
    d["Gam"] = GM.astype(np.float32)
    d["gC"] = [float(g[h] ** C3) for h in range(H)]
    RT = np.zeros((128, 128))
    i = np.arange(64)
    RT[2 * i + 1, 2 * i] = -1.0
    RT[2 * i, 2 * i + 1] = 1.0
    d["RT"] = RT.astype(BF)
    d["idf32"] = np.eye(128, dtype=np.float32)
    d["idbf"] = np.eye(128, dtype=BF)
    d["epsln"] = np.full((128, 1), 1e-5, np.float32)
    return d


class SemC:

    def __init__(self, nc, stack, name):
        self.h = stack.enter_context(nc.semaphore(name))
        self.n = 0

    def inc(self, instr, k=1):
        instr.then_inc(self.h, k)
        self.n += k
        return self.n

    def wait(self, eng, val=None):
        eng.wait_ge(self.h, self.n if val is None else val)


def build_nc(debug=False):
    nc = bass.Bass("TRN2", target_bir_lowering=False)

    x_d = nc.dram_tensor("x", [N, D], F32, kind="ExternalInput").ap()
    w_q = nc.dram_tensor("w_q", [D, D], F32, kind="ExternalInput").ap()
    w_k = nc.dram_tensor("w_k", [D, D], F32, kind="ExternalInput").ap()
    w_v = nc.dram_tensor("w_v", [D, D], F32, kind="ExternalInput").ap()
    w_g = nc.dram_tensor("w_g", [D, D], F32, kind="ExternalInput").ap()
    w_o = nc.dram_tensor("w_o", [D, D], F32, kind="ExternalInput").ap()
    ffw = nc.dram_tensor("ffw", [D, D], I8, kind="ExternalInput").ap()
    bq_t = nc.dram_tensor("bq_t", [128, KC], F32, kind="ExternalInput").ap()
    bk_t = nc.dram_tensor("bk_t", [128, KC], F32, kind="ExternalInput").ap()
    bg_t = nc.dram_tensor("bg_t", [128, KC], F32, kind="ExternalInput").ap()
    gnw_t = nc.dram_tensor("gnw_t", [128, KC], F32, kind="ExternalInput").ap()
    gnb_t = nc.dram_tensor("gnb_t", [128, KC], F32, kind="ExternalInput").ap()
    bv_bc = nc.dram_tensor("bv_bc", [128, D], F32, kind="ExternalInput").ap()
    bo_bc = nc.dram_tensor("bo_bc", [128, D], F32, kind="ExternalInput").ap()
    ws_bc = nc.dram_tensor("ws_bc", [128, D], F32, kind="ExternalInput").ap()
    fb_bc = nc.dram_tensor("fb_bc", [128, D], F32, kind="ExternalInput").ap()
    cosT = nc.dram_tensor("cosT", [2, 128, N], F32, kind="ExternalInput").ap()
    sinT = nc.dram_tensor("sinT", [2, 128, N], F32, kind="ExternalInput").ap()
    cosK = nc.dram_tensor("cosK", [2, 128, N], F32, kind="ExternalInput").ap()
    sinK = nc.dram_tensor("sinK", [2, 128, N], F32, kind="ExternalInput").ap()
    Mcomb = nc.dram_tensor("Mcomb", [128, H * 384], F32, kind="ExternalInput").ap()
    Gfull = nc.dram_tensor("Gfull", [128, KC * C3], F32, kind="ExternalInput").ap()
    Gam = nc.dram_tensor("Gam", [128, 2 * D], F32, kind="ExternalInput").ap()
    RT_d = nc.dram_tensor("RT", [128, 128], BF16, kind="ExternalInput").ap()
    idf_d = nc.dram_tensor("idf32", [128, 128], F32, kind="ExternalInput").ap()
    idb_d = nc.dram_tensor("idbf", [128, 128], BF16, kind="ExternalInput").ap()
    eps_d = nc.dram_tensor("epsln", [128, 1], F32, kind="ExternalInput").ap()
    gC = None

    out_d = nc.dram_tensor("out", [N, D], F32, kind="ExternalOutput").ap()

    xn_d = nc.dram_tensor("xn_d", [N, D], BF16, kind="Internal").ap()
    xnT_d = nc.dram_tensor("xnT_d", [D, N], BF16, kind="Internal").ap()
    xq_d = nc.dram_tensor("xq_d", [N, D], BF16, kind="Internal").ap()
    ffn_d = nc.dram_tensor("ffn_d", [N, D], F16, kind="Internal").ap()
    ffw16_d = nc.dram_tensor("ffw16_d", [D, D], BF16, kind="Internal").ap()
    wq16_d = nc.dram_tensor("wq16_d", [D, D], BF16, kind="Internal").ap()
    wk16_d = nc.dram_tensor("wk16_d", [D, D], BF16, kind="Internal").ap()
    wv16_d = nc.dram_tensor("wv16_d", [D, D], BF16, kind="Internal").ap()
    wg16_d = nc.dram_tensor("wg16_d", [D, D], BF16, kind="Internal").ap()
    wo16_d = nc.dram_tensor("wo16_d", [D, D], BF16, kind="Internal").ap()
    q_d = nc.dram_tensor("q_d", [D, N], BF16, kind="Internal").ap()
    k_d = nc.dram_tensor("k_d", [D, N], BF16, kind="Internal").ap()
    v_d = nc.dram_tensor("v_d", [N, D], BF16, kind="Internal").ap()
    ret_d = nc.dram_tensor("ret_d", [N, D], BF16, kind="Internal").ap()
    gate_d = nc.dram_tensor("gate_d", [D, N], BF16, kind="Internal").ap()

    dbg = {}
    if debug:
        for nm, shape, dt in [("xn", [N, D], BF16), ("xq", [N, D], BF16),
                              ("ffn", [N, D], F16),
                              ("q", [D, N], BF16), ("k", [D, N], BF16),
                              ("v", [N, D], BF16), ("ret", [N, D], BF16),
                              ("gate", [D, N], BF16)]:
            dbg[nm] = nc.dram_tensor("dbg_" + nm, shape, dt, kind="ExternalOutput").ap()

    gC_list = _host_constants()["gC"]

    with ExitStack() as PS:
        sbuf = lambda name, shape, dtype: PS.enter_context(nc.sbuf_tensor(name, shape, dtype))
        psum = lambda name, shape, dtype: nc.psum_tensor(name, shape, dtype)

        idf = sbuf("idf", [128, 128], F32)
        idb = sbuf("idb", [128, 128], BF16)
        RTt = sbuf("RTt", [128, 128], BF16)
        epst = sbuf("epst", [128, 1], F32)

        block = PS.enter_context(nc.Block())
        sems = []

        def sem(name):
            s = SemC(nc, PS, name)
            sems.append(s)
            return s

        L = sem("L")
        SS = sem("SS")
        GQ = sem("GQ")
        WCV = sem("WCV")

        def guard(eng):
            for s in sems:
                if s.n > 0:
                    eng.wait_ge(s.h, s.n)
        with ExitStack() as ph:
            sb0 = lambda name, shape, dtype: ph.enter_context(nc.sbuf_tensor(name, shape, dtype))
            wT = sb0("wT", [128, KC * D], BF16)
            ws_t = sb0("ws_t", [128, D], F32)
            fb_t = sb0("fb_t", [128, D], F32)
            xt = sb0("xt", [128, 4 * D], F32)
            xq8 = sb0("xq8", [128, 4 * D], I8)
            xnb = sb0("xnb", [128, 2 * D], BF16)
            xqb = sb0("xqb", [128, 2 * D], BF16)
            ep1 = sb0("ep1", [128, 2 * D], F32)
            f16t = sb0("f16t", [128, 2 * D], F16)
            qTg = sb0("qTg", [128, KC * 512], BF16)
            sclg = sb0("sclg", [128, 16 * NCH0], F32)
            stat = sb0("stat", [128, 240], F32)
            ffp = [ph.enter_context(psum(f"ffp{i}", [128, D], F32)) for i in range(2)]

            MNEG, MX, MN, VR, LNV, RSTD = 0, 16, 32, 48, 64, 80
            A1, NMN, A2, AMAX, SCL0, RCP, MR, RR, MRR = 96, 112, 128, 144, 160, 176, 192, 208, 224

            def blk(b):
                return stat[:, b:b + 16]

            def col(b, su):
                return stat[:, b + 4 * su: b + 4 * su + 1]

            T0 = sem("T0")
            A0 = sem("A0")
            SV0 = sem("SV0")
            SA0 = sem("SA0")
            E1 = sem("E1")
            PF = sem("PF")

            def _sy(s):
                s.dma_start(out=idf[:], in_=idf_d).then_inc(L.h, 16)
                s.dma_start(out=idb[:], in_=idb_d).then_inc(L.h, 16)
                s.dma_start(out=RTt[:], in_=RT_d).then_inc(L.h, 16)
                s.dma_start(out=epst[:], in_=eps_d).then_inc(L.h, 16)
                s.dma_start(out=ws_t[:], in_=ws_bc).then_inc(L.h, 16)
                s.dma_start(out=fb_t[:], in_=fb_bc).then_inc(L.h, 16)
                L.n += 6 * 16
            block.sync(_sy)

            def _g0(g):
                g.dma_start(out=ffw16_d, in_=ffw).then_inc(WCV.h, 16)
                g.dma_start(out=wq16_d, in_=w_q).then_inc(WCV.h, 16)
                WCV.n += 32
            block.gpsimd(_g0)

            def _s0(s):
                s.wait_ge(WCV.h, 16)
                for kc in range(KC):
                    s.dma_start(out=wT[:, kc * D:(kc + 1) * D],
                                in_=ffw16_d[:, kc * 128:(kc + 1) * 128],
                                transpose=True).then_inc(SS.h, 16)
                SS.n += 16 * KC
            block.scalar(_s0)
            ss_wT = SS.n

            x_r = x_d.rearrange("(c p) d -> c p d", p=128)
            xn_store = xn_d.rearrange("(c p) d -> c p d", p=128)
            xq_store = xq_d.rearrange("(c p) d -> c p d", p=128)
            ffn_store = ffn_d.rearrange("(c p) d -> c p d", p=128)
            NV = NCH0 + 4
            Lsub, Lffn = {}, {}
            sv_mneg, sv_sub, svMr, svChain = {}, {}, {}, {}
            saRstd = {}
            a0q_sub, ssxn_sub, ssxq_sub, ssxq, ssf16 = {}, {}, {}, {}, {}

            for c in range(NV):
                real = c < NCH0
                g_mm = c - 3

                def _sy(s, c=c, real=real):
                    if real:
                        for su in range(4):
                            gs = 4 * c + su
                            if c >= 1:
                                s.wait_ge(A0.h, a0q_sub[gs - 4])
                            s.dma_start(out=xt[:, su * D:(su + 1) * D],
                                        in_=x_r[gs]).then_inc(L.h, 16)
                            L.n += 16
                            Lsub[gs] = L.n
                    g = c - 2
                    if 0 <= g < NCH0:
                        if g > 0:
                            s.wait_ge(T0.h, 4 * (g - 1) + 4)
                        else:
                            s.wait_ge(SS.h, ss_wT)
                        s.wait_ge(SS.h, ssxq[g])
                        for kc in range(KC):
                            s.dma_start(out=qTg[:, kc * 512:(kc + 1) * 512],
                                        in_=xq_d[g * 512:(g + 1) * 512, kc * 128:(kc + 1) * 128],
                                        transpose=True).then_inc(L.h, 16)
                        L.n += 16 * KC
                        Lffn[g] = L.n
                block.sync(_sy)

                def _ve(v, c=c, real=real, g=g_mm):
                    def e1v_emit(gg, j):
                        v.wait_ge(T0.h, 4 * gg + j + 1)
                        if j < 2:
                            if gg >= 1:
                                v.wait_ge(PF.h, 4 * (gg - 1) + j + 3)
                        else:
                            v.wait_ge(PF.h, 4 * gg + (j - 2) + 1)
                        ins = v.scalar_tensor_tensor(ep1[:, (j % 2) * D:(j % 2 + 1) * D],
                                                     ffp[j % 2][:],
                                                     sclg[:, 16 * gg + 4 * j:16 * gg + 4 * j + 1],
                                                     ws_t[:], ALU.mult, ALU.mult)
                        E1.inc(ins)
                        assert E1.n == 4 * gg + j + 1
                    gp = c - 4
                    if 0 <= gp < NCH0:
                        e1v_emit(gp, 2)
                        e1v_emit(gp, 3)
                    if real:
                        for su in range(4):
                            gs = 4 * c + su
                            v.wait_ge(L.h, Lsub[gs])
                            SV0.inc(v.tensor_reduce(col(MNEG, su), xt[:, su * D:(su + 1) * D],
                                                    AX.X, ALU.add))
                            v.wait_ge(SV0.h, SV0.n)
                            SV0.inc(v.tensor_scalar(col(MNEG, su), col(MNEG, su),
                                                    -1.0 / D, None, ALU.mult))
                            sv_mneg[gs] = SV0.n
                            SV0.inc(v.tensor_reduce(col(MX, su), xt[:, su * D:(su + 1) * D],
                                                    AX.X, ALU.max))
                            SV0.inc(v.tensor_reduce(col(MN, su), xt[:, su * D:(su + 1) * D],
                                                    AX.X, ALU.min))
                            sv_sub[gs] = SV0.n
                            if su == 1 and 0 <= g < NCH0:
                                e1v_emit(g, 0)
                            if su == 2 and 0 <= g < NCH0:
                                e1v_emit(g, 1)
                        v.wait_ge(SA0.h, 6 * c + 6)
                        v.wait_ge(SV0.h, sv_sub[4 * c + 3])
                        SV0.inc(v.tensor_tensor(blk(MR), blk(MNEG), blk(RSTD), ALU.mult))
                        svMr[c] = SV0.n
                        SV0.inc(v.tensor_tensor(blk(A1), blk(MX), blk(MNEG), ALU.add))
                        SV0.inc(v.tensor_scalar(blk(NMN), blk(MN), -1.0, None, ALU.mult))
                        v.wait_ge(SV0.h, SV0.n)
                        SV0.inc(v.tensor_tensor(blk(A2), blk(NMN), blk(MNEG), ALU.subtract))
                        v.wait_ge(SV0.h, SV0.n)
                        SV0.inc(v.tensor_tensor(blk(AMAX), blk(A1), blk(A2), ALU.max))
                        v.wait_ge(SV0.h, SV0.n)
                        SV0.inc(v.tensor_tensor(blk(SCL0), blk(AMAX), blk(RSTD), ALU.mult))
                        v.wait_ge(SV0.h, SV0.n)
                        SV0.inc(v.tensor_scalar(sclg[:, 16 * c:16 * c + 16], blk(SCL0),
                                                1e-7, 1.0 / 127.0, ALU.max, ALU.mult))
                        v.wait_ge(SV0.h, SV0.n)
                        SV0.inc(v.reciprocal(blk(RCP), sclg[:, 16 * c:16 * c + 16]))
                        v.wait_ge(SV0.h, SV0.n)
                        SV0.inc(v.tensor_tensor(blk(RR), blk(RSTD), blk(RCP), ALU.mult))
                        v.wait_ge(SV0.h, SV0.n)
                        SV0.inc(v.tensor_tensor(blk(MRR), blk(MNEG), blk(RR), ALU.mult))
                        svChain[c] = SV0.n
                    elif 0 <= g < NCH0:
                        e1v_emit(g, 0)
                        e1v_emit(g, 1)
                block.vector(_ve)

                def _se(s, c=c, real=real, g=g_mm):
                    if 1 <= g <= NCH0:
                        gp = g - 1
                        for j in (2, 3):
                            s.wait_ge(PF.h, 4 * gp + j + 1)
                            s.dma_start(out=ffn_store[4 * gp + j],
                                        in_=f16t[:, (j % 2) * D:(j % 2 + 1) * D]).then_inc(SS.h, 16)
                            SS.n += 16
                            ssf16[(gp, j)] = SS.n
                    if real:
                        if c >= 1:
                            s.wait_ge(SS.h, ssxq[c - 1])
                        for su in range(4):
                            gs = 4 * c + su
                            s.wait_ge(SV0.h, sv_mneg[gs])
                            SA0.inc(s.activation(xqb[:, (su % 2) * D:(su % 2 + 1) * D],
                                                 xt[:, su * D:(su + 1) * D], AF.Square,
                                                 bias=col(MNEG, su), scale=1.0,
                                                 accum_out=col(VR, su)))
                        s.wait_ge(SA0.h, SA0.n)
                        SA0.inc(s.activation(blk(LNV), blk(VR), AF.Ln,
                                             bias=epst[:], scale=1.0 / D))
                        s.wait_ge(SA0.h, SA0.n)
                        SA0.inc(s.activation(blk(RSTD), blk(LNV), AF.Exp, bias=0.0, scale=-0.5))
                        saRstd[c] = SA0.n
                        assert SA0.n == 6 * c + 6
                        s.wait_ge(SV0.h, svMr[c])
                        for su in range(4):
                            gs = 4 * c + su
                            if gs >= 2:
                                s.wait_ge(SS.h, ssxn_sub[gs - 2])
                            A0.inc(s.activation(xnb[:, (su % 2) * D:(su % 2 + 1) * D],
                                                xt[:, su * D:(su + 1) * D], AF.Identity,
                                                bias=col(MR, su), scale=col(RSTD, su)))
                            s.wait_ge(A0.h, A0.n)
                            s.dma_start(out=xn_store[gs],
                                        in_=xnb[:, (su % 2) * D:(su % 2 + 1) * D]).then_inc(SS.h, 16)
                            SS.n += 16
                            ssxn_sub[gs] = SS.n
                    if 0 <= g < NCH0:
                        for j in (0, 1):
                            s.wait_ge(PF.h, 4 * g + j + 1)
                            s.dma_start(out=ffn_store[4 * g + j],
                                        in_=f16t[:, j * D:(j + 1) * D]).then_inc(SS.h, 16)
                            SS.n += 16
                            ssf16[(g, j)] = SS.n
                block.scalar(_se)

                def _te(t, c=c, g=g_mm):
                    if 0 <= g < NCH0:
                        if g == 0:
                            t.wait_ge(SS.h, ss_wT)
                        t.wait_ge(L.h, Lffn[g])
                        for j in range(4):
                            if j < 2:
                                if g >= 1:
                                    t.wait_ge(E1.h, 4 * (g - 1) + j + 3)
                            else:
                                t.wait_ge(E1.h, 4 * g + (j - 2) + 1)
                            for nf in range(4):
                                for kc in range(KC):
                                    ins = t.matmul(ffp[j % 2][:, nf * 512:(nf + 1) * 512],
                                                   qTg[:, kc * 512 + j * 128: kc * 512 + (j + 1) * 128],
                                                   wT[:, kc * D + nf * 512: kc * D + (nf + 1) * 512],
                                                   start=(kc == 0), stop=(kc == KC - 1))
                            T0.inc(ins)
                            assert T0.n == 4 * g + j + 1
                block.tensor(_te)

                def _gp(g2, c=c, g=g_mm):
                    if 0 <= g < NCH0:
                        for j in range(4):
                            g2.wait_ge(E1.h, 4 * g + j + 1)
                            if j < 2:
                                if g >= 1:
                                    g2.wait_ge(SS.h, ssf16[(g - 1, j + 2)])
                            else:
                                g2.wait_ge(SS.h, ssf16[(g, j - 2)])
                            PF.inc(g2.tensor_tensor(f16t[:, (j % 2) * D:(j % 2 + 1) * D],
                                                    ep1[:, (j % 2) * D:(j % 2 + 1) * D],
                                                    fb_t[:], ALU.add))
                            assert PF.n == 4 * g + j + 1
                block.gpsimd(_gp)

                def _sl(s, c=c, real=real):
                    if real:
                        s.wait_ge(SV0.h, svChain[c])
                        for su in range(4):
                            gs = 4 * c + su
                            A0.inc(s.activation(xq8[:, su * D:(su + 1) * D],
                                                xt[:, su * D:(su + 1) * D], AF.Identity,
                                                bias=col(MRR, su), scale=col(RR, su)))
                            a0q_sub[gs] = A0.n
                        for su in range(4):
                            gs = 4 * c + su
                            if gs >= 2:
                                s.wait_ge(SS.h, ssxq_sub[gs - 2])
                            A0.inc(s.activation(xqb[:, (su % 2) * D:(su % 2 + 1) * D],
                                                xq8[:, su * D:(su + 1) * D], AF.Identity,
                                                bias=0.0, scale=1.0))
                            s.wait_ge(A0.h, A0.n)
                            s.dma_start(out=xq_store[gs],
                                        in_=xqb[:, (su % 2) * D:(su % 2 + 1) * D]).then_inc(SS.h, 16)
                            SS.n += 16
                            ssxq_sub[gs] = SS.n
                        ssxq[c] = SS.n
                block.scalar(_sl)

        def proj_rope_phase(tag, w_ap, bias_ap, cos_ap, sin_ap, out_store, conv=None):
            with ExitStack() as ph:
                sb1 = lambda name, shape, dtype: ph.enter_context(nc.sbuf_tensor(tag + name, shape, dtype))
                wqb = sb1("wqb", [128, KC * D], BF16)
                bqt = sb1("bqt", [128, KC], F32)
                xnTs = sb1("xnTs", [128, 2 * KC * C1], BF16)
                cosS = sb1("cosS", [128, 2 * 2 * C1], F32)
                sinS = sb1("sinS", [128, 2 * 2 * C1], F32)
                q1 = sb1("q1", [128, KC * C1], BF16)
                qh = sb1("qh", [128, 2 * KC * C1], BF16)
                rot = sb1("rot", [128, 2 * C1], BF16)
                t1b = sb1("t1b", [128, 2 * C1], BF16)
                qps = [ph.enter_context(psum(tag + f"qps{i}", [128, C1], F32)) for i in range(2)]
                rps = [ph.enter_context(psum(tag + f"rps{i}", [128, C1], F32)) for i in range(2)]

                Tp = sem(tag + "Tp")
                Tr = sem(tag + "Tr")
                Ap = sem(tag + "Ap")
                Vp = sem(tag + "Vp")
                Pp = sem(tag + "Pp")

                def _g1(g):
                    guard(g)
                    g.dma_start(out=wqb.rearrange("p (kc m) -> p kc m", kc=KC),
                                in_=w_ap.rearrange("(kc p) m -> p kc m", p=128)).then_inc(GQ.h, 16)
                    GQ.n += 16
                    if conv is not None:
                        g.dma_start(out=conv[1], in_=conv[0]).then_inc(WCV.h, 16)
                        WCV.n += 16
                block.gpsimd(_g1)

                def _s1(s):
                    guard(s)
                    s.dma_start(out=bqt[:], in_=bias_ap).then_inc(SS.h, 16)
                    SS.n += 16
                block.scalar(_s1)

                tp_e, ap_e, vp_e, pp_e, l_e, ss_e = [], [], [], [], [], []
                KCC = KC * C1
                for c in range(NCH1):
                    pp_ = c % 2
                    xsl = slice(pp_ * KCC, (pp_ + 1) * KCC)
                    csl_base = pp_ * 2 * C1

                    def _sy(s, c=c):
                        if c == 0:
                            guard(s)
                        elif c >= 2:
                            s.wait_ge(Tp.h, tp_e[c - 2])
                            s.wait_ge(SS.h, ss_e[c - 2])
                            s.wait_ge(Pp.h, pp_e[c - 2])
                            s.wait_ge(Vp.h, vp_e[c - 2])
                        if tag == "P1q":
                            for kc in range(KC):
                                s.dma_start(out=xnTs[:, pp_ * KCC + kc * C1:(pp_ * KCC + (kc + 1) * C1)],
                                            in_=xn_d[c * C1:(c + 1) * C1, kc * 128:(kc + 1) * 128],
                                            transpose=True).then_inc(L.h, 16)
                        else:
                            s.dma_start(out=xnTs[:, pp_ * KCC:(pp_ + 1) * KCC].rearrange("p (kc t) -> p kc t", kc=KC),
                                        in_=xnT_d.rearrange("(kc p) t -> p kc t", p=128)[:, :, c * C1:(c + 1) * C1]).then_inc(L.h, 16)
                            L.n -= 16 * (KC - 1)
                        s.dma_start(out=cosS[:, csl_base:csl_base + 2 * C1].rearrange("p (dk t) -> p dk t", dk=2),
                                    in_=cos_ap[:, :, c * C1:(c + 1) * C1].rearrange("dk p t -> p dk t")).then_inc(L.h, 16)
                        s.dma_start(out=sinS[:, csl_base:csl_base + 2 * C1].rearrange("p (dk t) -> p dk t", dk=2),
                                    in_=sin_ap[:, :, c * C1:(c + 1) * C1].rearrange("dk p t -> p dk t")).then_inc(L.h, 16)
                        L.n += 16 * (KC + 2)
                        l_e.append(L.n)
                    block.sync(_sy)

                    tp_b, tr_b, ap_b, vp_b, pp_b = Tp.n, Tr.n, Ap.n, Vp.n, Pp.n

                    def _te(t, c=c):
                        t.wait_ge(L.h, l_e[c])
                        if c == 0:
                            t.wait_ge(GQ.h, GQ.n)
                            t.wait_ge(SS.h, SS.n)
                        else:
                            t.wait_ge(Ap.h, ap_b)
                            t.wait_ge(Vp.h, vp_b)
                        for mm in range(KC):
                            if mm >= 2:
                                t.wait_ge(Ap.h, ap_b + mm - 1)
                            for kc in range(KC):
                                ins = t.matmul(qps[mm % 2][:],
                                               wqb[:, kc * D + mm * 128: kc * D + (mm + 1) * 128],
                                               xnTs[:, pp_ * KCC + kc * C1: pp_ * KCC + (kc + 1) * C1],
                                               start=(kc == 0), stop=(kc == KC - 1))
                            Tp.inc(ins)
                            if mm >= 1:
                                r = mm - 1
                                t.wait_ge(Ap.h, ap_b + r + 1)
                                if r >= 2:
                                    t.wait_ge(Vp.h, vp_b + r - 1)
                                ins = t.matmul(rps[r % 2][:], RTt[:], q1[:, r * C1:(r + 1) * C1],
                                               start=True, stop=True)
                                Tr.inc(ins)
                        r = KC - 1
                        t.wait_ge(Ap.h, ap_b + r + 1)
                        t.wait_ge(Vp.h, vp_b + r - 1)
                        ins = t.matmul(rps[r % 2][:], RTt[:], q1[:, r * C1:(r + 1) * C1],
                                       start=True, stop=True)
                        Tr.inc(ins)
                    block.tensor(_te)

                    def _sc(s, c=c):
                        if c == 0:
                            s.wait_ge(SS.h, SS.n)
                        for mm in range(KC):
                            s.wait_ge(Tp.h, tp_b + mm + 1)
                            if c > 0 and mm == 0:
                                s.wait_ge(Pp.h, pp_b)
                            ins = s.activation(q1[:, mm * C1:(mm + 1) * C1], qps[mm % 2][:],
                                               AF.Identity, bias=bqt[:, mm:mm + 1], scale=1.0)
                            Ap.inc(ins)
                    block.scalar(_sc)

                    def _ve(v, c=c):
                        if c > 0:
                            v.wait_ge(Pp.h, pp_b)
                        for mm in range(KC):
                            v.wait_ge(Tr.h, tr_b + mm + 1)
                            if mm >= 2:
                                v.wait_ge(Pp.h, pp_b + mm - 1)
                            dk = mm % 2
                            v.tensor_tensor(rot[:, dk * C1:(dk + 1) * C1], rps[mm % 2][:],
                                            sinS[:, csl_base + dk * C1: csl_base + (dk + 1) * C1], ALU.mult)
                            ins = v.tensor_tensor(t1b[:, dk * C1:(dk + 1) * C1], q1[:, mm * C1:(mm + 1) * C1],
                                                  cosS[:, csl_base + dk * C1: csl_base + (dk + 1) * C1], ALU.mult)
                            Vp.inc(ins)
                    block.vector(_ve)

                    def _gp(g, c=c):
                        if c >= 2:
                            g.wait_ge(SS.h, ss_e[c - 2])
                        for mm in range(KC):
                            dk = mm % 2
                            g.wait_ge(Vp.h, vp_b + mm + 1)
                            ins = g.tensor_tensor(qh[:, pp_ * KCC + mm * C1: pp_ * KCC + (mm + 1) * C1],
                                                  t1b[:, dk * C1:(dk + 1) * C1],
                                                  rot[:, dk * C1:(dk + 1) * C1], ALU.add)
                            Pp.inc(ins)
                    block.gpsimd(_gp)

                    def _sc(s, c=c):
                        if tag == "P1q":
                            s.wait_ge(L.h, l_e[c])
                            s.dma_start(out=xnT_d.rearrange("(kc p) t -> p kc t", p=128)[:, :, c * C1:(c + 1) * C1],
                                        in_=xnTs[:, xsl].rearrange("p (kc t) -> p kc t", kc=KC)).then_inc(SS.h, 16)
                            SS.n += 16
                        s.wait_ge(Pp.h, pp_b + KC)
                        s.dma_start(out=out_store[:, :, c * C1:(c + 1) * C1],
                                    in_=qh[:, xsl].rearrange("p (kc t) -> p kc t", kc=KC)).then_inc(SS.h, 16)
                        SS.n += 16
                        ss_e.append(SS.n)
                    block.scalar(_sc)

                    tp_e.append(Tp.n)
                    ap_e.append(Ap.n)
                    vp_e.append(Vp.n)
                    pp_e.append(Pp.n)

        q_store = q_d.rearrange("(kc p) t -> p kc t", p=128)
        k_store = k_d.rearrange("(kc p) t -> p kc t", p=128)
        proj_rope_phase("P1q", wq16_d, bq_t, cosT, sinT, q_store, conv=(w_k, wk16_d))
        proj_rope_phase("P1k", wk16_d, bk_t, cosK, sinK, k_store, conv=(w_v, wv16_d))

        with ExitStack() as ph:
            sb2 = lambda name, shape, dtype: ph.enter_context(nc.sbuf_tensor("P2" + name, shape, dtype))
            wvb = sb2("wvb", [128, KC * D], BF16)
            bvt = sb2("bvt", [128, D], F32)
            xnTs = sb2("xnTs", [128, 2 * KC * C1], BF16)
            vsb = sb2("vsb", [128, 2 * D], BF16)
            vps = [ph.enter_context(psum(f"vps{i}", [128, D], F32)) for i in range(2)]

            T2 = sem("T2")
            V2 = sem("V2")

            def _g1(g):
                guard(g)
                g.dma_start(out=wvb.rearrange("p (kc m) -> p kc m", kc=KC),
                            in_=wv16_d.rearrange("(kc p) m -> p kc m", p=128)).then_inc(GQ.h, 16)
                GQ.n += 16
                g.dma_start(out=wg16_d, in_=w_g).then_inc(WCV.h, 16)
                WCV.n += 16
            block.gpsimd(_g1)

            def _s1(s):
                guard(s)
                s.dma_start(out=bvt[:], in_=bv_bc).then_inc(SS.h, 16)
                SS.n += 16
            block.scalar(_s1)

            v_store = v_d.rearrange("(r p) d -> r p d", p=128)
            KCC = KC * C1
            t2_e, l2_e = [], []
            for c in range(NCH1):
                pp_ = c % 2
                def _sy(s, c=c):
                    if c == 0:
                        guard(s)
                    elif c >= 2:
                        s.wait_ge(T2.h, t2_e[c - 2])
                    s.dma_start(out=xnTs[:, pp_ * KCC:(pp_ + 1) * KCC].rearrange("p (kc t) -> p kc t", kc=KC),
                                in_=xnT_d.rearrange("(kc p) t -> p kc t", p=128)[:, :, c * C1:(c + 1) * C1]).then_inc(L.h, 16)
                    L.n += 16
                    l2_e.append(L.n)
                block.sync(_sy)

                t2_b, v2_b, ss_b = T2.n, V2.n, SS.n

                def _te(t, c=c):
                    t.wait_ge(L.h, l2_e[c])
                    if c == 0:
                        t.wait_ge(GQ.h, GQ.n)
                        t.wait_ge(SS.h, SS.n)
                    else:
                        t.wait_ge(V2.h, v2_b)
                    for ts in range(4):
                        if ts >= 2:
                            t.wait_ge(V2.h, v2_b + ts - 1)
                        for nf in range(4):
                            for kc in range(KC):
                                ins = t.matmul(vps[ts % 2][:, nf * 512:(nf + 1) * 512],
                                               xnTs[:, pp_ * KCC + kc * C1 + ts * 128: pp_ * KCC + kc * C1 + (ts + 1) * 128],
                                               wvb[:, kc * D + nf * 512: kc * D + (nf + 1) * 512],
                                               start=(kc == 0), stop=(kc == KC - 1))
                        T2.inc(ins)
                    t2_e.append(T2.n)
                block.tensor(_te)

                def _ve(v, c=c):
                    v.wait_ge(SS.h, ss_b if c > 0 else SS.n)
                    for ts in range(4):
                        v.wait_ge(T2.h, t2_b + ts + 1)
                        if ts >= 2:
                            v.wait_ge(SS.h, ss_b + 16 * (ts - 1))
                        ins = v.scalar_tensor_tensor(vsb[:, (ts % 2) * D:(ts % 2 + 1) * D],
                                                     vps[ts % 2][:], 1.0, bvt[:], ALU.mult, ALU.add)
                        V2.inc(ins)
                block.vector(_ve)

                def _sc(s, c=c):
                    for ts in range(4):
                        s.wait_ge(V2.h, v2_b + ts + 1)
                        s.dma_start(out=v_store[c * 4 + ts],
                                    in_=vsb[:, (ts % 2) * D:(ts % 2 + 1) * D]).then_inc(SS.h, 16)
                        SS.n += 16
                block.scalar(_sc)

        def _gconv(g):
            g.dma_start(out=wo16_d, in_=w_o).then_inc(WCV.h, 16)
            WCV.n += 16
        block.gpsimd(_gconv)
        with ExitStack() as ph:
            sb3 = lambda name, shape, dtype: ph.enter_context(nc.sbuf_tensor("P3" + name, shape, dtype))
            KCC3 = KC * C3
            qsb = sb3("qsb", [128, 2 * KCC3], BF16)
            ksb = sb3("ksb", [128, 2 * KCC3], BF16)
            vsb = sb3("vsb", [128, 2 * 2 * D], BF16)
            ktok = sb3("ktok", [128, 2 * D], BF16)
            ktg = sb3("ktg", [128, 2 * D], BF16)
            qg = sb3("qg", [128, KCC3], BF16)
            Mct = sb3("Mct", [128, H * 384], F32)
            Gt = sb3("Gt", [128, KCC3], F32)
            Gmt = sb3("Gmt", [128, 2 * D], F32)
            Ssb = sb3("Ssb", [128, KC * DH], F32)
            Sbf = sb3("Sbf", [128, KC * DH], BF16)
            simb = sb3("simb", [128, 2 * 384], BF16)
            retb = sb3("retb", [128, 2 * 2 * D], BF16)
            rtb = sb3("rtb", [128, 2 * 2 * D], BF16)
            scrB = sb3("scrB", [128, 2 * D], F32)
            gsum = sb3("gsum", [128, 16], F32)
            gmneg = sb3("gmneg", [128, 16], F32)
            gmsq = sb3("gmsq", [128, 16], F32)
            gsq = sb3("gsq", [128, 16], F32)
            gvar = sb3("gvar", [128, 16], F32)
            grs = sb3("grs", [128, 16], F32)

            simc = [ph.enter_context(psum(f"simc{i}", [128, 384], F32)) for i in range(2)]
            ret_ps = [[ph.enter_context(psum(f"ret{i}{j}", [128, DH], F32)) for j in range(2)]
                      for i in range(2)]
            upd_ps = [ph.enter_context(psum(f"upd{i}", [128, 2 * DH], F32)) for i in range(2)]

            T3s = sem("T3s")
            T3r = sem("T3r")
            T3u = sem("T3u")
            V3s = sem("V3s")
            V3u = sem("V3u")
            A3r = sem("A3r")
            P3p = sem("P3p")
            V3g = sem("V3g")
            A3g = sem("A3g")
            P3q = sem("P3q")
            P3g = sem("P3g")
            SV3 = sem("SV3")
            SP3 = sem("SP3")
            V3p = sem("V3p")

            def _sy(s):
                guard(s)
                s.dma_start(out=Mct[:], in_=Mcomb).then_inc(L.h, 16)
                s.dma_start(out=Gt[:], in_=Gfull).then_inc(L.h, 16)
                s.dma_start(out=Gmt[:], in_=Gam).then_inc(L.h, 16)
                L.n += 48
            block.sync(_sy)

            q_src = q_d.rearrange("(kc p) t -> p kc t", p=128)
            k_src = k_d.rearrange("(kc p) t -> p kc t", p=128)
            v_src = v_d.rearrange("(r p) d -> r p d", p=128)
            ret_store = ret_d.rearrange("(r p) d -> r p d", p=128)

            t3s_e, t3r_e, t3u_e, v3s_e, v3u_e, a3r_e = [], [], [], [], [], []
            p3p_e, l3_e, k3_e, p3g_e, ss3_e = [], [], [], [], []

            def P3_emit_gn(c):
                pp = c % 2
                rb = pp * 2 * D

                def _sq(s, c=c):
                    s.wait_ge(V3g.h, V3g.n)
                    for i in range(2):
                        ins = s.activation(scrB[:, i * D:(i + 1) * D],
                                           retb[:, rb + i * D: rb + (i + 1) * D], AF.Square)
                        A3g.inc(ins)
                block.scalar(_sq)
                a3g_b = A3g.n - 2

                def _vg(v, c=c):
                    for i in range(2):
                        v.wait_ge(A3r.h, a3r_e[c])
                        if c >= 2:
                            v.wait_ge(P3q.h, P3q.n)
                        SV3.inc(v.tensor_reduce(gsum[:, i * 8:(i + 1) * 8].rearrange("p (h o) -> p h o", o=1),
                                                retb[:, rb + i * D: rb + (i + 1) * D].rearrange("p (h d) -> p h d", h=H),
                                                AX.X, ALU.add))
                        v.wait_ge(A3g.h, a3g_b + i + 1)
                        ins = v.tensor_reduce(gsq[:, i * 8:(i + 1) * 8].rearrange("p (h o) -> p h o", o=1),
                                              scrB[:, i * D:(i + 1) * D].rearrange("p (h d) -> p h d", h=H),
                                              AX.X, ALU.add)
                        V3g.inc(ins)
                block.vector(_vg)
                v3g_b = V3g.n - 2

                def _pm(g, c=c):
                    for i in range(2):
                        g.wait_ge(V3g.h, v3g_b + i + 1)
                        SP3.inc(g.tensor_scalar(gmneg[:, i * 8:(i + 1) * 8], gsum[:, i * 8:(i + 1) * 8],
                                                -1.0 / DH, None, ALU.mult))
                        g.wait_ge(SP3.h, SP3.n)
                        SP3.inc(g.tensor_tensor(gmsq[:, i * 8:(i + 1) * 8], gmneg[:, i * 8:(i + 1) * 8],
                                                gmneg[:, i * 8:(i + 1) * 8], ALU.mult))
                        g.wait_ge(SP3.h, SP3.n)
                        SP3.inc(g.tensor_scalar(gvar[:, i * 8:(i + 1) * 8], gsq[:, i * 8:(i + 1) * 8],
                                                1.0 / DH, None, ALU.mult))
                        g.wait_ge(SP3.h, SP3.n)
                        ins = g.tensor_tensor(gvar[:, i * 8:(i + 1) * 8], gvar[:, i * 8:(i + 1) * 8],
                                              gmsq[:, i * 8:(i + 1) * 8], ALU.subtract)
                        P3q.inc(ins)
                block.gpsimd(_pm)
                p3q_b = P3q.n - 2

                def _sd(s, c=c):
                    for i in range(2):
                        s.wait_ge(P3q.h, p3q_b + i + 1)
                        SA3g = s.activation(gvar[:, i * 8:(i + 1) * 8], gvar[:, i * 8:(i + 1) * 8],
                                            AF.Ln, bias=epst[:], scale=1.0)
                        A3g.inc(SA3g)
                        s.wait_ge(A3g.h, A3g.n)
                        ins = s.activation(grs[:, i * 8:(i + 1) * 8], gvar[:, i * 8:(i + 1) * 8],
                                           AF.Exp, bias=0.0, scale=-0.5)
                        A3g.inc(ins)
                block.scalar(_sd)
                a3g_b2 = A3g.n - 4

                def _pn(g, c=c):
                    for i in range(2):
                        g.wait_ge(A3g.h, a3g_b2 + 2 * (i + 1))
                        if c >= 2:
                            g.wait_ge(SS.h, ss3_e[c - 2])
                        for h in range(H):
                            ins = g.tensor_scalar(rtb[:, rb + i * D + h * DH: rb + i * D + (h + 1) * DH],
                                                  retb[:, rb + i * D + h * DH: rb + i * D + (h + 1) * DH],
                                                  gmneg[:, i * 8 + h: i * 8 + h + 1],
                                                  grs[:, i * 8 + h: i * 8 + h + 1],
                                                  ALU.add, ALU.mult)
                            P3g.inc(ins)
                block.gpsimd(_pn)
                p3g_e.append(P3g.n)

                def _st(s, c=c):
                    for i in range(2):
                        s.wait_ge(P3g.h, p3g_e[-1] - (1 - i) * 8)
                        s.dma_start(out=ret_store[2 * c + i],
                                    in_=rtb[:, rb + i * D: rb + (i + 1) * D]).then_inc(SS.h, 16)
                        SS.n += 16
                    ss3_e.append(SS.n)
                block.scalar(_st)

            for c in range(NCH3):
                pp = c % 2
                qb = pp * KCC3
                vb = pp * 2 * D
                rb = pp * 2 * D

                def _sy(s, c=c):
                    if c == 0:
                        guard(s)
                    elif c >= 2:
                        s.wait_ge(T3s.h, t3s_e[c - 2])
                        s.wait_ge(T3r.h, t3r_e[c - 2])
                        s.wait_ge(T3u.h, t3u_e[c - 2])
                        s.wait_ge(P3p.h, p3p_e[c - 2])
                    s.dma_start(out=qsb[:, qb:qb + KCC3].rearrange("p (kc t) -> p kc t", kc=KC),
                                in_=q_src[:, :, c * C3:(c + 1) * C3]).then_inc(L.h, 16)
                    s.dma_start(out=ksb[:, qb:qb + KCC3].rearrange("p (kc t) -> p kc t", kc=KC),
                                in_=k_src[:, :, c * C3:(c + 1) * C3]).then_inc(L.h, 16)
                    s.dma_start(out=vsb[:, vb:vb + 2 * D].rearrange("p (r d) -> p r d", r=2),
                                in_=v_src[2 * c: 2 * c + 2].rearrange("r p d -> p r d")).then_inc(L.h, 16)
                    L.n += 48
                    l3_e.append(L.n)
                block.sync(_sy)

                def _sck(s, c=c):
                    if c == 0:
                        guard(s)
                    else:
                        s.wait_ge(P3p.h, p3p_e[c - 1])
                    for ts in range(2):
                        s.dma_start(out=ktok[:, ts * D:(ts + 1) * D],
                                    in_=k_d[:, c * C3 + ts * 128: c * C3 + (ts + 1) * 128],
                                    transpose=True).then_inc(SS.h, 16)
                    SS.n += 32
                    k3_e.append(SS.n)
                block.scalar(_sck)

                p3_b = P3p.n

                def _gp(g, c=c):
                    g.wait_ge(SS.h, k3_e[c])
                    if c > 0:
                        g.wait_ge(T3u.h, t3u_e[c - 1])
                    P3p.inc(g.tensor_tensor(ktg[:], ktok[:], Gmt[:], ALU.mult))
                    p3p_e.append(P3p.n)
                block.gpsimd(_gp)

                t3s_b, t3r_b, t3u_b = T3s.n, T3r.n, T3u.n
                v3s_b, v3u_b, a3r_b = V3s.n, V3u.n, A3r.n
                v3p_b = V3p.n

                def _te(t, c=c):
                    t.wait_ge(L.h, l3_e[c])
                    for h in range(H):
                        hp = h % 2
                        if h > 1:
                            t.wait_ge(V3s.h, v3s_b + h - 1)
                        elif c > 0:
                            t.wait_ge(V3s.h, v3s_b)
                        for dk in range(2):
                            blk = 2 * h + dk
                            t.matmul(simc[hp][:, 0:C3],
                                     ksb[:, qb + blk * C3: qb + blk * C3 + 128],
                                     qsb[:, qb + blk * C3: qb + blk * C3 + C3],
                                     start=(dk == 0), stop=(dk == 1))
                        for dk in range(2):
                            blk = 2 * h + dk
                            ins = t.matmul(simc[hp][:, 256:384],
                                           ksb[:, qb + blk * C3 + 128: qb + blk * C3 + 256],
                                           qsb[:, qb + blk * C3 + 128: qb + blk * C3 + 256],
                                           start=(dk == 0), stop=(dk == 1))
                        T3s.inc(ins)
                        if h > 1:
                            t.wait_ge(A3r.h, a3r_b + 2 * (h - 1))
                        elif c > 0:
                            t.wait_ge(A3r.h, a3r_b)
                        if c > 0 and h == 0:
                            t.wait_ge(V3p.h, v3p_b + 2)
                        t.wait_ge(V3s.h, v3s_b + h + 1)
                        for i in range(2):
                            ops = [(simb[:, hp * 384 + i * 128: hp * 384 + (i + 1) * 128],
                                    vsb[:, vb + 0 * D + h * DH: vb + 0 * D + (h + 1) * DH])]
                            if i == 1:
                                ops.append((simb[:, hp * 384 + 2 * 128: hp * 384 + 3 * 128],
                                            vsb[:, vb + 1 * D + h * DH: vb + 1 * D + (h + 1) * DH]))
                            if c > 0:
                                for dk in range(2):
                                    blk = 2 * h + dk
                                    ops.append((qg[:, blk * C3 + i * 128: blk * C3 + (i + 1) * 128],
                                                Sbf[:, blk * DH:(blk + 1) * DH]))
                            for oi, (lh, rh) in enumerate(ops):
                                ins = t.matmul(ret_ps[i][hp][:], lh, rh,
                                               start=(oi == 0), stop=(oi == len(ops) - 1))
                            T3r.inc(ins)
                    t.wait_ge(P3p.h, p3_b + 1)
                    for h in range(H):
                        hp = h % 2
                        if h > 1:
                            t.wait_ge(V3u.h, v3u_b + h - 1)
                        elif c > 0:
                            t.wait_ge(V3u.h, v3u_b)
                        for dk in range(2):
                            for ts in range(2):
                                ins = t.matmul(upd_ps[hp][:, dk * DH:(dk + 1) * DH],
                                               ktg[:, ts * D + h * DH + dk * 128: ts * D + h * DH + (dk + 1) * 128],
                                               vsb[:, vb + ts * D + h * DH: vb + ts * D + (h + 1) * DH],
                                               start=(ts == 0), stop=(ts == 1))
                        T3u.inc(ins)
                    t3s_e.append(T3s.n)
                    t3r_e.append(T3r.n)
                    t3u_e.append(T3u.n)
                block.tensor(_te)

                def _ve(v, c=c):
                    v.wait_ge(L.h, l3_e[c])
                    if c > 0:
                        v.wait_ge(V3u.h, v3u_e[c - 1])
                        v.wait_ge(T3r.h, t3r_e[c - 1])
                        V3p.inc(v.tensor_copy(Sbf[:], Ssb[:]))
                        V3p.inc(v.tensor_tensor(qg[:], qsb[:, qb:qb + KCC3], Gt[:], ALU.mult))
                    for h in range(H):
                        hp = h % 2
                        v.wait_ge(T3s.h, t3s_b + h + 1)
                        if h > 1:
                            v.wait_ge(T3r.h, t3r_b + 2 * (h - 1))
                        elif c > 0:
                            v.wait_ge(T3r.h, t3r_b)
                        ins = v.tensor_tensor(simb[:, hp * 384:(hp + 1) * 384], simc[hp][:],
                                              Mct[:, h * 384:(h + 1) * 384], ALU.mult)
                        V3s.inc(ins)
                    for h in range(H):
                        hp = h % 2
                        v.wait_ge(T3u.h, t3u_b + h + 1)
                        if c == 0:
                            ins = v.tensor_copy(Ssb[:, 2 * h * DH:(2 * h + 2) * DH], upd_ps[hp][:])
                        else:
                            ins = v.scalar_tensor_tensor(Ssb[:, 2 * h * DH:(2 * h + 2) * DH],
                                                         Ssb[:, 2 * h * DH:(2 * h + 2) * DH],
                                                         gC_list[h], upd_ps[hp][:],
                                                         ALU.mult, ALU.add)
                        V3u.inc(ins)
                    v3s_e.append(V3s.n)
                    v3u_e.append(V3u.n)
                block.vector(_ve)

                def _sc(s, c=c):
                    for h in range(H):
                        hp = h % 2
                        for i in range(2):
                            s.wait_ge(T3r.h, t3r_b + 2 * h + i + 1)
                            if c >= 2 and h == 0 and i == 0:
                                s.wait_ge(P3g.h, p3g_e[c - 2])
                            A3r.inc(s.copy(retb[:, rb + i * D + h * DH: rb + i * D + (h + 1) * DH],
                                           ret_ps[i][hp][:]))
                    a3r_e.append(A3r.n)
                block.scalar(_sc)

                if c >= 1:
                    P3_emit_gn(c - 1)

            P3_emit_gn(NCH3 - 1)

        with ExitStack() as ph:
            sb4 = lambda name, shape, dtype: ph.enter_context(nc.sbuf_tensor("P4g" + name, shape, dtype))
            wgb = sb4("wgb", [128, KC * D], BF16)
            bgt = sb4("bgt", [128, KC], F32)
            bgtn = sb4("bgtn", [128, KC], F32)
            g1b = sb4("g1b", [128, 2 * C1], BF16)
            teb = sb4("teb", [128, 2 * C1], F32)
            srb = sb4("srb", [128, 2 * C1], F32)
            gnwt = sb4("gnwt", [128, KC], F32)
            gnbt = sb4("gnbt", [128, KC], F32)
            xnTs = sb4("xnTs", [128, 2 * KC * C1], BF16)
            rtf = sb4("rtf", [128, 2 * KC * C1], BF16)
            rta = sb4("rta", [128, 2 * C1], BF16)
            gateb = sb4("gateb", [128, 2 * KC * C1], BF16)
            gps_ = [ph.enter_context(psum(f"gps{i}", [128, C1], F32)) for i in range(2)]

            T4 = sem("T4")
            A4 = sem("A4")
            V4 = sem("V4")
            V4s = sem("V4s")
            P4 = sem("P4")
            SV4 = sem("SV4")

            def _g1(g):
                guard(g)
                g.dma_start(out=wgb.rearrange("p (kc m) -> p kc m", kc=KC),
                            in_=wg16_d.rearrange("(kc p) m -> p kc m", p=128)).then_inc(GQ.h, 16)
                GQ.n += 16
            block.gpsimd(_g1)

            def _s1(s):
                guard(s)
                s.dma_start(out=bgt[:], in_=bg_t).then_inc(SS.h, 16)
                s.dma_start(out=gnwt[:], in_=gnw_t).then_inc(SS.h, 16)
                s.dma_start(out=gnbt[:], in_=gnb_t).then_inc(SS.h, 16)
                SS.n += 48
            block.scalar(_s1)

            def _v1(v):
                guard(v)
                v.wait_ge(SS.h, SS.n)
                SV4.inc(v.tensor_scalar(bgtn[:], bgt[:], -1.0, None, ALU.mult))
                v.wait_ge(SV4.h, SV4.n)
            block.vector(_v1)

            gate_store = gate_d.rearrange("(kc p) t -> p kc t", p=128)
            KCC = KC * C1
            t4_e, l4_e, r4_e, v4_e, p4_e, ss4_e = [], [], [], [], [], []
            for c in range(NCH1):
                pp_ = c % 2
                xsl = slice(pp_ * KCC, (pp_ + 1) * KCC)
                def _sck(s, c=c):
                    if c == 0:
                        guard(s)
                    else:
                        if c >= 2:
                            s.wait_ge(T4.h, t4_e[c - 2])
                            s.wait_ge(V4s.h, v4_e[c - 2])
                    s.dma_start(out=xnTs[:, pp_ * KCC:(pp_ + 1) * KCC].rearrange("p (kc t) -> p kc t", kc=KC),
                                in_=xnT_d.rearrange("(kc p) t -> p kc t", p=128)[:, :, c * C1:(c + 1) * C1]).then_inc(L.h, 16)
                    L.n += 16
                    l4_e.append(L.n)
                    for mm in range(KC):
                        s.dma_start(out=rtf[:, pp_ * KCC + mm * C1: pp_ * KCC + (mm + 1) * C1],
                                    in_=ret_d[c * C1:(c + 1) * C1, mm * 128:(mm + 1) * 128],
                                    transpose=True).then_inc(SS.h, 16)
                    SS.n += 16 * KC
                    r4_e.append(SS.n)
                block.scalar(_sck)

                t4_b, a4_b, v4_b, p4_b, v4s_b = T4.n, A4.n, V4.n, P4.n, V4s.n

                def _te(t, c=c):
                    t.wait_ge(L.h, l4_e[c])
                    if c == 0:
                        t.wait_ge(GQ.h, GQ.n)
                        t.wait_ge(SS.h, SS.n)
                    else:
                        t.wait_ge(A4.h, a4_b)
                    for mm in range(KC):
                        if mm >= 2:
                            t.wait_ge(A4.h, a4_b + mm - 1)
                        for kc in range(KC):
                            ins = t.matmul(gps_[mm % 2][:],
                                           wgb[:, kc * D + mm * 128: kc * D + (mm + 1) * 128],
                                           xnTs[:, pp_ * KCC + kc * C1: pp_ * KCC + (kc + 1) * C1],
                                           start=(kc == 0), stop=(kc == KC - 1))
                        T4.inc(ins)
                    t4_e.append(T4.n)
                block.tensor(_te)

                def _sc(s, c=c):
                    if c == 0:
                        s.wait_ge(SS.h, SS.n)
                    else:
                        s.wait_ge(P4.h, p4_b)
                    for mm in range(KC):
                        pm = mm % 2
                        s.wait_ge(T4.h, t4_b + mm + 1)
                        if mm >= 2:
                            s.wait_ge(P4.h, p4_b + mm - 1)
                            s.wait_ge(V4s.h, v4s_b + mm - 1)
                        s.activation(g1b[:, pm * C1:(pm + 1) * C1], gps_[mm % 2][:],
                                     AF.Identity, bias=bgt[:, mm:mm + 1], scale=1.0)
                        ins = s.activation(teb[:, pm * C1:(pm + 1) * C1],
                                           g1b[:, pm * C1:(pm + 1) * C1],
                                           AF.Exp, bias=0.0, scale=-1.0)
                        A4.inc(ins)
                block.scalar(_sc)

                def _ve(v, c=c):
                    v.wait_ge(SS.h, r4_e[c])
                    if c > 0:
                        v.wait_ge(P4.h, p4_b)
                    for mm in range(KC):
                        pm = mm % 2
                        if mm >= 2:
                            v.wait_ge(P4.h, p4_b + mm - 1)
                        v.tensor_scalar(rta[:, pm * C1:(pm + 1) * C1],
                                        rtf[:, pp_ * KCC + mm * C1: pp_ * KCC + (mm + 1) * C1],
                                        gnwt[:, mm:mm + 1], gnbt[:, mm:mm + 1],
                                        ALU.mult, ALU.add)
                        v.wait_ge(A4.h, a4_b + mm + 1)
                        v.tensor_scalar(teb[:, pm * C1:(pm + 1) * C1],
                                        teb[:, pm * C1:(pm + 1) * C1], 1.0, None, ALU.add)
                        ins = v.reciprocal(srb[:, pm * C1:(pm + 1) * C1],
                                           teb[:, pm * C1:(pm + 1) * C1])
                        V4s.inc(ins)
                    v4_e.append(V4s.n)
                block.vector(_ve)

                def _gp(g, c=c):
                    if c >= 2:
                        g.wait_ge(SS.h, ss4_e[c - 2])
                    for mm in range(KC):
                        pm = mm % 2
                        g.wait_ge(V4s.h, v4s_b + mm + 1)
                        g.tensor_tensor(gateb[:, pp_ * KCC + mm * C1: pp_ * KCC + (mm + 1) * C1],
                                        g1b[:, pm * C1:(pm + 1) * C1],
                                        srb[:, pm * C1:(pm + 1) * C1], ALU.mult)
                        ins = g.tensor_tensor(gateb[:, pp_ * KCC + mm * C1: pp_ * KCC + (mm + 1) * C1],
                                              gateb[:, pp_ * KCC + mm * C1: pp_ * KCC + (mm + 1) * C1],
                                              rta[:, pm * C1:(pm + 1) * C1], ALU.mult)
                        P4.inc(ins)
                block.gpsimd(_gp)


                def _sc(s, c=c):
                    s.wait_ge(P4.h, p4_b + KC)
                    s.dma_start(out=gate_store[:, :, c * C1:(c + 1) * C1],
                                in_=gateb[:, xsl].rearrange("p (kc t) -> p kc t", kc=KC)).then_inc(SS.h, 16)
                    SS.n += 16
                    ss4_e.append(SS.n)
                block.scalar(_sc)

        with ExitStack() as ph:
            sb5 = lambda name, shape, dtype: ph.enter_context(nc.sbuf_tensor("P4o" + name, shape, dtype))
            wob = sb5("wob", [128, KC * D], BF16)
            bot = sb5("bot", [128, D], F32)
            gateb = sb5("gateb", [128, 2 * KC * C1], BF16)
            xb = sb5("xb", [128, 2 * D], F32)
            fnb = sb5("fnb", [128, 2 * D], F16)
            otmp = sb5("otmp", [128, 2 * D], F32)
            otm2 = sb5("otm2", [128, 2 * D], F32)
            osb = sb5("osb", [128, 2 * D], F32)
            ops_ = [ph.enter_context(psum(f"ops{i}", [128, D], F32)) for i in range(2)]

            T5 = sem("T5")
            V5 = sem("V5")
            V5b = sem("V5b")
            P5 = sem("P5")
            L5 = sem("L5")
            SO = sem("SO")

            def _g1(g):
                guard(g)
                g.dma_start(out=wob.rearrange("p (kc m) -> p kc m", kc=KC),
                            in_=wo16_d.rearrange("(kc p) m -> p kc m", p=128)).then_inc(GQ.h, 16)
                GQ.n += 16
            block.gpsimd(_g1)

            def _s1(s):
                guard(s)
                s.dma_start(out=bot[:], in_=bo_bc).then_inc(SS.h, 16)
                SS.n += 16
            block.scalar(_s1)

            gate_src = gate_d.rearrange("(kc p) t -> p kc t", p=128)
            x_src = x_d.rearrange("(r p) d -> r p d", p=128)
            f_src = ffn_d.rearrange("(r p) d -> r p d", p=128)
            out_store = out_d.rearrange("(r p) d -> r p d", p=128)
            KCC = KC * C1
            t5_e, v5_e, l5_e = [], [], []
            for c in range(NCH1):
                pp_ = c % 2
                xsl = slice(pp_ * KCC, (pp_ + 1) * KCC)
                def _sy(s, c=c):
                    if c == 0:
                        guard(s)
                    elif c >= 2:
                        s.wait_ge(T5.h, t5_e[c - 2])
                    s.dma_start(out=gateb[:, xsl].rearrange("p (kc t) -> p kc t", kc=KC),
                                in_=gate_src[:, :, c * C1:(c + 1) * C1]).then_inc(L.h, 16)
                    L.n += 16
                    l5_e.append(L.n)
                    for ts in range(4):
                        tsp = ts % 2
                        gi = 4 * c + ts
                        if gi >= 2:
                            s.wait_ge(V5.h, max(0, gi - 1))
                            s.wait_ge(V5b.h, max(0, gi - 1))
                        s.dma_start(out=xb[:, tsp * D:(tsp + 1) * D], in_=x_src[gi]).then_inc(L5.h, 16)
                        s.dma_start(out=fnb[:, tsp * D:(tsp + 1) * D], in_=f_src[gi]).then_inc(L5.h, 16)
                        L5.n += 32
                block.sync(_sy)

                t5_b, v5_b, v5b_b, p5_b, so_b = T5.n, V5.n, V5b.n, P5.n, SO.n

                def _te(t, c=c):
                    t.wait_ge(L.h, l5_e[c])
                    if c == 0:
                        t.wait_ge(GQ.h, GQ.n)
                        t.wait_ge(SS.h, SS.n)
                    else:
                        t.wait_ge(V5.h, v5_b)
                    for ts in range(4):
                        if ts >= 2:
                            t.wait_ge(V5.h, v5_b + ts - 1)
                        for nf in range(4):
                            for kc in range(KC):
                                ins = t.matmul(ops_[ts % 2][:, nf * 512:(nf + 1) * 512],
                                               gateb[:, pp_ * KCC + kc * C1 + ts * 128: pp_ * KCC + kc * C1 + (ts + 1) * 128],
                                               wob[:, kc * D + nf * 512: kc * D + (nf + 1) * 512],
                                               start=(kc == 0), stop=(kc == KC - 1))
                        T5.inc(ins)
                    t5_e.append(T5.n)
                block.tensor(_te)

                def _ve(v, c=c):
                    if c > 0:
                        v.wait_ge(P5.h, p5_b)
                    for ts in range(4):
                        tsp = ts % 2
                        v.wait_ge(T5.h, t5_b + ts + 1)
                        v.wait_ge(L5.h, 32 * (4 * c + ts) + 32)
                        if ts >= 2:
                            v.wait_ge(P5.h, p5_b + ts - 1)
                        ins = v.scalar_tensor_tensor(otmp[:, tsp * D:(tsp + 1) * D],
                                                     ops_[ts % 2][:], 1.0, xb[:, tsp * D:(tsp + 1) * D],
                                                     ALU.mult, ALU.add)
                        V5.inc(ins)
                        ins = v.tensor_tensor(otm2[:, tsp * D:(tsp + 1) * D],
                                              otmp[:, tsp * D:(tsp + 1) * D],
                                              fnb[:, tsp * D:(tsp + 1) * D], ALU.add)
                        V5b.inc(ins)
                block.vector(_ve)

                def _gp(g, c=c):
                    if c == 0:
                        g.wait_ge(SS.h, SS.n)
                    else:
                        g.wait_ge(SO.h, so_b)
                    for ts in range(4):
                        tsp = ts % 2
                        g.wait_ge(V5b.h, v5b_b + ts + 1)
                        if ts >= 2:
                            g.wait_ge(SO.h, so_b + 16 * (ts - 1))
                        ins = g.tensor_tensor(osb[:, tsp * D:(tsp + 1) * D],
                                              otm2[:, tsp * D:(tsp + 1) * D], bot[:], ALU.add)
                        P5.inc(ins)
                block.gpsimd(_gp)

                def _sc(s, c=c):
                    for ts in range(4):
                        s.wait_ge(P5.h, p5_b + ts + 1)
                        s.dma_start(out=out_store[4 * c + ts],
                                    in_=osb[:, (ts % 2) * D:(ts % 2 + 1) * D]).then_inc(SO.h, 16)
                        SO.n += 16
                block.scalar(_sc)

            if debug:
                DBG = sem("DBG")

                def _dbg(s):
                    guard(s)
                    for nm, srcap in [("xn", xn_d), ("xq", xq_d), ("ffn", ffn_d), ("q", q_d),
                                      ("k", k_d), ("v", v_d), ("ret", ret_d), ("gate", gate_d)]:
                        s.dma_start(out=dbg[nm], in_=srcap).then_inc(DBG.h, 16)
                        DBG.n += 16
                    s.wait_ge(DBG.h, DBG.n)
                block.sync(_dbg)

            def _fin(s):
                s.wait_ge(SO.h, SO.n)
            block.scalar(_fin)

    return nc


def kernel(L0_attn_x, L0_attn_w_q, L0_attn_b_q, L0_attn_w_k, L0_attn_b_k,
           L0_attn_w_v, L0_attn_b_v, L0_attn_w_g, L0_attn_b_g,
           L0_attn_w_o, L0_attn_b_o, L0_attn_gn_weight, L0_attn_gn_bias,
           L0_ffn_weight, L0_ffn_w_scale, L0_ffn_bias):
    x = np.ascontiguousarray(np.asarray(L0_attn_x, dtype=np.float32))
    consts = _host_constants()

    def col16(v):
        return np.ascontiguousarray(np.asarray(v, np.float32).reshape(KC, 128).T)

    def bc(v):
        return np.ascontiguousarray(np.broadcast_to(np.asarray(v, np.float32).reshape(1, D), (128, D)))

    shared = {
        "w_q": np.ascontiguousarray(np.asarray(L0_attn_w_q, np.float32)),
        "w_k": np.ascontiguousarray(np.asarray(L0_attn_w_k, np.float32)),
        "w_v": np.ascontiguousarray(np.asarray(L0_attn_w_v, np.float32)),
        "w_g": np.ascontiguousarray(np.asarray(L0_attn_w_g, np.float32)),
        "w_o": np.ascontiguousarray(np.asarray(L0_attn_w_o, np.float32)),
        "ffw": np.ascontiguousarray(np.asarray(L0_ffn_weight, np.int8)),
        "bq_t": col16(L0_attn_b_q), "bk_t": col16(L0_attn_b_k), "bg_t": col16(L0_attn_b_g),
        "gnw_t": col16(L0_attn_gn_weight), "gnb_t": col16(L0_attn_gn_bias),
        "bv_bc": bc(L0_attn_b_v), "bo_bc": bc(L0_attn_b_o),
        "ws_bc": bc(np.asarray(L0_ffn_w_scale, np.float32)[:, 0]),
        "fb_bc": bc(L0_ffn_bias),
        "cosT": consts["cosT"], "sinT": consts["sinT"],
        "cosK": consts["cosK"], "sinK": consts["sinK"],
        "Mcomb": consts["Mcomb"], "Gfull": consts["Gfull"], "Gam": consts["Gam"],
        "RT": consts["RT"], "idf32": consts["idf32"], "idbf": consts["idbf"],
        "epsln": consts["epsln"],
    }
    nc = build_nc()
    in_maps = [dict(shared, x=np.ascontiguousarray(x[b])) for b in range(8)]
    res = run_bass_kernel_spmd(nc, in_maps, core_ids=list(range(8)))
    out = np.stack([np.asarray(res.results[b]["out"]) for b in range(8)], axis=0)
    return out


# version 12
# speedup vs baseline: 1.1099x; 1.0037x over previous
import numpy as np
import ml_dtypes
from contextlib import ExitStack

import concourse.bass as bass
import concourse.mybir as mybir
from concourse.bass_utils import run_bass_kernel_spmd

F32 = mybir.dt.float32
BF16 = mybir.dt.bfloat16
F16 = mybir.dt.float16
I8 = mybir.dt.int8
AF = mybir.ActivationFunctionType
ALU = mybir.AluOpType
AX = mybir.AxisListType
BF = ml_dtypes.bfloat16

N, D, H, DH, KC = 4096, 2048, 8, 256, 16
C0, NCH0, TS0 = 512, 8, 1
C1, NCH1 = 512, 8
C3, NCH3 = 256, 16


def _host_constants():
    d = {}
    xs = np.linspace(0.0, 1.0, DH // 2)
    thetas = np.repeat(1.0 / (10000.0 ** xs), 2)
    t = np.arange(N, dtype=np.float64)
    ang = t[None, :] * thetas[:, None]
    cosT = np.cos(ang).reshape(2, 128, N)
    sinT = np.sin(ang).reshape(2, 128, N)
    d["cosT"] = cosT.astype(np.float32)
    d["sinT"] = sinT.astype(np.float32)
    d["cosK"] = (cosT / 16.0).astype(np.float32)
    d["sinK"] = (sinT / 16.0).astype(np.float32)
    g = 1.0 - np.exp(np.linspace(np.log(1 / 32), np.log(1 / 512), H))
    p = np.arange(128)[:, None]
    m = np.arange(C3)[None, :]
    M = np.zeros((128, H * C3))
    for h in range(H):
        M[:, h * C3:(h + 1) * C3] = np.where(m >= p, g[h] ** (m - p), 0.0)
    d["Mfull"] = M.astype(np.float32)
    MC = np.zeros((128, H * 384))
    for h in range(H):
        MC[:, h * 384:h * 384 + C3] = M[:, h * C3:(h + 1) * C3]
        MC[:, h * 384 + C3:h * 384 + 384] = M[:, h * C3:h * C3 + 128]
    d["Mcomb"] = MC.astype(np.float32)
    GF = np.zeros((128, KC * C3))
    for h in range(H):
        for dk in range(2):
            GF[:, (2 * h + dk) * C3:(2 * h + dk + 1) * C3] = (g[h] ** (np.arange(C3) + 1.0))[None, :]
    d["Gfull"] = GF.astype(np.float32)
    GM = np.zeros((128, 2 * D))
    for ts in range(2):
        e = C3 - 1 - (ts * 128 + np.arange(128))
        for h in range(H):
            GM[:, ts * D + h * DH:ts * D + (h + 1) * DH] = (g[h] ** e)[:, None]
    d["Gam"] = GM.astype(np.float32)
    d["gC"] = [float(g[h] ** C3) for h in range(H)]
    RT = np.zeros((128, 128))
    i = np.arange(64)
    RT[2 * i + 1, 2 * i] = -1.0
    RT[2 * i, 2 * i + 1] = 1.0
    d["RT"] = RT.astype(BF)
    d["idf32"] = np.eye(128, dtype=np.float32)
    d["idbf"] = np.eye(128, dtype=BF)
    d["epsln"] = np.full((128, 1), 1e-5, np.float32)
    return d


class SemC:

    def __init__(self, nc, stack, name):
        self.h = stack.enter_context(nc.semaphore(name))
        self.n = 0

    def inc(self, instr, k=1):
        instr.then_inc(self.h, k)
        self.n += k
        return self.n

    def wait(self, eng, val=None):
        eng.wait_ge(self.h, self.n if val is None else val)


def build_nc(debug=False):
    nc = bass.Bass("TRN2", target_bir_lowering=False)

    x_d = nc.dram_tensor("x", [N, D], F32, kind="ExternalInput").ap()
    w_q = nc.dram_tensor("w_q", [D, D], F32, kind="ExternalInput").ap()
    w_k = nc.dram_tensor("w_k", [D, D], F32, kind="ExternalInput").ap()
    w_v = nc.dram_tensor("w_v", [D, D], F32, kind="ExternalInput").ap()
    w_g = nc.dram_tensor("w_g", [D, D], F32, kind="ExternalInput").ap()
    w_o = nc.dram_tensor("w_o", [D, D], F32, kind="ExternalInput").ap()
    ffw = nc.dram_tensor("ffw", [D, D], I8, kind="ExternalInput").ap()
    bq_t = nc.dram_tensor("bq_t", [128, KC], F32, kind="ExternalInput").ap()
    bk_t = nc.dram_tensor("bk_t", [128, KC], F32, kind="ExternalInput").ap()
    bg_t = nc.dram_tensor("bg_t", [128, KC], F32, kind="ExternalInput").ap()
    gnw_t = nc.dram_tensor("gnw_t", [128, KC], F32, kind="ExternalInput").ap()
    gnb_t = nc.dram_tensor("gnb_t", [128, KC], F32, kind="ExternalInput").ap()
    bv_bc = nc.dram_tensor("bv_bc", [128, D], F32, kind="ExternalInput").ap()
    bo_bc = nc.dram_tensor("bo_bc", [128, D], F32, kind="ExternalInput").ap()
    ws_bc = nc.dram_tensor("ws_bc", [128, D], F32, kind="ExternalInput").ap()
    fb_bc = nc.dram_tensor("fb_bc", [128, D], F32, kind="ExternalInput").ap()
    cosT = nc.dram_tensor("cosT", [2, 128, N], F32, kind="ExternalInput").ap()
    sinT = nc.dram_tensor("sinT", [2, 128, N], F32, kind="ExternalInput").ap()
    cosK = nc.dram_tensor("cosK", [2, 128, N], F32, kind="ExternalInput").ap()
    sinK = nc.dram_tensor("sinK", [2, 128, N], F32, kind="ExternalInput").ap()
    Mcomb = nc.dram_tensor("Mcomb", [128, H * 384], F32, kind="ExternalInput").ap()
    Gfull = nc.dram_tensor("Gfull", [128, KC * C3], F32, kind="ExternalInput").ap()
    Gam = nc.dram_tensor("Gam", [128, 2 * D], F32, kind="ExternalInput").ap()
    RT_d = nc.dram_tensor("RT", [128, 128], BF16, kind="ExternalInput").ap()
    idf_d = nc.dram_tensor("idf32", [128, 128], F32, kind="ExternalInput").ap()
    idb_d = nc.dram_tensor("idbf", [128, 128], BF16, kind="ExternalInput").ap()
    eps_d = nc.dram_tensor("epsln", [128, 1], F32, kind="ExternalInput").ap()
    gC = None

    out_d = nc.dram_tensor("out", [N, D], F32, kind="ExternalOutput").ap()

    xn_d = nc.dram_tensor("xn_d", [N, D], BF16, kind="Internal").ap()
    xnT_d = nc.dram_tensor("xnT_d", [D, N], BF16, kind="Internal").ap()
    xq_d = nc.dram_tensor("xq_d", [N, D], BF16, kind="Internal").ap()
    ffn_d = nc.dram_tensor("ffn_d", [N, D], F16, kind="Internal").ap()
    ffw16_d = nc.dram_tensor("ffw16_d", [D, D], BF16, kind="Internal").ap()
    wq16_d = nc.dram_tensor("wq16_d", [D, D], BF16, kind="Internal").ap()
    wk16_d = nc.dram_tensor("wk16_d", [D, D], BF16, kind="Internal").ap()
    wv16_d = nc.dram_tensor("wv16_d", [D, D], BF16, kind="Internal").ap()
    wg16_d = nc.dram_tensor("wg16_d", [D, D], BF16, kind="Internal").ap()
    wo16_d = nc.dram_tensor("wo16_d", [D, D], BF16, kind="Internal").ap()
    q_d = nc.dram_tensor("q_d", [D, N], BF16, kind="Internal").ap()
    k_d = nc.dram_tensor("k_d", [D, N], BF16, kind="Internal").ap()
    v_d = nc.dram_tensor("v_d", [N, D], BF16, kind="Internal").ap()
    ret_d = nc.dram_tensor("ret_d", [N, D], BF16, kind="Internal").ap()
    gate_d = nc.dram_tensor("gate_d", [D, N], BF16, kind="Internal").ap()

    dbg = {}
    if debug:
        for nm, shape, dt in [("xn", [N, D], BF16), ("xq", [N, D], BF16),
                              ("ffn", [N, D], F16),
                              ("q", [D, N], BF16), ("k", [D, N], BF16),
                              ("v", [N, D], BF16), ("ret", [N, D], BF16),
                              ("gate", [D, N], BF16)]:
            dbg[nm] = nc.dram_tensor("dbg_" + nm, shape, dt, kind="ExternalOutput").ap()

    gC_list = _host_constants()["gC"]

    with ExitStack() as PS:
        sbuf = lambda name, shape, dtype: PS.enter_context(nc.sbuf_tensor(name, shape, dtype))
        psum = lambda name, shape, dtype: nc.psum_tensor(name, shape, dtype)

        idf = sbuf("idf", [128, 128], F32)
        idb = sbuf("idb", [128, 128], BF16)
        RTt = sbuf("RTt", [128, 128], BF16)
        epst = sbuf("epst", [128, 1], F32)

        block = PS.enter_context(nc.Block())
        sems = []

        def sem(name):
            s = SemC(nc, PS, name)
            sems.append(s)
            return s

        L = sem("L")
        SS = sem("SS")
        GQ = sem("GQ")
        WCV = sem("WCV")

        def guard(eng):
            for s in sems:
                if s.n > 0:
                    eng.wait_ge(s.h, s.n)
        with ExitStack() as ph:
            sb0 = lambda name, shape, dtype: ph.enter_context(nc.sbuf_tensor(name, shape, dtype))
            wT = sb0("wT", [128, KC * D], BF16)
            ws_t = sb0("ws_t", [128, D], F32)
            fb_t = sb0("fb_t", [128, D], F32)
            xt = sb0("xt", [128, 4 * D], F32)
            xq8 = sb0("xq8", [128, 4 * D], I8)
            xnb = sb0("xnb", [128, 2 * D], BF16)
            xqb = sb0("xqb", [128, 2 * D], BF16)
            ep1 = sb0("ep1", [128, 2 * D], F32)
            f16t = sb0("f16t", [128, 2 * D], F16)
            qTg = sb0("qTg", [128, KC * 512], BF16)
            sclg = sb0("sclg", [128, 16 * NCH0], F32)
            stat = sb0("stat", [128, 240], F32)
            ffp = [ph.enter_context(psum(f"ffp{i}", [128, D], F32)) for i in range(2)]

            MNEG, MX, MN, VR, LNV, RSTD = 0, 16, 32, 48, 64, 80
            A1, NMN, A2, AMAX, SCL0, RCP, MR, RR, MRR = 96, 112, 128, 144, 160, 176, 192, 208, 224

            def blk(b):
                return stat[:, b:b + 16]

            def col(b, su):
                return stat[:, b + 4 * su: b + 4 * su + 1]

            T0 = sem("T0")
            A0 = sem("A0")
            SV0 = sem("SV0")
            SA0 = sem("SA0")
            E1 = sem("E1")
            PF = sem("PF")

            def _sy(s):
                s.dma_start(out=idf[:], in_=idf_d).then_inc(L.h, 16)
                s.dma_start(out=idb[:], in_=idb_d).then_inc(L.h, 16)
                s.dma_start(out=RTt[:], in_=RT_d).then_inc(L.h, 16)
                s.dma_start(out=epst[:], in_=eps_d).then_inc(L.h, 16)
                s.dma_start(out=ws_t[:], in_=ws_bc).then_inc(L.h, 16)
                s.dma_start(out=fb_t[:], in_=fb_bc).then_inc(L.h, 16)
                L.n += 6 * 16
            block.sync(_sy)

            def _g0(g):
                g.dma_start(out=ffw16_d, in_=ffw).then_inc(WCV.h, 16)
                g.dma_start(out=wq16_d, in_=w_q).then_inc(WCV.h, 16)
                WCV.n += 32
            block.gpsimd(_g0)

            def _s0(s):
                s.wait_ge(WCV.h, 16)
                for kc in range(KC):
                    s.dma_start(out=wT[:, kc * D:(kc + 1) * D],
                                in_=ffw16_d[:, kc * 128:(kc + 1) * 128],
                                transpose=True).then_inc(SS.h, 16)
                SS.n += 16 * KC
            block.scalar(_s0)
            ss_wT = SS.n

            x_r = x_d.rearrange("(c p) d -> c p d", p=128)
            xn_store = xn_d.rearrange("(c p) d -> c p d", p=128)
            xq_store = xq_d.rearrange("(c p) d -> c p d", p=128)
            ffn_store = ffn_d.rearrange("(c p) d -> c p d", p=128)
            NV = NCH0 + 4
            Lsub, Lffn = {}, {}
            sv_mneg, sv_sub, svMr, svChain = {}, {}, {}, {}
            saRstd = {}
            a0q_sub, ssxn_sub, ssxq_sub, ssxq, ssf16 = {}, {}, {}, {}, {}

            for c in range(NV):
                real = c < NCH0
                g_mm = c - 3

                def _sy(s, c=c, real=real):
                    if real:
                        for su in range(4):
                            gs = 4 * c + su
                            if c >= 1:
                                s.wait_ge(A0.h, a0q_sub[gs - 4])
                            s.dma_start(out=xt[:, su * D:(su + 1) * D],
                                        in_=x_r[gs]).then_inc(L.h, 16)
                            L.n += 16
                            Lsub[gs] = L.n
                    g = c - 2
                    if 0 <= g < NCH0:
                        if g > 0:
                            s.wait_ge(T0.h, 4 * (g - 1) + 4)
                        else:
                            s.wait_ge(SS.h, ss_wT)
                        s.wait_ge(SS.h, ssxq[g])
                        for kc in range(KC):
                            s.dma_start(out=qTg[:, kc * 512:(kc + 1) * 512],
                                        in_=xq_d[g * 512:(g + 1) * 512, kc * 128:(kc + 1) * 128],
                                        transpose=True).then_inc(L.h, 16)
                        L.n += 16 * KC
                        Lffn[g] = L.n
                block.sync(_sy)

                def _ve(v, c=c, real=real, g=g_mm):
                    def e1v_emit(gg, j):
                        v.wait_ge(T0.h, 4 * gg + j + 1)
                        if j < 2:
                            if gg >= 1:
                                v.wait_ge(PF.h, 4 * (gg - 1) + j + 3)
                        else:
                            v.wait_ge(PF.h, 4 * gg + (j - 2) + 1)
                        ins = v.scalar_tensor_tensor(ep1[:, (j % 2) * D:(j % 2 + 1) * D],
                                                     ffp[j % 2][:],
                                                     sclg[:, 16 * gg + 4 * j:16 * gg + 4 * j + 1],
                                                     ws_t[:], ALU.mult, ALU.mult)
                        E1.inc(ins)
                        assert E1.n == 4 * gg + j + 1
                    gp = c - 4
                    if 0 <= gp < NCH0:
                        e1v_emit(gp, 2)
                        e1v_emit(gp, 3)
                    if real:
                        for su in range(4):
                            gs = 4 * c + su
                            v.wait_ge(L.h, Lsub[gs])
                            SV0.inc(v.tensor_reduce(col(MNEG, su), xt[:, su * D:(su + 1) * D],
                                                    AX.X, ALU.add))
                            v.wait_ge(SV0.h, SV0.n)
                            SV0.inc(v.tensor_scalar(col(MNEG, su), col(MNEG, su),
                                                    -1.0 / D, None, ALU.mult))
                            sv_mneg[gs] = SV0.n
                            SV0.inc(v.tensor_reduce(col(MX, su), xt[:, su * D:(su + 1) * D],
                                                    AX.X, ALU.max))
                            SV0.inc(v.tensor_reduce(col(MN, su), xt[:, su * D:(su + 1) * D],
                                                    AX.X, ALU.min))
                            sv_sub[gs] = SV0.n
                            if su == 1 and 0 <= g < NCH0:
                                e1v_emit(g, 0)
                            if su == 2 and 0 <= g < NCH0:
                                e1v_emit(g, 1)
                        v.wait_ge(SA0.h, 6 * c + 6)
                        v.wait_ge(SV0.h, sv_sub[4 * c + 3])
                        SV0.inc(v.tensor_tensor(blk(MR), blk(MNEG), blk(RSTD), ALU.mult))
                        svMr[c] = SV0.n
                        SV0.inc(v.tensor_tensor(blk(A1), blk(MX), blk(MNEG), ALU.add))
                        SV0.inc(v.tensor_scalar(blk(NMN), blk(MN), -1.0, None, ALU.mult))
                        v.wait_ge(SV0.h, SV0.n)
                        SV0.inc(v.tensor_tensor(blk(A2), blk(NMN), blk(MNEG), ALU.subtract))
                        v.wait_ge(SV0.h, SV0.n)
                        SV0.inc(v.tensor_tensor(blk(AMAX), blk(A1), blk(A2), ALU.max))
                        v.wait_ge(SV0.h, SV0.n)
                        SV0.inc(v.tensor_tensor(blk(SCL0), blk(AMAX), blk(RSTD), ALU.mult))
                        v.wait_ge(SV0.h, SV0.n)
                        SV0.inc(v.tensor_scalar(sclg[:, 16 * c:16 * c + 16], blk(SCL0),
                                                1e-7, 1.0 / 127.0, ALU.max, ALU.mult))
                        v.wait_ge(SV0.h, SV0.n)
                        SV0.inc(v.reciprocal(blk(RCP), sclg[:, 16 * c:16 * c + 16]))
                        v.wait_ge(SV0.h, SV0.n)
                        SV0.inc(v.tensor_tensor(blk(RR), blk(RSTD), blk(RCP), ALU.mult))
                        v.wait_ge(SV0.h, SV0.n)
                        SV0.inc(v.tensor_tensor(blk(MRR), blk(MNEG), blk(RR), ALU.mult))
                        svChain[c] = SV0.n
                    elif 0 <= g < NCH0:
                        e1v_emit(g, 0)
                        e1v_emit(g, 1)
                block.vector(_ve)

                def _se(s, c=c, real=real, g=g_mm):
                    if 1 <= g <= NCH0:
                        gp = g - 1
                        for j in (2, 3):
                            s.wait_ge(PF.h, 4 * gp + j + 1)
                            s.dma_start(out=ffn_store[4 * gp + j],
                                        in_=f16t[:, (j % 2) * D:(j % 2 + 1) * D]).then_inc(SS.h, 16)
                            SS.n += 16
                            ssf16[(gp, j)] = SS.n
                    if real:
                        if c >= 1:
                            s.wait_ge(SS.h, ssxq[c - 1])
                        for su in range(4):
                            gs = 4 * c + su
                            s.wait_ge(SV0.h, sv_mneg[gs])
                            SA0.inc(s.activation(xqb[:, (su % 2) * D:(su % 2 + 1) * D],
                                                 xt[:, su * D:(su + 1) * D], AF.Square,
                                                 bias=col(MNEG, su), scale=1.0,
                                                 accum_out=col(VR, su)))
                        s.wait_ge(SA0.h, SA0.n)
                        SA0.inc(s.activation(blk(LNV), blk(VR), AF.Ln,
                                             bias=epst[:], scale=1.0 / D))
                        s.wait_ge(SA0.h, SA0.n)
                        SA0.inc(s.activation(blk(RSTD), blk(LNV), AF.Exp, bias=0.0, scale=-0.5))
                        saRstd[c] = SA0.n
                        assert SA0.n == 6 * c + 6
                        s.wait_ge(SV0.h, svMr[c])
                        for su in range(4):
                            gs = 4 * c + su
                            if gs >= 2:
                                s.wait_ge(SS.h, ssxn_sub[gs - 2])
                            A0.inc(s.activation(xnb[:, (su % 2) * D:(su % 2 + 1) * D],
                                                xt[:, su * D:(su + 1) * D], AF.Identity,
                                                bias=col(MR, su), scale=col(RSTD, su)))
                            s.wait_ge(A0.h, A0.n)
                            s.dma_start(out=xn_store[gs],
                                        in_=xnb[:, (su % 2) * D:(su % 2 + 1) * D]).then_inc(SS.h, 16)
                            SS.n += 16
                            ssxn_sub[gs] = SS.n
                    if 0 <= g < NCH0:
                        for j in (0, 1):
                            s.wait_ge(PF.h, 4 * g + j + 1)
                            s.dma_start(out=ffn_store[4 * g + j],
                                        in_=f16t[:, j * D:(j + 1) * D]).then_inc(SS.h, 16)
                            SS.n += 16
                            ssf16[(g, j)] = SS.n
                block.scalar(_se)

                def _te(t, c=c, g=g_mm):
                    if 0 <= g < NCH0:
                        if g == 0:
                            t.wait_ge(SS.h, ss_wT)
                        t.wait_ge(L.h, Lffn[g])
                        for j in range(4):
                            if j < 2:
                                if g >= 1:
                                    t.wait_ge(E1.h, 4 * (g - 1) + j + 3)
                            else:
                                t.wait_ge(E1.h, 4 * g + (j - 2) + 1)
                            for nf in range(4):
                                for kc in range(KC):
                                    ins = t.matmul(ffp[j % 2][:, nf * 512:(nf + 1) * 512],
                                                   qTg[:, kc * 512 + j * 128: kc * 512 + (j + 1) * 128],
                                                   wT[:, kc * D + nf * 512: kc * D + (nf + 1) * 512],
                                                   start=(kc == 0), stop=(kc == KC - 1))
                            T0.inc(ins)
                            assert T0.n == 4 * g + j + 1
                block.tensor(_te)

                def _gp(g2, c=c, g=g_mm):
                    if 0 <= g < NCH0:
                        for j in range(4):
                            g2.wait_ge(E1.h, 4 * g + j + 1)
                            if j < 2:
                                if g >= 1:
                                    g2.wait_ge(SS.h, ssf16[(g - 1, j + 2)])
                            else:
                                g2.wait_ge(SS.h, ssf16[(g, j - 2)])
                            PF.inc(g2.tensor_tensor(f16t[:, (j % 2) * D:(j % 2 + 1) * D],
                                                    ep1[:, (j % 2) * D:(j % 2 + 1) * D],
                                                    fb_t[:], ALU.add))
                            assert PF.n == 4 * g + j + 1
                block.gpsimd(_gp)

                def _sl(s, c=c, real=real):
                    if real:
                        s.wait_ge(SV0.h, svChain[c])
                        for su in range(4):
                            gs = 4 * c + su
                            A0.inc(s.activation(xq8[:, su * D:(su + 1) * D],
                                                xt[:, su * D:(su + 1) * D], AF.Identity,
                                                bias=col(MRR, su), scale=col(RR, su)))
                            a0q_sub[gs] = A0.n
                        for su in range(4):
                            gs = 4 * c + su
                            if gs >= 2:
                                s.wait_ge(SS.h, ssxq_sub[gs - 2])
                            A0.inc(s.activation(xqb[:, (su % 2) * D:(su % 2 + 1) * D],
                                                xq8[:, su * D:(su + 1) * D], AF.Identity,
                                                bias=0.0, scale=1.0))
                            s.wait_ge(A0.h, A0.n)
                            s.dma_start(out=xq_store[gs],
                                        in_=xqb[:, (su % 2) * D:(su % 2 + 1) * D]).then_inc(SS.h, 16)
                            SS.n += 16
                            ssxq_sub[gs] = SS.n
                        ssxq[c] = SS.n
                block.scalar(_sl)

        def proj_rope_phase(tag, w_ap, bias_ap, cos_ap, sin_ap, out_store, conv=None):
            with ExitStack() as ph:
                sb1 = lambda name, shape, dtype: ph.enter_context(nc.sbuf_tensor(tag + name, shape, dtype))
                wqb = sb1("wqb", [128, KC * D], BF16)
                bqt = sb1("bqt", [128, KC], F32)
                xnTs = sb1("xnTs", [128, 2 * KC * C1], BF16)
                cosS = sb1("cosS", [128, 2 * 2 * C1], F32)
                sinS = sb1("sinS", [128, 2 * 2 * C1], F32)
                q1 = sb1("q1", [128, KC * C1], BF16)
                qh = sb1("qh", [128, 2 * KC * C1], BF16)
                rot = sb1("rot", [128, 2 * C1], BF16)
                t1b = sb1("t1b", [128, 2 * C1], BF16)
                qps = [ph.enter_context(psum(tag + f"qps{i}", [128, C1], F32)) for i in range(2)]
                rps = [ph.enter_context(psum(tag + f"rps{i}", [128, C1], F32)) for i in range(2)]

                Tp = sem(tag + "Tp")
                Tr = sem(tag + "Tr")
                Ap = sem(tag + "Ap")
                Vp = sem(tag + "Vp")
                Pp = sem(tag + "Pp")

                def _g1(g):
                    guard(g)
                    g.dma_start(out=wqb.rearrange("p (kc m) -> p kc m", kc=KC)[:, 0:8],
                                in_=w_ap.rearrange("(kc p) m -> p kc m", p=128)[:, 0:8]).then_inc(GQ.h, 16)
                    GQ.n += 16
                    if conv is not None:
                        g.dma_start(out=conv[1], in_=conv[0]).then_inc(WCV.h, 16)
                        WCV.n += 16
                block.gpsimd(_g1)

                def _sy0(s):
                    guard(s)
                    s.dma_start(out=wqb.rearrange("p (kc m) -> p kc m", kc=KC)[:, 8:16],
                                in_=w_ap.rearrange("(kc p) m -> p kc m", p=128)[:, 8:16]).then_inc(L.h, 16)
                    L.n += 16
                block.sync(_sy0)

                def _s1(s):
                    guard(s)
                    s.dma_start(out=bqt[:], in_=bias_ap).then_inc(SS.h, 16)
                    SS.n += 16
                block.scalar(_s1)

                tp_e, ap_e, vp_e, pp_e, l_e, ss_e = [], [], [], [], [], []
                KCC = KC * C1
                for c in range(NCH1):
                    pp_ = c % 2
                    xsl = slice(pp_ * KCC, (pp_ + 1) * KCC)
                    csl_base = pp_ * 2 * C1

                    def _sy(s, c=c):
                        if c >= 2:
                            s.wait_ge(Tp.h, tp_e[c - 2])
                            s.wait_ge(SS.h, ss_e[c - 2])
                            s.wait_ge(Pp.h, pp_e[c - 2])
                            s.wait_ge(Vp.h, vp_e[c - 2])
                        if tag == "P1q":
                            for kc in range(KC):
                                s.dma_start(out=xnTs[:, pp_ * KCC + kc * C1:(pp_ * KCC + (kc + 1) * C1)],
                                            in_=xn_d[c * C1:(c + 1) * C1, kc * 128:(kc + 1) * 128],
                                            transpose=True).then_inc(L.h, 16)
                        else:
                            s.dma_start(out=xnTs[:, pp_ * KCC:(pp_ + 1) * KCC].rearrange("p (kc t) -> p kc t", kc=KC),
                                        in_=xnT_d.rearrange("(kc p) t -> p kc t", p=128)[:, :, c * C1:(c + 1) * C1]).then_inc(L.h, 16)
                            L.n -= 16 * (KC - 1)
                        s.dma_start(out=cosS[:, csl_base:csl_base + 2 * C1].rearrange("p (dk t) -> p dk t", dk=2),
                                    in_=cos_ap[:, :, c * C1:(c + 1) * C1].rearrange("dk p t -> p dk t")).then_inc(L.h, 16)
                        s.dma_start(out=sinS[:, csl_base:csl_base + 2 * C1].rearrange("p (dk t) -> p dk t", dk=2),
                                    in_=sin_ap[:, :, c * C1:(c + 1) * C1].rearrange("dk p t -> p dk t")).then_inc(L.h, 16)
                        L.n += 16 * (KC + 2)
                        l_e.append(L.n)
                    block.sync(_sy)

                    tp_b, tr_b, ap_b, vp_b, pp_b = Tp.n, Tr.n, Ap.n, Vp.n, Pp.n

                    def _te(t, c=c):
                        t.wait_ge(L.h, l_e[c])
                        if c == 0:
                            t.wait_ge(GQ.h, GQ.n)
                            t.wait_ge(SS.h, SS.n)
                        else:
                            t.wait_ge(Ap.h, ap_b)
                            t.wait_ge(Vp.h, vp_b)
                        for mm in range(KC):
                            if mm >= 2:
                                t.wait_ge(Ap.h, ap_b + mm - 1)
                            for kc in range(KC):
                                ins = t.matmul(qps[mm % 2][:],
                                               wqb[:, kc * D + mm * 128: kc * D + (mm + 1) * 128],
                                               xnTs[:, pp_ * KCC + kc * C1: pp_ * KCC + (kc + 1) * C1],
                                               start=(kc == 0), stop=(kc == KC - 1))
                            Tp.inc(ins)
                            if mm >= 1:
                                r = mm - 1
                                t.wait_ge(Ap.h, ap_b + r + 1)
                                if r >= 2:
                                    t.wait_ge(Vp.h, vp_b + r - 1)
                                ins = t.matmul(rps[r % 2][:], RTt[:], q1[:, r * C1:(r + 1) * C1],
                                               start=True, stop=True)
                                Tr.inc(ins)
                        r = KC - 1
                        t.wait_ge(Ap.h, ap_b + r + 1)
                        t.wait_ge(Vp.h, vp_b + r - 1)
                        ins = t.matmul(rps[r % 2][:], RTt[:], q1[:, r * C1:(r + 1) * C1],
                                       start=True, stop=True)
                        Tr.inc(ins)
                    block.tensor(_te)

                    def _sc(s, c=c):
                        if c == 0:
                            s.wait_ge(SS.h, SS.n)
                        for mm in range(KC):
                            s.wait_ge(Tp.h, tp_b + mm + 1)
                            if c > 0 and mm == 0:
                                s.wait_ge(Pp.h, pp_b)
                            ins = s.activation(q1[:, mm * C1:(mm + 1) * C1], qps[mm % 2][:],
                                               AF.Identity, bias=bqt[:, mm:mm + 1], scale=1.0)
                            Ap.inc(ins)
                    block.scalar(_sc)

                    def _ve(v, c=c):
                        if c > 0:
                            v.wait_ge(Pp.h, pp_b)
                        for mm in range(KC):
                            v.wait_ge(Tr.h, tr_b + mm + 1)
                            if mm >= 2:
                                v.wait_ge(Pp.h, pp_b + mm - 1)
                            dk = mm % 2
                            v.tensor_tensor(rot[:, dk * C1:(dk + 1) * C1], rps[mm % 2][:],
                                            sinS[:, csl_base + dk * C1: csl_base + (dk + 1) * C1], ALU.mult)
                            ins = v.tensor_tensor(t1b[:, dk * C1:(dk + 1) * C1], q1[:, mm * C1:(mm + 1) * C1],
                                                  cosS[:, csl_base + dk * C1: csl_base + (dk + 1) * C1], ALU.mult)
                            Vp.inc(ins)
                    block.vector(_ve)

                    def _gp(g, c=c):
                        if c >= 2:
                            g.wait_ge(SS.h, ss_e[c - 2])
                        for mm in range(KC):
                            dk = mm % 2
                            g.wait_ge(Vp.h, vp_b + mm + 1)
                            ins = g.tensor_tensor(qh[:, pp_ * KCC + mm * C1: pp_ * KCC + (mm + 1) * C1],
                                                  t1b[:, dk * C1:(dk + 1) * C1],
                                                  rot[:, dk * C1:(dk + 1) * C1], ALU.add)
                            Pp.inc(ins)
                    block.gpsimd(_gp)

                    def _sc(s, c=c):
                        if tag == "P1q":
                            s.wait_ge(L.h, l_e[c])
                            s.dma_start(out=xnT_d.rearrange("(kc p) t -> p kc t", p=128)[:, :, c * C1:(c + 1) * C1],
                                        in_=xnTs[:, xsl].rearrange("p (kc t) -> p kc t", kc=KC)).then_inc(SS.h, 16)
                            SS.n += 16
                        s.wait_ge(Pp.h, pp_b + KC)
                        s.dma_start(out=out_store[:, :, c * C1:(c + 1) * C1],
                                    in_=qh[:, xsl].rearrange("p (kc t) -> p kc t", kc=KC)).then_inc(SS.h, 16)
                        SS.n += 16
                        ss_e.append(SS.n)
                    block.scalar(_sc)

                    tp_e.append(Tp.n)
                    ap_e.append(Ap.n)
                    vp_e.append(Vp.n)
                    pp_e.append(Pp.n)

        q_store = q_d.rearrange("(kc p) t -> p kc t", p=128)
        k_store = k_d.rearrange("(kc p) t -> p kc t", p=128)
        proj_rope_phase("P1q", wq16_d, bq_t, cosT, sinT, q_store, conv=(w_k, wk16_d))
        proj_rope_phase("P1k", wk16_d, bk_t, cosK, sinK, k_store, conv=(w_v, wv16_d))

        with ExitStack() as ph:
            sb2 = lambda name, shape, dtype: ph.enter_context(nc.sbuf_tensor("P2" + name, shape, dtype))
            wvb = sb2("wvb", [128, KC * D], BF16)
            bvt = sb2("bvt", [128, D], F32)
            xnTs = sb2("xnTs", [128, 2 * KC * C1], BF16)
            vsb = sb2("vsb", [128, 2 * D], BF16)
            vps = [ph.enter_context(psum(f"vps{i}", [128, D], F32)) for i in range(2)]

            T2 = sem("T2")
            V2 = sem("V2")

            def _g1(g):
                guard(g)
                g.dma_start(out=wvb.rearrange("p (kc m) -> p kc m", kc=KC)[:, 0:8],
                            in_=wv16_d.rearrange("(kc p) m -> p kc m", p=128)[:, 0:8]).then_inc(GQ.h, 16)
                GQ.n += 16
                g.dma_start(out=wg16_d, in_=w_g).then_inc(WCV.h, 16)
                WCV.n += 16
            block.gpsimd(_g1)

            def _sy0(s):
                guard(s)
                s.dma_start(out=wvb.rearrange("p (kc m) -> p kc m", kc=KC)[:, 8:16],
                            in_=wv16_d.rearrange("(kc p) m -> p kc m", p=128)[:, 8:16]).then_inc(L.h, 16)
                L.n += 16
            block.sync(_sy0)

            def _s1(s):
                guard(s)
                s.dma_start(out=bvt[:], in_=bv_bc).then_inc(SS.h, 16)
                SS.n += 16
            block.scalar(_s1)

            v_store = v_d.rearrange("(r p) d -> r p d", p=128)
            KCC = KC * C1
            t2_e, l2_e = [], []
            for c in range(NCH1):
                pp_ = c % 2
                def _sy(s, c=c):
                    if c >= 2:
                        s.wait_ge(T2.h, t2_e[c - 2])
                    s.dma_start(out=xnTs[:, pp_ * KCC:(pp_ + 1) * KCC].rearrange("p (kc t) -> p kc t", kc=KC),
                                in_=xnT_d.rearrange("(kc p) t -> p kc t", p=128)[:, :, c * C1:(c + 1) * C1]).then_inc(L.h, 16)
                    L.n += 16
                    l2_e.append(L.n)
                block.sync(_sy)

                t2_b, v2_b, ss_b = T2.n, V2.n, SS.n

                def _te(t, c=c):
                    t.wait_ge(L.h, l2_e[c])
                    if c == 0:
                        t.wait_ge(GQ.h, GQ.n)
                        t.wait_ge(SS.h, SS.n)
                    else:
                        t.wait_ge(V2.h, v2_b)
                    for ts in range(4):
                        if ts >= 2:
                            t.wait_ge(V2.h, v2_b + ts - 1)
                        for nf in range(4):
                            for kc in range(KC):
                                ins = t.matmul(vps[ts % 2][:, nf * 512:(nf + 1) * 512],
                                               xnTs[:, pp_ * KCC + kc * C1 + ts * 128: pp_ * KCC + kc * C1 + (ts + 1) * 128],
                                               wvb[:, kc * D + nf * 512: kc * D + (nf + 1) * 512],
                                               start=(kc == 0), stop=(kc == KC - 1))
                        T2.inc(ins)
                    t2_e.append(T2.n)
                block.tensor(_te)

                def _ve(v, c=c):
                    v.wait_ge(SS.h, ss_b if c > 0 else SS.n)
                    for ts in range(4):
                        v.wait_ge(T2.h, t2_b + ts + 1)
                        if ts >= 2:
                            v.wait_ge(SS.h, ss_b + 16 * (ts - 1))
                        ins = v.scalar_tensor_tensor(vsb[:, (ts % 2) * D:(ts % 2 + 1) * D],
                                                     vps[ts % 2][:], 1.0, bvt[:], ALU.mult, ALU.add)
                        V2.inc(ins)
                block.vector(_ve)

                def _sc(s, c=c):
                    for ts in range(4):
                        s.wait_ge(V2.h, v2_b + ts + 1)
                        s.dma_start(out=v_store[c * 4 + ts],
                                    in_=vsb[:, (ts % 2) * D:(ts % 2 + 1) * D]).then_inc(SS.h, 16)
                        SS.n += 16
                block.scalar(_sc)

        def _gconv(g):
            g.dma_start(out=wo16_d, in_=w_o).then_inc(WCV.h, 16)
            WCV.n += 16
        block.gpsimd(_gconv)
        with ExitStack() as ph:
            sb3 = lambda name, shape, dtype: ph.enter_context(nc.sbuf_tensor("P3" + name, shape, dtype))
            KCC3 = KC * C3
            qsb = sb3("qsb", [128, 2 * KCC3], BF16)
            ksb = sb3("ksb", [128, 2 * KCC3], BF16)
            vsb = sb3("vsb", [128, 2 * 2 * D], BF16)
            ktok = sb3("ktok", [128, 2 * D], BF16)
            ktg = sb3("ktg", [128, 2 * D], BF16)
            qg = sb3("qg", [128, KCC3], BF16)
            Mct = sb3("Mct", [128, H * 384], F32)
            Gt = sb3("Gt", [128, KCC3], F32)
            Gmt = sb3("Gmt", [128, 2 * D], F32)
            Ssb = sb3("Ssb", [128, KC * DH], F32)
            Sbf = sb3("Sbf", [128, KC * DH], BF16)
            simb = sb3("simb", [128, 2 * 384], BF16)
            retb = sb3("retb", [128, 2 * 2 * D], BF16)
            rtb = sb3("rtb", [128, 2 * 2 * D], BF16)
            scrB = sb3("scrB", [128, 2 * D], F32)
            gsum = sb3("gsum", [128, 16], F32)
            gmneg = sb3("gmneg", [128, 16], F32)
            gmsq = sb3("gmsq", [128, 16], F32)
            gsq = sb3("gsq", [128, 16], F32)
            gvar = sb3("gvar", [128, 16], F32)
            grs = sb3("grs", [128, 16], F32)

            simc = [ph.enter_context(psum(f"simc{i}", [128, 384], F32)) for i in range(2)]
            ret_ps = [[ph.enter_context(psum(f"ret{i}{j}", [128, DH], F32)) for j in range(2)]
                      for i in range(2)]
            upd_ps = [ph.enter_context(psum(f"upd{i}", [128, 2 * DH], F32)) for i in range(2)]

            T3s = sem("T3s")
            T3r = sem("T3r")
            T3u = sem("T3u")
            V3s = sem("V3s")
            V3u = sem("V3u")
            A3r = sem("A3r")
            P3p = sem("P3p")
            V3g = sem("V3g")
            A3g = sem("A3g")
            P3q = sem("P3q")
            P3g = sem("P3g")
            SV3 = sem("SV3")
            SP3 = sem("SP3")
            V3p = sem("V3p")

            def _sy(s):
                guard(s)
                s.dma_start(out=Mct[:], in_=Mcomb).then_inc(L.h, 16)
                s.dma_start(out=Gt[:], in_=Gfull).then_inc(L.h, 16)
                s.dma_start(out=Gmt[:], in_=Gam).then_inc(L.h, 16)
                L.n += 48
            block.sync(_sy)

            q_src = q_d.rearrange("(kc p) t -> p kc t", p=128)
            k_src = k_d.rearrange("(kc p) t -> p kc t", p=128)
            v_src = v_d.rearrange("(r p) d -> r p d", p=128)
            ret_store = ret_d.rearrange("(r p) d -> r p d", p=128)

            t3s_e, t3r_e, t3u_e, v3s_e, v3u_e, a3r_e = [], [], [], [], [], []
            p3p_e, l3_e, k3_e, p3g_e, ss3_e = [], [], [], [], []

            def P3_emit_gn(c):
                pp = c % 2
                rb = pp * 2 * D

                def _sq(s, c=c):
                    s.wait_ge(V3g.h, V3g.n)
                    for i in range(2):
                        ins = s.activation(scrB[:, i * D:(i + 1) * D],
                                           retb[:, rb + i * D: rb + (i + 1) * D], AF.Square)
                        A3g.inc(ins)
                block.scalar(_sq)
                a3g_b = A3g.n - 2

                def _vg(v, c=c):
                    for i in range(2):
                        v.wait_ge(A3r.h, a3r_e[c])
                        if c >= 2:
                            v.wait_ge(P3q.h, P3q.n)
                        SV3.inc(v.tensor_reduce(gsum[:, i * 8:(i + 1) * 8].rearrange("p (h o) -> p h o", o=1),
                                                retb[:, rb + i * D: rb + (i + 1) * D].rearrange("p (h d) -> p h d", h=H),
                                                AX.X, ALU.add))
                        v.wait_ge(A3g.h, a3g_b + i + 1)
                        ins = v.tensor_reduce(gsq[:, i * 8:(i + 1) * 8].rearrange("p (h o) -> p h o", o=1),
                                              scrB[:, i * D:(i + 1) * D].rearrange("p (h d) -> p h d", h=H),
                                              AX.X, ALU.add)
                        V3g.inc(ins)
                block.vector(_vg)
                v3g_b = V3g.n - 2

                def _pm(g, c=c):
                    for i in range(2):
                        g.wait_ge(V3g.h, v3g_b + i + 1)
                        SP3.inc(g.tensor_scalar(gmneg[:, i * 8:(i + 1) * 8], gsum[:, i * 8:(i + 1) * 8],
                                                -1.0 / DH, None, ALU.mult))
                        g.wait_ge(SP3.h, SP3.n)
                        SP3.inc(g.tensor_tensor(gmsq[:, i * 8:(i + 1) * 8], gmneg[:, i * 8:(i + 1) * 8],
                                                gmneg[:, i * 8:(i + 1) * 8], ALU.mult))
                        g.wait_ge(SP3.h, SP3.n)
                        SP3.inc(g.tensor_scalar(gvar[:, i * 8:(i + 1) * 8], gsq[:, i * 8:(i + 1) * 8],
                                                1.0 / DH, None, ALU.mult))
                        g.wait_ge(SP3.h, SP3.n)
                        ins = g.tensor_tensor(gvar[:, i * 8:(i + 1) * 8], gvar[:, i * 8:(i + 1) * 8],
                                              gmsq[:, i * 8:(i + 1) * 8], ALU.subtract)
                        P3q.inc(ins)
                block.gpsimd(_pm)
                p3q_b = P3q.n - 2

                def _sd(s, c=c):
                    for i in range(2):
                        s.wait_ge(P3q.h, p3q_b + i + 1)
                        SA3g = s.activation(gvar[:, i * 8:(i + 1) * 8], gvar[:, i * 8:(i + 1) * 8],
                                            AF.Ln, bias=epst[:], scale=1.0)
                        A3g.inc(SA3g)
                        s.wait_ge(A3g.h, A3g.n)
                        ins = s.activation(grs[:, i * 8:(i + 1) * 8], gvar[:, i * 8:(i + 1) * 8],
                                           AF.Exp, bias=0.0, scale=-0.5)
                        A3g.inc(ins)
                block.scalar(_sd)
                a3g_b2 = A3g.n - 4

                def _pn(g, c=c):
                    for i in range(2):
                        g.wait_ge(A3g.h, a3g_b2 + 2 * (i + 1))
                        if c >= 2:
                            g.wait_ge(SS.h, ss3_e[c - 2])
                        for h in range(H):
                            ins = g.tensor_scalar(rtb[:, rb + i * D + h * DH: rb + i * D + (h + 1) * DH],
                                                  retb[:, rb + i * D + h * DH: rb + i * D + (h + 1) * DH],
                                                  gmneg[:, i * 8 + h: i * 8 + h + 1],
                                                  grs[:, i * 8 + h: i * 8 + h + 1],
                                                  ALU.add, ALU.mult)
                            P3g.inc(ins)
                block.gpsimd(_pn)
                p3g_e.append(P3g.n)

                def _st(s, c=c):
                    for i in range(2):
                        s.wait_ge(P3g.h, p3g_e[-1] - (1 - i) * 8)
                        s.dma_start(out=ret_store[2 * c + i],
                                    in_=rtb[:, rb + i * D: rb + (i + 1) * D]).then_inc(SS.h, 16)
                        SS.n += 16
                    ss3_e.append(SS.n)
                block.scalar(_st)

            for c in range(NCH3):
                pp = c % 2
                qb = pp * KCC3
                vb = pp * 2 * D
                rb = pp * 2 * D

                def _sy(s, c=c):
                    if c == 0:
                        guard(s)
                    elif c >= 2:
                        s.wait_ge(T3s.h, t3s_e[c - 2])
                        s.wait_ge(T3r.h, t3r_e[c - 2])
                        s.wait_ge(T3u.h, t3u_e[c - 2])
                        s.wait_ge(P3p.h, p3p_e[c - 2])
                    s.dma_start(out=qsb[:, qb:qb + KCC3].rearrange("p (kc t) -> p kc t", kc=KC),
                                in_=q_src[:, :, c * C3:(c + 1) * C3]).then_inc(L.h, 16)
                    s.dma_start(out=ksb[:, qb:qb + KCC3].rearrange("p (kc t) -> p kc t", kc=KC),
                                in_=k_src[:, :, c * C3:(c + 1) * C3]).then_inc(L.h, 16)
                    s.dma_start(out=vsb[:, vb:vb + 2 * D].rearrange("p (r d) -> p r d", r=2),
                                in_=v_src[2 * c: 2 * c + 2].rearrange("r p d -> p r d")).then_inc(L.h, 16)
                    L.n += 48
                    l3_e.append(L.n)
                block.sync(_sy)

                def _sck(s, c=c):
                    if c == 0:
                        guard(s)
                    else:
                        s.wait_ge(P3p.h, p3p_e[c - 1])
                    for ts in range(2):
                        s.dma_start(out=ktok[:, ts * D:(ts + 1) * D],
                                    in_=k_d[:, c * C3 + ts * 128: c * C3 + (ts + 1) * 128],
                                    transpose=True).then_inc(SS.h, 16)
                    SS.n += 32
                    k3_e.append(SS.n)
                block.scalar(_sck)

                p3_b = P3p.n

                def _gp(g, c=c):
                    g.wait_ge(SS.h, k3_e[c])
                    if c > 0:
                        g.wait_ge(T3u.h, t3u_e[c - 1])
                    P3p.inc(g.tensor_tensor(ktg[:], ktok[:], Gmt[:], ALU.mult))
                    p3p_e.append(P3p.n)
                block.gpsimd(_gp)

                t3s_b, t3r_b, t3u_b = T3s.n, T3r.n, T3u.n
                v3s_b, v3u_b, a3r_b = V3s.n, V3u.n, A3r.n
                v3p_b = V3p.n

                def _te(t, c=c):
                    t.wait_ge(L.h, l3_e[c])
                    for h in range(H):
                        hp = h % 2
                        if h > 1:
                            t.wait_ge(V3s.h, v3s_b + h - 1)
                        elif c > 0:
                            t.wait_ge(V3s.h, v3s_b)
                        for dk in range(2):
                            blk = 2 * h + dk
                            t.matmul(simc[hp][:, 0:C3],
                                     ksb[:, qb + blk * C3: qb + blk * C3 + 128],
                                     qsb[:, qb + blk * C3: qb + blk * C3 + C3],
                                     start=(dk == 0), stop=(dk == 1))
                        for dk in range(2):
                            blk = 2 * h + dk
                            ins = t.matmul(simc[hp][:, 256:384],
                                           ksb[:, qb + blk * C3 + 128: qb + blk * C3 + 256],
                                           qsb[:, qb + blk * C3 + 128: qb + blk * C3 + 256],
                                           start=(dk == 0), stop=(dk == 1))
                        T3s.inc(ins)
                        if h > 1:
                            t.wait_ge(A3r.h, a3r_b + 2 * (h - 1))
                        elif c > 0:
                            t.wait_ge(A3r.h, a3r_b)
                        if c > 0 and h == 0:
                            t.wait_ge(V3p.h, v3p_b + 2)
                        t.wait_ge(V3s.h, v3s_b + h + 1)
                        for i in range(2):
                            ops = [(simb[:, hp * 384 + i * 128: hp * 384 + (i + 1) * 128],
                                    vsb[:, vb + 0 * D + h * DH: vb + 0 * D + (h + 1) * DH])]
                            if i == 1:
                                ops.append((simb[:, hp * 384 + 2 * 128: hp * 384 + 3 * 128],
                                            vsb[:, vb + 1 * D + h * DH: vb + 1 * D + (h + 1) * DH]))
                            if c > 0:
                                for dk in range(2):
                                    blk = 2 * h + dk
                                    ops.append((qg[:, blk * C3 + i * 128: blk * C3 + (i + 1) * 128],
                                                Sbf[:, blk * DH:(blk + 1) * DH]))
                            for oi, (lh, rh) in enumerate(ops):
                                ins = t.matmul(ret_ps[i][hp][:], lh, rh,
                                               start=(oi == 0), stop=(oi == len(ops) - 1))
                            T3r.inc(ins)
                    t.wait_ge(P3p.h, p3_b + 1)
                    for h in range(H):
                        hp = h % 2
                        if h > 1:
                            t.wait_ge(V3u.h, v3u_b + h - 1)
                        elif c > 0:
                            t.wait_ge(V3u.h, v3u_b)
                        for dk in range(2):
                            for ts in range(2):
                                ins = t.matmul(upd_ps[hp][:, dk * DH:(dk + 1) * DH],
                                               ktg[:, ts * D + h * DH + dk * 128: ts * D + h * DH + (dk + 1) * 128],
                                               vsb[:, vb + ts * D + h * DH: vb + ts * D + (h + 1) * DH],
                                               start=(ts == 0), stop=(ts == 1))
                        T3u.inc(ins)
                    t3s_e.append(T3s.n)
                    t3r_e.append(T3r.n)
                    t3u_e.append(T3u.n)
                block.tensor(_te)

                def _ve(v, c=c):
                    v.wait_ge(L.h, l3_e[c])
                    if c > 0:
                        v.wait_ge(V3u.h, v3u_e[c - 1])
                        v.wait_ge(T3r.h, t3r_e[c - 1])
                        V3p.inc(v.tensor_copy(Sbf[:], Ssb[:]))
                        V3p.inc(v.tensor_tensor(qg[:], qsb[:, qb:qb + KCC3], Gt[:], ALU.mult))
                    for h in range(H):
                        hp = h % 2
                        v.wait_ge(T3s.h, t3s_b + h + 1)
                        if h > 1:
                            v.wait_ge(T3r.h, t3r_b + 2 * (h - 1))
                        elif c > 0:
                            v.wait_ge(T3r.h, t3r_b)
                        ins = v.tensor_tensor(simb[:, hp * 384:(hp + 1) * 384], simc[hp][:],
                                              Mct[:, h * 384:(h + 1) * 384], ALU.mult)
                        V3s.inc(ins)
                    for h in range(H):
                        hp = h % 2
                        v.wait_ge(T3u.h, t3u_b + h + 1)
                        if c == 0:
                            ins = v.tensor_copy(Ssb[:, 2 * h * DH:(2 * h + 2) * DH], upd_ps[hp][:])
                        else:
                            ins = v.scalar_tensor_tensor(Ssb[:, 2 * h * DH:(2 * h + 2) * DH],
                                                         Ssb[:, 2 * h * DH:(2 * h + 2) * DH],
                                                         gC_list[h], upd_ps[hp][:],
                                                         ALU.mult, ALU.add)
                        V3u.inc(ins)
                    v3s_e.append(V3s.n)
                    v3u_e.append(V3u.n)
                block.vector(_ve)

                def _sc(s, c=c):
                    for h in range(H):
                        hp = h % 2
                        for i in range(2):
                            s.wait_ge(T3r.h, t3r_b + 2 * h + i + 1)
                            if c >= 2 and h == 0 and i == 0:
                                s.wait_ge(P3g.h, p3g_e[c - 2])
                            A3r.inc(s.copy(retb[:, rb + i * D + h * DH: rb + i * D + (h + 1) * DH],
                                           ret_ps[i][hp][:]))
                    a3r_e.append(A3r.n)
                block.scalar(_sc)

                if c >= 1:
                    P3_emit_gn(c - 1)

            P3_emit_gn(NCH3 - 1)

        with ExitStack() as ph:
            sb4 = lambda name, shape, dtype: ph.enter_context(nc.sbuf_tensor("P4g" + name, shape, dtype))
            wgb = sb4("wgb", [128, KC * D], BF16)
            bgt = sb4("bgt", [128, KC], F32)
            silb = sb4("silb", [128, 2 * C1], BF16)
            gnwt = sb4("gnwt", [128, KC], F32)
            gnbt = sb4("gnbt", [128, KC], F32)
            xnTs = sb4("xnTs", [128, 2 * KC * C1], BF16)
            rtf = sb4("rtf", [128, 2 * KC * C1], BF16)
            rta = sb4("rta", [128, 2 * C1], BF16)
            gateb = sb4("gateb", [128, 2 * KC * C1], BF16)
            gps_ = [ph.enter_context(psum(f"gps{i}", [128, C1], F32)) for i in range(2)]

            T4 = sem("T4")
            A4 = sem("A4")
            V4 = sem("V4")
            V4s = sem("V4s")
            P4 = sem("P4")
            SV4 = sem("SV4")

            def _g1(g):
                guard(g)
                g.dma_start(out=wgb.rearrange("p (kc m) -> p kc m", kc=KC)[:, 0:8],
                            in_=wg16_d.rearrange("(kc p) m -> p kc m", p=128)[:, 0:8]).then_inc(GQ.h, 16)
                GQ.n += 16
            block.gpsimd(_g1)

            def _sy0(s):
                guard(s)
                s.dma_start(out=wgb.rearrange("p (kc m) -> p kc m", kc=KC)[:, 8:16],
                            in_=wg16_d.rearrange("(kc p) m -> p kc m", p=128)[:, 8:16]).then_inc(L.h, 16)
                L.n += 16
            block.sync(_sy0)

            def _s1(s):
                guard(s)
                s.dma_start(out=bgt[:], in_=bg_t).then_inc(SS.h, 16)
                s.dma_start(out=gnwt[:], in_=gnw_t).then_inc(SS.h, 16)
                s.dma_start(out=gnbt[:], in_=gnb_t).then_inc(SS.h, 16)
                SS.n += 48
            block.scalar(_s1)

            gate_store = gate_d.rearrange("(kc p) t -> p kc t", p=128)
            KCC = KC * C1
            t4_e, l4_e, r4_e, v4_e, p4_e, ss4_e = [], [], [], [], [], []
            for c in range(NCH1):
                pp_ = c % 2
                xsl = slice(pp_ * KCC, (pp_ + 1) * KCC)
                def _sck(s, c=c):
                    if c == 0:
                        guard(s)
                    else:
                        if c >= 2:
                            s.wait_ge(T4.h, t4_e[c - 2])
                            s.wait_ge(V4s.h, v4_e[c - 2])
                    s.dma_start(out=xnTs[:, pp_ * KCC:(pp_ + 1) * KCC].rearrange("p (kc t) -> p kc t", kc=KC),
                                in_=xnT_d.rearrange("(kc p) t -> p kc t", p=128)[:, :, c * C1:(c + 1) * C1]).then_inc(L.h, 16)
                    L.n += 16
                    l4_e.append(L.n)
                    for mm in range(KC):
                        s.dma_start(out=rtf[:, pp_ * KCC + mm * C1: pp_ * KCC + (mm + 1) * C1],
                                    in_=ret_d[c * C1:(c + 1) * C1, mm * 128:(mm + 1) * 128],
                                    transpose=True).then_inc(SS.h, 16)
                    SS.n += 16 * KC
                    r4_e.append(SS.n)
                block.scalar(_sck)

                t4_b, a4_b, v4_b, p4_b, v4s_b = T4.n, A4.n, V4.n, P4.n, V4s.n

                def _te(t, c=c):
                    t.wait_ge(L.h, l4_e[c])
                    if c == 0:
                        t.wait_ge(GQ.h, GQ.n)
                        t.wait_ge(SS.h, SS.n)
                    else:
                        t.wait_ge(A4.h, a4_b)
                    for mm in range(KC):
                        if mm >= 2:
                            t.wait_ge(A4.h, a4_b + mm - 1)
                        for kc in range(KC):
                            ins = t.matmul(gps_[mm % 2][:],
                                           wgb[:, kc * D + mm * 128: kc * D + (mm + 1) * 128],
                                           xnTs[:, pp_ * KCC + kc * C1: pp_ * KCC + (kc + 1) * C1],
                                           start=(kc == 0), stop=(kc == KC - 1))
                        T4.inc(ins)
                    t4_e.append(T4.n)
                block.tensor(_te)

                def _sc(s, c=c):
                    if c == 0:
                        s.wait_ge(SS.h, SS.n)
                    else:
                        s.wait_ge(P4.h, p4_b)
                    for mm in range(KC):
                        pm = mm % 2
                        s.wait_ge(T4.h, t4_b + mm + 1)
                        if mm >= 2:
                            s.wait_ge(P4.h, p4_b + mm - 1)
                        ins = s.activation(silb[:, pm * C1:(pm + 1) * C1], gps_[mm % 2][:],
                                           AF.Silu, bias=bgt[:, mm:mm + 1], scale=1.0)
                        A4.inc(ins)
                block.scalar(_sc)

                def _ve(v, c=c):
                    v.wait_ge(SS.h, r4_e[c])
                    if c > 0:
                        v.wait_ge(P4.h, p4_b)
                    for mm in range(KC):
                        pm = mm % 2
                        if mm >= 2:
                            v.wait_ge(P4.h, p4_b + mm - 1)
                        ins = v.tensor_scalar(rta[:, pm * C1:(pm + 1) * C1],
                                              rtf[:, pp_ * KCC + mm * C1: pp_ * KCC + (mm + 1) * C1],
                                              gnwt[:, mm:mm + 1], gnbt[:, mm:mm + 1],
                                              ALU.mult, ALU.add)
                        V4s.inc(ins)
                    v4_e.append(V4s.n)
                block.vector(_ve)

                def _gp(g, c=c):
                    if c >= 2:
                        g.wait_ge(SS.h, ss4_e[c - 2])
                    for mm in range(KC):
                        pm = mm % 2
                        g.wait_ge(V4s.h, v4s_b + mm + 1)
                        g.wait_ge(A4.h, a4_b + mm + 1)
                        ins = g.tensor_tensor(gateb[:, pp_ * KCC + mm * C1: pp_ * KCC + (mm + 1) * C1],
                                              silb[:, pm * C1:(pm + 1) * C1],
                                              rta[:, pm * C1:(pm + 1) * C1], ALU.mult)
                        P4.inc(ins)
                block.gpsimd(_gp)


                def _sc(s, c=c):
                    s.wait_ge(P4.h, p4_b + KC)
                    s.dma_start(out=gate_store[:, :, c * C1:(c + 1) * C1],
                                in_=gateb[:, xsl].rearrange("p (kc t) -> p kc t", kc=KC)).then_inc(SS.h, 16)
                    SS.n += 16
                    ss4_e.append(SS.n)
                block.scalar(_sc)

        with ExitStack() as ph:
            sb5 = lambda name, shape, dtype: ph.enter_context(nc.sbuf_tensor("P4o" + name, shape, dtype))
            wob = sb5("wob", [128, KC * D], BF16)
            bot = sb5("bot", [128, D], F32)
            gateb = sb5("gateb", [128, 2 * KC * C1], BF16)
            xb = sb5("xb", [128, 2 * D], F32)
            fnb = sb5("fnb", [128, 2 * D], F16)
            otmp = sb5("otmp", [128, 2 * D], F32)
            otm2 = sb5("otm2", [128, 2 * D], F32)
            osb = sb5("osb", [128, 2 * D], F32)
            ops_ = [ph.enter_context(psum(f"ops{i}", [128, D], F32)) for i in range(2)]

            T5 = sem("T5")
            V5 = sem("V5")
            V5b = sem("V5b")
            P5 = sem("P5")
            L5 = sem("L5")
            SO = sem("SO")

            def _g1(g):
                guard(g)
                g.dma_start(out=wob.rearrange("p (kc m) -> p kc m", kc=KC)[:, 0:8],
                            in_=wo16_d.rearrange("(kc p) m -> p kc m", p=128)[:, 0:8]).then_inc(GQ.h, 16)
                GQ.n += 16
            block.gpsimd(_g1)

            def _sy0(s):
                guard(s)
                s.dma_start(out=wob.rearrange("p (kc m) -> p kc m", kc=KC)[:, 8:16],
                            in_=wo16_d.rearrange("(kc p) m -> p kc m", p=128)[:, 8:16]).then_inc(L.h, 16)
                L.n += 16
            block.sync(_sy0)

            def _s1(s):
                guard(s)
                s.dma_start(out=bot[:], in_=bo_bc).then_inc(SS.h, 16)
                SS.n += 16
            block.scalar(_s1)

            gate_src = gate_d.rearrange("(kc p) t -> p kc t", p=128)
            x_src = x_d.rearrange("(r p) d -> r p d", p=128)
            f_src = ffn_d.rearrange("(r p) d -> r p d", p=128)
            out_store = out_d.rearrange("(r p) d -> r p d", p=128)
            KCC = KC * C1
            t5_e, v5_e, l5_e = [], [], []
            for c in range(NCH1):
                pp_ = c % 2
                xsl = slice(pp_ * KCC, (pp_ + 1) * KCC)
                def _sy(s, c=c):
                    if c >= 2:
                        s.wait_ge(T5.h, t5_e[c - 2])
                    s.dma_start(out=gateb[:, xsl].rearrange("p (kc t) -> p kc t", kc=KC),
                                in_=gate_src[:, :, c * C1:(c + 1) * C1]).then_inc(L.h, 16)
                    L.n += 16
                    l5_e.append(L.n)
                    for ts in range(4):
                        tsp = ts % 2
                        gi = 4 * c + ts
                        if gi >= 2:
                            s.wait_ge(V5.h, max(0, gi - 1))
                            s.wait_ge(V5b.h, max(0, gi - 1))
                        s.dma_start(out=xb[:, tsp * D:(tsp + 1) * D], in_=x_src[gi]).then_inc(L5.h, 16)
                        s.dma_start(out=fnb[:, tsp * D:(tsp + 1) * D], in_=f_src[gi]).then_inc(L5.h, 16)
                        L5.n += 32
                block.sync(_sy)

                t5_b, v5_b, v5b_b, p5_b, so_b = T5.n, V5.n, V5b.n, P5.n, SO.n

                def _te(t, c=c):
                    t.wait_ge(L.h, l5_e[c])
                    if c == 0:
                        t.wait_ge(GQ.h, GQ.n)
                        t.wait_ge(SS.h, SS.n)
                    else:
                        t.wait_ge(V5.h, v5_b)
                    for ts in range(4):
                        if ts >= 2:
                            t.wait_ge(V5.h, v5_b + ts - 1)
                        for nf in range(4):
                            for kc in range(KC):
                                ins = t.matmul(ops_[ts % 2][:, nf * 512:(nf + 1) * 512],
                                               gateb[:, pp_ * KCC + kc * C1 + ts * 128: pp_ * KCC + kc * C1 + (ts + 1) * 128],
                                               wob[:, kc * D + nf * 512: kc * D + (nf + 1) * 512],
                                               start=(kc == 0), stop=(kc == KC - 1))
                        T5.inc(ins)
                    t5_e.append(T5.n)
                block.tensor(_te)

                def _ve(v, c=c):
                    if c > 0:
                        v.wait_ge(P5.h, p5_b)
                    for ts in range(4):
                        tsp = ts % 2
                        v.wait_ge(T5.h, t5_b + ts + 1)
                        v.wait_ge(L5.h, 32 * (4 * c + ts) + 32)
                        if ts >= 2:
                            v.wait_ge(P5.h, p5_b + ts - 1)
                        ins = v.scalar_tensor_tensor(otmp[:, tsp * D:(tsp + 1) * D],
                                                     ops_[ts % 2][:], 1.0, xb[:, tsp * D:(tsp + 1) * D],
                                                     ALU.mult, ALU.add)
                        V5.inc(ins)
                        ins = v.tensor_tensor(otm2[:, tsp * D:(tsp + 1) * D],
                                              otmp[:, tsp * D:(tsp + 1) * D],
                                              fnb[:, tsp * D:(tsp + 1) * D], ALU.add)
                        V5b.inc(ins)
                block.vector(_ve)

                def _gp(g, c=c):
                    if c == 0:
                        g.wait_ge(SS.h, SS.n)
                    else:
                        g.wait_ge(SO.h, so_b)
                    for ts in range(4):
                        tsp = ts % 2
                        g.wait_ge(V5b.h, v5b_b + ts + 1)
                        if ts >= 2:
                            g.wait_ge(SO.h, so_b + 16 * (ts - 1))
                        ins = g.tensor_tensor(osb[:, tsp * D:(tsp + 1) * D],
                                              otm2[:, tsp * D:(tsp + 1) * D], bot[:], ALU.add)
                        P5.inc(ins)
                block.gpsimd(_gp)

                def _sc(s, c=c):
                    for ts in range(4):
                        s.wait_ge(P5.h, p5_b + ts + 1)
                        s.dma_start(out=out_store[4 * c + ts],
                                    in_=osb[:, (ts % 2) * D:(ts % 2 + 1) * D]).then_inc(SO.h, 16)
                        SO.n += 16
                block.scalar(_sc)

            if debug:
                DBG = sem("DBG")

                def _dbg(s):
                    guard(s)
                    for nm, srcap in [("xn", xn_d), ("xq", xq_d), ("ffn", ffn_d), ("q", q_d),
                                      ("k", k_d), ("v", v_d), ("ret", ret_d), ("gate", gate_d)]:
                        s.dma_start(out=dbg[nm], in_=srcap).then_inc(DBG.h, 16)
                        DBG.n += 16
                    s.wait_ge(DBG.h, DBG.n)
                block.sync(_dbg)

            def _fin(s):
                s.wait_ge(SO.h, SO.n)
            block.scalar(_fin)

    return nc


def kernel(L0_attn_x, L0_attn_w_q, L0_attn_b_q, L0_attn_w_k, L0_attn_b_k,
           L0_attn_w_v, L0_attn_b_v, L0_attn_w_g, L0_attn_b_g,
           L0_attn_w_o, L0_attn_b_o, L0_attn_gn_weight, L0_attn_gn_bias,
           L0_ffn_weight, L0_ffn_w_scale, L0_ffn_bias):
    x = np.ascontiguousarray(np.asarray(L0_attn_x, dtype=np.float32))
    consts = _host_constants()

    def col16(v):
        return np.ascontiguousarray(np.asarray(v, np.float32).reshape(KC, 128).T)

    def bc(v):
        return np.ascontiguousarray(np.broadcast_to(np.asarray(v, np.float32).reshape(1, D), (128, D)))

    shared = {
        "w_q": np.ascontiguousarray(np.asarray(L0_attn_w_q, np.float32)),
        "w_k": np.ascontiguousarray(np.asarray(L0_attn_w_k, np.float32)),
        "w_v": np.ascontiguousarray(np.asarray(L0_attn_w_v, np.float32)),
        "w_g": np.ascontiguousarray(np.asarray(L0_attn_w_g, np.float32)),
        "w_o": np.ascontiguousarray(np.asarray(L0_attn_w_o, np.float32)),
        "ffw": np.ascontiguousarray(np.asarray(L0_ffn_weight, np.int8)),
        "bq_t": col16(L0_attn_b_q), "bk_t": col16(L0_attn_b_k), "bg_t": col16(L0_attn_b_g),
        "gnw_t": col16(L0_attn_gn_weight), "gnb_t": col16(L0_attn_gn_bias),
        "bv_bc": bc(L0_attn_b_v), "bo_bc": bc(L0_attn_b_o),
        "ws_bc": bc(np.asarray(L0_ffn_w_scale, np.float32)[:, 0]),
        "fb_bc": bc(L0_ffn_bias),
        "cosT": consts["cosT"], "sinT": consts["sinT"],
        "cosK": consts["cosK"], "sinK": consts["sinK"],
        "Mcomb": consts["Mcomb"], "Gfull": consts["Gfull"], "Gam": consts["Gam"],
        "RT": consts["RT"], "idf32": consts["idf32"], "idbf": consts["idbf"],
        "epsln": consts["epsln"],
    }
    nc = build_nc()
    in_maps = [dict(shared, x=np.ascontiguousarray(x[b])) for b in range(8)]
    res = run_bass_kernel_spmd(nc, in_maps, core_ids=list(range(8)))
    out = np.stack([np.asarray(res.results[b]["out"]) for b in range(8)], axis=0)
    return out


# version 13
# speedup vs baseline: 1.1892x; 1.0714x over previous
import numpy as np
import ml_dtypes
from contextlib import ExitStack

import concourse.bass as bass
import concourse.mybir as mybir
from concourse.bass_utils import run_bass_kernel_spmd

F32 = mybir.dt.float32
BF16 = mybir.dt.bfloat16
F16 = mybir.dt.float16
I8 = mybir.dt.int8
AF = mybir.ActivationFunctionType
ALU = mybir.AluOpType
AX = mybir.AxisListType
BF = ml_dtypes.bfloat16

N, D, H, DH, KC = 4096, 2048, 8, 256, 16
C0, NCH0, TS0 = 512, 8, 1
C1, NCH1 = 512, 8
C3, NCH3 = 256, 16


def _host_constants():
    d = {}
    xs = np.linspace(0.0, 1.0, DH // 2)
    thetas = np.repeat(1.0 / (10000.0 ** xs), 2)
    t = np.arange(N, dtype=np.float64)
    ang = t[None, :] * thetas[:, None]
    cosT = np.cos(ang).reshape(2, 128, N)
    sinT = np.sin(ang).reshape(2, 128, N)
    d["cosT"] = cosT.astype(np.float32)
    d["sinT"] = sinT.astype(np.float32)
    d["cosK"] = (cosT / 16.0).astype(np.float32)
    d["sinK"] = (sinT / 16.0).astype(np.float32)
    g = 1.0 - np.exp(np.linspace(np.log(1 / 32), np.log(1 / 512), H))
    p = np.arange(128)[:, None]
    m = np.arange(C3)[None, :]
    M = np.zeros((128, H * C3))
    for h in range(H):
        M[:, h * C3:(h + 1) * C3] = np.where(m >= p, g[h] ** (m - p), 0.0)
    d["Mfull"] = M.astype(np.float32)
    MC = np.zeros((128, H * 384))
    for h in range(H):
        MC[:, h * 384:h * 384 + C3] = M[:, h * C3:(h + 1) * C3]
        MC[:, h * 384 + C3:h * 384 + 384] = M[:, h * C3:h * C3 + 128]
    d["Mcomb"] = MC.astype(np.float32)
    GF = np.zeros((128, KC * C3))
    for h in range(H):
        for dk in range(2):
            GF[:, (2 * h + dk) * C3:(2 * h + dk + 1) * C3] = (g[h] ** (np.arange(C3) + 1.0))[None, :]
    d["Gfull"] = GF.astype(np.float32)
    GM = np.zeros((128, 2 * D))
    for ts in range(2):
        e = C3 - 1 - (ts * 128 + np.arange(128))
        for h in range(H):
            GM[:, ts * D + h * DH:ts * D + (h + 1) * DH] = (g[h] ** e)[:, None]
    d["Gam"] = GM.astype(np.float32)
    d["gC"] = [float(g[h] ** C3) for h in range(H)]
    RT = np.zeros((128, 128))
    i = np.arange(64)
    RT[2 * i + 1, 2 * i] = -1.0
    RT[2 * i, 2 * i + 1] = 1.0
    d["RT"] = RT.astype(BF)
    d["idf32"] = np.eye(128, dtype=np.float32)
    d["idbf"] = np.eye(128, dtype=BF)
    d["epsln"] = np.full((128, 1), 1e-5, np.float32)
    return d


class SemC:

    def __init__(self, nc, stack, name):
        self.h = stack.enter_context(nc.semaphore(name))
        self.n = 0

    def inc(self, instr, k=1):
        instr.then_inc(self.h, k)
        self.n += k
        return self.n

    def wait(self, eng, val=None):
        eng.wait_ge(self.h, self.n if val is None else val)


def build_nc(debug=False):
    nc = bass.Bass("TRN2", target_bir_lowering=False)

    x_d = nc.dram_tensor("x", [N, D], F32, kind="ExternalInput").ap()
    w_q = nc.dram_tensor("w_q", [D, D], F32, kind="ExternalInput").ap()
    w_k = nc.dram_tensor("w_k", [D, D], F32, kind="ExternalInput").ap()
    w_v = nc.dram_tensor("w_v", [D, D], F32, kind="ExternalInput").ap()
    w_g = nc.dram_tensor("w_g", [D, D], F32, kind="ExternalInput").ap()
    w_o = nc.dram_tensor("w_o", [D, D], F32, kind="ExternalInput").ap()
    ffw = nc.dram_tensor("ffw", [D, D], I8, kind="ExternalInput").ap()
    bq_t = nc.dram_tensor("bq_t", [128, KC], F32, kind="ExternalInput").ap()
    bk_t = nc.dram_tensor("bk_t", [128, KC], F32, kind="ExternalInput").ap()
    bg_t = nc.dram_tensor("bg_t", [128, KC], F32, kind="ExternalInput").ap()
    gnw_t = nc.dram_tensor("gnw_t", [128, KC], F32, kind="ExternalInput").ap()
    gnb_t = nc.dram_tensor("gnb_t", [128, KC], F32, kind="ExternalInput").ap()
    bv_bc = nc.dram_tensor("bv_bc", [128, D], F32, kind="ExternalInput").ap()
    bo_bc = nc.dram_tensor("bo_bc", [128, D], F32, kind="ExternalInput").ap()
    ws_bc = nc.dram_tensor("ws_bc", [128, D], F32, kind="ExternalInput").ap()
    fb_bc = nc.dram_tensor("fb_bc", [128, D], F32, kind="ExternalInput").ap()
    cosT = nc.dram_tensor("cosT", [2, 128, N], F32, kind="ExternalInput").ap()
    sinT = nc.dram_tensor("sinT", [2, 128, N], F32, kind="ExternalInput").ap()
    cosK = nc.dram_tensor("cosK", [2, 128, N], F32, kind="ExternalInput").ap()
    sinK = nc.dram_tensor("sinK", [2, 128, N], F32, kind="ExternalInput").ap()
    Mcomb = nc.dram_tensor("Mcomb", [128, H * 384], F32, kind="ExternalInput").ap()
    Gfull = nc.dram_tensor("Gfull", [128, KC * C3], F32, kind="ExternalInput").ap()
    Gam = nc.dram_tensor("Gam", [128, 2 * D], F32, kind="ExternalInput").ap()
    RT_d = nc.dram_tensor("RT", [128, 128], BF16, kind="ExternalInput").ap()
    idf_d = nc.dram_tensor("idf32", [128, 128], F32, kind="ExternalInput").ap()
    idb_d = nc.dram_tensor("idbf", [128, 128], BF16, kind="ExternalInput").ap()
    eps_d = nc.dram_tensor("epsln", [128, 1], F32, kind="ExternalInput").ap()
    gC = None

    out_d = nc.dram_tensor("out", [N, D], F32, kind="ExternalOutput").ap()

    xn_d = nc.dram_tensor("xn_d", [N, D], BF16, kind="Internal").ap()
    xnT_d = nc.dram_tensor("xnT_d", [D, N], BF16, kind="Internal").ap()
    xq_d = nc.dram_tensor("xq_d", [N, D], BF16, kind="Internal").ap()
    ffn_d = nc.dram_tensor("ffn_d", [N, D], F16, kind="Internal").ap()
    ffw16_d = nc.dram_tensor("ffw16_d", [D, D], BF16, kind="Internal").ap()
    wq16_d = nc.dram_tensor("wq16_d", [D, D], BF16, kind="Internal").ap()
    wk16_d = nc.dram_tensor("wk16_d", [D, D], BF16, kind="Internal").ap()
    wv16_d = nc.dram_tensor("wv16_d", [D, D], BF16, kind="Internal").ap()
    wg16_d = nc.dram_tensor("wg16_d", [D, D], BF16, kind="Internal").ap()
    wo16_d = nc.dram_tensor("wo16_d", [D, D], BF16, kind="Internal").ap()
    q_d = nc.dram_tensor("q_d", [D, N], BF16, kind="Internal").ap()
    k_d = nc.dram_tensor("k_d", [D, N], BF16, kind="Internal").ap()
    v_d = nc.dram_tensor("v_d", [N, D], BF16, kind="Internal").ap()
    ret_d = nc.dram_tensor("ret_d", [N, D], BF16, kind="Internal").ap()
    gate_d = nc.dram_tensor("gate_d", [D, N], BF16, kind="Internal").ap()

    dbg = {}
    if debug:
        for nm, shape, dt in [("xn", [N, D], BF16), ("xq", [N, D], BF16),
                              ("ffn", [N, D], F16),
                              ("q", [D, N], BF16), ("k", [D, N], BF16),
                              ("v", [N, D], BF16), ("ret", [N, D], BF16),
                              ("gate", [D, N], BF16)]:
            dbg[nm] = nc.dram_tensor("dbg_" + nm, shape, dt, kind="ExternalOutput").ap()

    gC_list = _host_constants()["gC"]

    with ExitStack() as PS:
        sbuf = lambda name, shape, dtype: PS.enter_context(nc.sbuf_tensor(name, shape, dtype))
        psum = lambda name, shape, dtype: nc.psum_tensor(name, shape, dtype)

        idf = sbuf("idf", [128, 128], F32)
        idb = sbuf("idb", [128, 128], BF16)
        RTt = sbuf("RTt", [128, 128], BF16)
        epst = sbuf("epst", [128, 1], F32)

        block = PS.enter_context(nc.Block())
        sems = []

        def sem(name):
            s = SemC(nc, PS, name)
            sems.append(s)
            return s

        L = sem("L")
        SS = sem("SS")
        GQ = sem("GQ")
        WCV = sem("WCV")

        def guard(eng):
            for s in sems:
                if s.n > 0:
                    eng.wait_ge(s.h, s.n)
        with ExitStack() as ph:
            sb0 = lambda name, shape, dtype: ph.enter_context(nc.sbuf_tensor(name, shape, dtype))
            wT = sb0("wT", [128, KC * D], BF16)
            ws_t = sb0("ws_t", [128, D], F32)
            fb_t = sb0("fb_t", [128, D], F32)
            xt = sb0("xt", [128, 6 * D], F32)
            xq8 = sb0("xq8", [128, 4 * D], I8)
            xnb = sb0("xnb", [128, 2 * D], BF16)
            xqb = sb0("xqb", [128, 2 * D], BF16)
            scrq = sb0("scrq", [128, D], BF16)
            ep1 = sb0("ep1", [128, 2 * D], F32)
            f16t = sb0("f16t", [128, 2 * D], F16)
            qTg = sb0("qTg", [128, KC * 512], BF16)
            sclg = sb0("sclg", [128, 16 * NCH0], F32)
            stat = sb0("stat", [128, 240], F32)
            ffp = [ph.enter_context(psum(f"ffp{i}", [128, D], F32)) for i in range(2)]

            MNEG, MX, MN, VR, LNV, RSTD = 0, 16, 32, 48, 64, 80
            A1, NMN, A2, AMAX, SCL0, RCP, MR, RR, MRR = 96, 112, 128, 144, 160, 176, 192, 208, 224

            def blk(b):
                return stat[:, b:b + 16]

            def col(b, su):
                return stat[:, b + 4 * su: b + 4 * su + 1]

            T0 = sem("T0")
            A0 = sem("A0")
            SV0 = sem("SV0")
            SA0 = sem("SA0")
            E1 = sem("E1")
            PF = sem("PF")

            def _sy(s):
                s.dma_start(out=idf[:], in_=idf_d).then_inc(L.h, 16)
                s.dma_start(out=idb[:], in_=idb_d).then_inc(L.h, 16)
                s.dma_start(out=RTt[:], in_=RT_d).then_inc(L.h, 16)
                s.dma_start(out=epst[:], in_=eps_d).then_inc(L.h, 16)
                s.dma_start(out=ws_t[:], in_=ws_bc).then_inc(L.h, 16)
                s.dma_start(out=fb_t[:], in_=fb_bc).then_inc(L.h, 16)
                L.n += 6 * 16
            block.sync(_sy)

            def _g0(g):
                g.dma_start(out=ffw16_d, in_=ffw).then_inc(WCV.h, 16)
                g.dma_start(out=wq16_d, in_=w_q).then_inc(WCV.h, 16)
                WCV.n += 32
            block.gpsimd(_g0)

            def _s0(s):
                s.wait_ge(WCV.h, 16)
                for kc in range(KC):
                    s.dma_start(out=wT[:, kc * D:(kc + 1) * D],
                                in_=ffw16_d[:, kc * 128:(kc + 1) * 128],
                                transpose=True).then_inc(SS.h, 16)
                SS.n += 16 * KC
            block.scalar(_s0)
            ss_wT = SS.n

            x_r = x_d.rearrange("(c p) d -> c p d", p=128)
            xn_store = xn_d.rearrange("(c p) d -> c p d", p=128)
            xq_store = xq_d.rearrange("(c p) d -> c p d", p=128)
            ffn_store = ffn_d.rearrange("(c p) d -> c p d", p=128)
            NV = NCH0 + 4
            Lsub, Lffn = {}, {}
            sv_mneg, sv_sub, svMr, svChain = {}, {}, {}, {}
            saRstd = {}
            a0q_sub, ssxn_sub, ssxq_sub, ssxq, ssf16 = {}, {}, {}, {}, {}

            for c in range(NV):
                real = c < NCH0
                g_mm = c - 3

                def _sy(s, c=c, real=real):
                    if real:
                        for su in range(4):
                            gs = 4 * c + su
                            if gs >= 6:
                                s.wait_ge(A0.h, a0q_sub[gs - 6])
                            s.dma_start(out=xt[:, (gs % 6) * D:((gs % 6) + 1) * D],
                                        in_=x_r[gs]).then_inc(L.h, 16)
                            L.n += 16
                            Lsub[gs] = L.n
                    g = c - 2
                    if 0 <= g < NCH0:
                        if g > 0:
                            s.wait_ge(T0.h, 4 * (g - 1) + 4)
                        else:
                            s.wait_ge(SS.h, ss_wT)
                        s.wait_ge(SS.h, ssxq[g])
                        for kc in range(KC):
                            s.dma_start(out=qTg[:, kc * 512:(kc + 1) * 512],
                                        in_=xq_d[g * 512:(g + 1) * 512, kc * 128:(kc + 1) * 128],
                                        transpose=True).then_inc(L.h, 16)
                        L.n += 16 * KC
                        Lffn[g] = L.n
                block.sync(_sy)

                def _ve(v, c=c, real=real, g=g_mm):
                    def e1v_emit(gg, j):
                        v.wait_ge(T0.h, 4 * gg + j + 1)
                        if j < 2:
                            if gg >= 1:
                                v.wait_ge(PF.h, 4 * (gg - 1) + j + 3)
                        else:
                            v.wait_ge(PF.h, 4 * gg + (j - 2) + 1)
                        ins = v.scalar_tensor_tensor(ep1[:, (j % 2) * D:(j % 2 + 1) * D],
                                                     ffp[j % 2][:],
                                                     sclg[:, 16 * gg + 4 * j:16 * gg + 4 * j + 1],
                                                     ws_t[:], ALU.mult, ALU.mult)
                        E1.inc(ins)
                        assert E1.n == 4 * gg + j + 1
                    gp = c - 4
                    if 0 <= gp < NCH0:
                        e1v_emit(gp, 2)
                        e1v_emit(gp, 3)
                    if real:
                        for su in range(4):
                            gs = 4 * c + su
                            v.wait_ge(L.h, Lsub[gs])
                            xts = xt[:, (gs % 6) * D:((gs % 6) + 1) * D]
                            SV0.inc(v.tensor_reduce(col(MNEG, su), xts, AX.X, ALU.add))
                            v.wait_ge(SV0.h, SV0.n)
                            SV0.inc(v.tensor_scalar(col(MNEG, su), col(MNEG, su),
                                                    -1.0 / D, None, ALU.mult))
                            sv_mneg[gs] = SV0.n
                            SV0.inc(v.tensor_reduce(col(MX, su), xts, AX.X, ALU.max))
                            SV0.inc(v.tensor_reduce(col(MN, su), xts, AX.X, ALU.min))
                            sv_sub[gs] = SV0.n
                            if su == 1 and 0 <= g < NCH0:
                                e1v_emit(g, 0)
                            if su == 2 and 0 <= g < NCH0:
                                e1v_emit(g, 1)
                        v.wait_ge(SA0.h, 6 * c + 6)
                        v.wait_ge(SV0.h, sv_sub[4 * c + 3])
                        SV0.inc(v.tensor_tensor(blk(MR), blk(MNEG), blk(RSTD), ALU.mult))
                        svMr[c] = SV0.n
                        SV0.inc(v.tensor_tensor(blk(A1), blk(MX), blk(MNEG), ALU.add))
                        SV0.inc(v.tensor_scalar(blk(NMN), blk(MN), -1.0, None, ALU.mult))
                        v.wait_ge(SV0.h, SV0.n)
                        SV0.inc(v.tensor_tensor(blk(A2), blk(NMN), blk(MNEG), ALU.subtract))
                        v.wait_ge(SV0.h, SV0.n)
                        SV0.inc(v.tensor_tensor(blk(AMAX), blk(A1), blk(A2), ALU.max))
                        v.wait_ge(SV0.h, SV0.n)
                        SV0.inc(v.tensor_tensor(blk(SCL0), blk(AMAX), blk(RSTD), ALU.mult))
                        v.wait_ge(SV0.h, SV0.n)
                        SV0.inc(v.tensor_scalar(sclg[:, 16 * c:16 * c + 16], blk(SCL0),
                                                1e-7, 1.0 / 127.0, ALU.max, ALU.mult))
                        v.wait_ge(SV0.h, SV0.n)
                        SV0.inc(v.reciprocal(blk(RCP), sclg[:, 16 * c:16 * c + 16]))
                        v.wait_ge(SV0.h, SV0.n)
                        SV0.inc(v.tensor_tensor(blk(RR), blk(RSTD), blk(RCP), ALU.mult))
                        v.wait_ge(SV0.h, SV0.n)
                        SV0.inc(v.tensor_tensor(blk(MRR), blk(MNEG), blk(RR), ALU.mult))
                        svChain[c] = SV0.n
                    elif 0 <= g < NCH0:
                        e1v_emit(g, 0)
                        e1v_emit(g, 1)
                block.vector(_ve)

                def _se(s, c=c, real=real, g=g_mm):
                    if 1 <= g <= NCH0:
                        gp = g - 1
                        for j in (2, 3):
                            s.wait_ge(PF.h, 4 * gp + j + 1)
                            s.dma_start(out=ffn_store[4 * gp + j],
                                        in_=f16t[:, (j % 2) * D:(j % 2 + 1) * D]).then_inc(SS.h, 16)
                            SS.n += 16
                            ssf16[(gp, j)] = SS.n
                    if real:
                        for su in range(4):
                            gs = 4 * c + su
                            s.wait_ge(SV0.h, sv_mneg[gs])
                            SA0.inc(s.activation(scrq[:],
                                                 xt[:, (gs % 6) * D:((gs % 6) + 1) * D], AF.Square,
                                                 bias=col(MNEG, su), scale=1.0,
                                                 accum_out=col(VR, su)))
                        s.wait_ge(SA0.h, SA0.n)
                        SA0.inc(s.activation(blk(LNV), blk(VR), AF.Ln,
                                             bias=epst[:], scale=1.0 / D))
                        s.wait_ge(SA0.h, SA0.n)
                        SA0.inc(s.activation(blk(RSTD), blk(LNV), AF.Exp, bias=0.0, scale=-0.5))
                        saRstd[c] = SA0.n
                        assert SA0.n == 6 * c + 6
                        s.wait_ge(SV0.h, svMr[c])
                        for su in range(4):
                            gs = 4 * c + su
                            if gs >= 2:
                                s.wait_ge(SS.h, ssxn_sub[gs - 2])
                            A0.inc(s.activation(xnb[:, (su % 2) * D:(su % 2 + 1) * D],
                                                xt[:, (gs % 6) * D:((gs % 6) + 1) * D], AF.Identity,
                                                bias=col(MR, su), scale=col(RSTD, su)))
                            s.wait_ge(A0.h, A0.n)
                            s.dma_start(out=xn_store[gs],
                                        in_=xnb[:, (su % 2) * D:(su % 2 + 1) * D]).then_inc(SS.h, 16)
                            SS.n += 16
                            ssxn_sub[gs] = SS.n
                    if 0 <= g < NCH0:
                        for j in (0, 1):
                            s.wait_ge(PF.h, 4 * g + j + 1)
                            s.dma_start(out=ffn_store[4 * g + j],
                                        in_=f16t[:, j * D:(j + 1) * D]).then_inc(SS.h, 16)
                            SS.n += 16
                            ssf16[(g, j)] = SS.n
                block.scalar(_se)

                def _te(t, c=c, g=g_mm):
                    if 0 <= g < NCH0:
                        if g == 0:
                            t.wait_ge(SS.h, ss_wT)
                        t.wait_ge(L.h, Lffn[g])
                        for j in range(4):
                            if j < 2:
                                if g >= 1:
                                    t.wait_ge(E1.h, 4 * (g - 1) + j + 3)
                            else:
                                t.wait_ge(E1.h, 4 * g + (j - 2) + 1)
                            for nf in range(4):
                                for kc in range(KC):
                                    ins = t.matmul(ffp[j % 2][:, nf * 512:(nf + 1) * 512],
                                                   qTg[:, kc * 512 + j * 128: kc * 512 + (j + 1) * 128],
                                                   wT[:, kc * D + nf * 512: kc * D + (nf + 1) * 512],
                                                   start=(kc == 0), stop=(kc == KC - 1))
                            T0.inc(ins)
                            assert T0.n == 4 * g + j + 1
                block.tensor(_te)

                def _gp(g2, c=c, g=g_mm):
                    if 0 <= g < NCH0:
                        for j in range(4):
                            g2.wait_ge(E1.h, 4 * g + j + 1)
                            if j < 2:
                                if g >= 1:
                                    g2.wait_ge(SS.h, ssf16[(g - 1, j + 2)])
                            else:
                                g2.wait_ge(SS.h, ssf16[(g, j - 2)])
                            PF.inc(g2.tensor_tensor(f16t[:, (j % 2) * D:(j % 2 + 1) * D],
                                                    ep1[:, (j % 2) * D:(j % 2 + 1) * D],
                                                    fb_t[:], ALU.add))
                            assert PF.n == 4 * g + j + 1
                block.gpsimd(_gp)

                def _sl(s, c=c, real=real):
                    if real:
                        s.wait_ge(SV0.h, svChain[c])
                        for su in range(4):
                            gs = 4 * c + su
                            A0.inc(s.activation(xq8[:, su * D:(su + 1) * D],
                                                xt[:, (gs % 6) * D:((gs % 6) + 1) * D], AF.Identity,
                                                bias=col(MRR, su), scale=col(RR, su)))
                            a0q_sub[gs] = A0.n
                        for su in range(4):
                            gs = 4 * c + su
                            if gs >= 2:
                                s.wait_ge(SS.h, ssxq_sub[gs - 2])
                            A0.inc(s.activation(xqb[:, (su % 2) * D:(su % 2 + 1) * D],
                                                xq8[:, su * D:(su + 1) * D], AF.Identity,
                                                bias=0.0, scale=1.0))
                            s.wait_ge(A0.h, A0.n)
                            s.dma_start(out=xq_store[gs],
                                        in_=xqb[:, (su % 2) * D:(su % 2 + 1) * D]).then_inc(SS.h, 16)
                            SS.n += 16
                            ssxq_sub[gs] = SS.n
                        ssxq[c] = SS.n
                block.scalar(_sl)

        def proj_rope_phase(tag, w_ap, bias_ap, cos_ap, sin_ap, out_store, conv=None):
            with ExitStack() as ph:
                sb1 = lambda name, shape, dtype: ph.enter_context(nc.sbuf_tensor(tag + name, shape, dtype))
                wqb = sb1("wqb", [128, KC * D], BF16)
                bqt = sb1("bqt", [128, KC], F32)
                xnTs = sb1("xnTs", [128, 2 * KC * C1], BF16)
                cosS = sb1("cosS", [128, 2 * 2 * C1], F32)
                sinS = sb1("sinS", [128, 2 * 2 * C1], F32)
                q1 = sb1("q1", [128, KC * C1], BF16)
                qh = sb1("qh", [128, 2 * KC * C1], BF16)
                rot = sb1("rot", [128, 2 * C1], BF16)
                t1b = sb1("t1b", [128, 2 * C1], BF16)
                qps = [ph.enter_context(psum(tag + f"qps{i}", [128, C1], F32)) for i in range(2)]
                rps = [ph.enter_context(psum(tag + f"rps{i}", [128, C1], F32)) for i in range(2)]

                Tp = sem(tag + "Tp")
                Tr = sem(tag + "Tr")
                Ap = sem(tag + "Ap")
                Vp = sem(tag + "Vp")
                Pp = sem(tag + "Pp")

                def _g1(g):
                    guard(g)
                    g.dma_start(out=wqb.rearrange("p (kc m) -> p kc m", kc=KC)[:, 0:8],
                                in_=w_ap.rearrange("(kc p) m -> p kc m", p=128)[:, 0:8]).then_inc(GQ.h, 16)
                    GQ.n += 16
                    if conv is not None:
                        g.dma_start(out=conv[1], in_=conv[0]).then_inc(WCV.h, 16)
                        WCV.n += 16
                block.gpsimd(_g1)

                def _sy0(s):
                    guard(s)
                    s.dma_start(out=wqb.rearrange("p (kc m) -> p kc m", kc=KC)[:, 8:16],
                                in_=w_ap.rearrange("(kc p) m -> p kc m", p=128)[:, 8:16]).then_inc(L.h, 16)
                    L.n += 16
                block.sync(_sy0)

                def _s1(s):
                    guard(s)
                    s.dma_start(out=bqt[:], in_=bias_ap).then_inc(SS.h, 16)
                    SS.n += 16
                block.scalar(_s1)

                tp_e, ap_e, vp_e, pp_e, l_e, ss_e = [], [], [], [], [], []
                KCC = KC * C1
                for c in range(NCH1):
                    pp_ = c % 2
                    xsl = slice(pp_ * KCC, (pp_ + 1) * KCC)
                    csl_base = pp_ * 2 * C1

                    def _sy(s, c=c):
                        if c >= 2:
                            s.wait_ge(Tp.h, tp_e[c - 2])
                            s.wait_ge(SS.h, ss_e[c - 2])
                            s.wait_ge(Pp.h, pp_e[c - 2])
                            s.wait_ge(Vp.h, vp_e[c - 2])
                        if tag == "P1q":
                            for kc in range(KC):
                                s.dma_start(out=xnTs[:, pp_ * KCC + kc * C1:(pp_ * KCC + (kc + 1) * C1)],
                                            in_=xn_d[c * C1:(c + 1) * C1, kc * 128:(kc + 1) * 128],
                                            transpose=True).then_inc(L.h, 16)
                        else:
                            s.dma_start(out=xnTs[:, pp_ * KCC:(pp_ + 1) * KCC].rearrange("p (kc t) -> p kc t", kc=KC),
                                        in_=xnT_d.rearrange("(kc p) t -> p kc t", p=128)[:, :, c * C1:(c + 1) * C1]).then_inc(L.h, 16)
                            L.n -= 16 * (KC - 1)
                        s.dma_start(out=cosS[:, csl_base:csl_base + 2 * C1].rearrange("p (dk t) -> p dk t", dk=2),
                                    in_=cos_ap[:, :, c * C1:(c + 1) * C1].rearrange("dk p t -> p dk t")).then_inc(L.h, 16)
                        s.dma_start(out=sinS[:, csl_base:csl_base + 2 * C1].rearrange("p (dk t) -> p dk t", dk=2),
                                    in_=sin_ap[:, :, c * C1:(c + 1) * C1].rearrange("dk p t -> p dk t")).then_inc(L.h, 16)
                        L.n += 16 * (KC + 2)
                        l_e.append(L.n)
                    block.sync(_sy)

                    tp_b, tr_b, ap_b, vp_b, pp_b = Tp.n, Tr.n, Ap.n, Vp.n, Pp.n

                    def _te(t, c=c):
                        t.wait_ge(L.h, l_e[c])
                        if c == 0:
                            t.wait_ge(GQ.h, GQ.n)
                            t.wait_ge(SS.h, SS.n)
                        else:
                            t.wait_ge(Ap.h, ap_b)
                            t.wait_ge(Vp.h, vp_b)
                        for mm in range(KC):
                            if mm >= 2:
                                t.wait_ge(Ap.h, ap_b + mm - 1)
                            for kc in range(KC):
                                ins = t.matmul(qps[mm % 2][:],
                                               wqb[:, kc * D + mm * 128: kc * D + (mm + 1) * 128],
                                               xnTs[:, pp_ * KCC + kc * C1: pp_ * KCC + (kc + 1) * C1],
                                               start=(kc == 0), stop=(kc == KC - 1))
                            Tp.inc(ins)
                            if mm >= 1:
                                r = mm - 1
                                t.wait_ge(Ap.h, ap_b + r + 1)
                                if r >= 2:
                                    t.wait_ge(Vp.h, vp_b + r - 1)
                                ins = t.matmul(rps[r % 2][:], RTt[:], q1[:, r * C1:(r + 1) * C1],
                                               start=True, stop=True)
                                Tr.inc(ins)
                        r = KC - 1
                        t.wait_ge(Ap.h, ap_b + r + 1)
                        t.wait_ge(Vp.h, vp_b + r - 1)
                        ins = t.matmul(rps[r % 2][:], RTt[:], q1[:, r * C1:(r + 1) * C1],
                                       start=True, stop=True)
                        Tr.inc(ins)
                    block.tensor(_te)

                    def _sc(s, c=c):
                        if c == 0:
                            s.wait_ge(SS.h, SS.n)
                        for mm in range(KC):
                            s.wait_ge(Tp.h, tp_b + mm + 1)
                            if c > 0 and mm == 0:
                                s.wait_ge(Pp.h, pp_b)
                            ins = s.activation(q1[:, mm * C1:(mm + 1) * C1], qps[mm % 2][:],
                                               AF.Identity, bias=bqt[:, mm:mm + 1], scale=1.0)
                            Ap.inc(ins)
                    block.scalar(_sc)

                    def _ve(v, c=c):
                        if c > 0:
                            v.wait_ge(Pp.h, pp_b)
                        for mm in range(KC):
                            v.wait_ge(Tr.h, tr_b + mm + 1)
                            if mm >= 2:
                                v.wait_ge(Pp.h, pp_b + mm - 1)
                            dk = mm % 2
                            v.tensor_tensor(rot[:, dk * C1:(dk + 1) * C1], rps[mm % 2][:],
                                            sinS[:, csl_base + dk * C1: csl_base + (dk + 1) * C1], ALU.mult)
                            ins = v.tensor_tensor(t1b[:, dk * C1:(dk + 1) * C1], q1[:, mm * C1:(mm + 1) * C1],
                                                  cosS[:, csl_base + dk * C1: csl_base + (dk + 1) * C1], ALU.mult)
                            Vp.inc(ins)
                    block.vector(_ve)

                    def _gp(g, c=c):
                        if c >= 2:
                            g.wait_ge(SS.h, ss_e[c - 2])
                        for mm in range(KC):
                            dk = mm % 2
                            g.wait_ge(Vp.h, vp_b + mm + 1)
                            ins = g.tensor_tensor(qh[:, pp_ * KCC + mm * C1: pp_ * KCC + (mm + 1) * C1],
                                                  t1b[:, dk * C1:(dk + 1) * C1],
                                                  rot[:, dk * C1:(dk + 1) * C1], ALU.add)
                            Pp.inc(ins)
                    block.gpsimd(_gp)

                    def _sc(s, c=c):
                        if tag == "P1q":
                            s.wait_ge(L.h, l_e[c])
                            s.dma_start(out=xnT_d.rearrange("(kc p) t -> p kc t", p=128)[:, :, c * C1:(c + 1) * C1],
                                        in_=xnTs[:, xsl].rearrange("p (kc t) -> p kc t", kc=KC)).then_inc(SS.h, 16)
                            SS.n += 16
                        s.wait_ge(Pp.h, pp_b + KC)
                        s.dma_start(out=out_store[:, :, c * C1:(c + 1) * C1],
                                    in_=qh[:, xsl].rearrange("p (kc t) -> p kc t", kc=KC)).then_inc(SS.h, 16)
                        SS.n += 16
                        ss_e.append(SS.n)
                    block.scalar(_sc)

                    tp_e.append(Tp.n)
                    ap_e.append(Ap.n)
                    vp_e.append(Vp.n)
                    pp_e.append(Pp.n)

        q_store = q_d.rearrange("(kc p) t -> p kc t", p=128)
        k_store = k_d.rearrange("(kc p) t -> p kc t", p=128)
        proj_rope_phase("P1q", wq16_d, bq_t, cosT, sinT, q_store, conv=(w_k, wk16_d))
        proj_rope_phase("P1k", wk16_d, bk_t, cosK, sinK, k_store, conv=(w_v, wv16_d))

        with ExitStack() as ph:
            sb2 = lambda name, shape, dtype: ph.enter_context(nc.sbuf_tensor("P2" + name, shape, dtype))
            wvb = sb2("wvb", [128, KC * D], BF16)
            bvt = sb2("bvt", [128, D], F32)
            xnTs = sb2("xnTs", [128, 2 * KC * C1], BF16)
            vsb = sb2("vsb", [128, 2 * D], BF16)
            vps = [ph.enter_context(psum(f"vps{i}", [128, D], F32)) for i in range(2)]

            T2 = sem("T2")
            V2 = sem("V2")

            def _g1(g):
                guard(g)
                g.dma_start(out=wvb.rearrange("p (kc m) -> p kc m", kc=KC)[:, 0:8],
                            in_=wv16_d.rearrange("(kc p) m -> p kc m", p=128)[:, 0:8]).then_inc(GQ.h, 16)
                GQ.n += 16
                g.dma_start(out=wg16_d, in_=w_g).then_inc(WCV.h, 16)
                WCV.n += 16
            block.gpsimd(_g1)

            def _sy0(s):
                guard(s)
                s.dma_start(out=wvb.rearrange("p (kc m) -> p kc m", kc=KC)[:, 8:16],
                            in_=wv16_d.rearrange("(kc p) m -> p kc m", p=128)[:, 8:16]).then_inc(L.h, 16)
                L.n += 16
            block.sync(_sy0)

            def _s1(s):
                guard(s)
                s.dma_start(out=bvt[:], in_=bv_bc).then_inc(SS.h, 16)
                SS.n += 16
            block.scalar(_s1)

            v_store = v_d.rearrange("(r p) d -> r p d", p=128)
            KCC = KC * C1
            t2_e, l2_e = [], []
            for c in range(NCH1):
                pp_ = c % 2
                def _sy(s, c=c):
                    if c >= 2:
                        s.wait_ge(T2.h, t2_e[c - 2])
                    s.dma_start(out=xnTs[:, pp_ * KCC:(pp_ + 1) * KCC].rearrange("p (kc t) -> p kc t", kc=KC),
                                in_=xnT_d.rearrange("(kc p) t -> p kc t", p=128)[:, :, c * C1:(c + 1) * C1]).then_inc(L.h, 16)
                    L.n += 16
                    l2_e.append(L.n)
                block.sync(_sy)

                t2_b, v2_b, ss_b = T2.n, V2.n, SS.n

                def _te(t, c=c):
                    t.wait_ge(L.h, l2_e[c])
                    if c == 0:
                        t.wait_ge(GQ.h, GQ.n)
                        t.wait_ge(SS.h, SS.n)
                    else:
                        t.wait_ge(V2.h, v2_b)
                    for ts in range(4):
                        if ts >= 2:
                            t.wait_ge(V2.h, v2_b + ts - 1)
                        for nf in range(4):
                            for kc in range(KC):
                                ins = t.matmul(vps[ts % 2][:, nf * 512:(nf + 1) * 512],
                                               xnTs[:, pp_ * KCC + kc * C1 + ts * 128: pp_ * KCC + kc * C1 + (ts + 1) * 128],
                                               wvb[:, kc * D + nf * 512: kc * D + (nf + 1) * 512],
                                               start=(kc == 0), stop=(kc == KC - 1))
                        T2.inc(ins)
                    t2_e.append(T2.n)
                block.tensor(_te)

                def _ve(v, c=c):
                    v.wait_ge(SS.h, ss_b if c > 0 else SS.n)
                    for ts in range(4):
                        v.wait_ge(T2.h, t2_b + ts + 1)
                        if ts >= 2:
                            v.wait_ge(SS.h, ss_b + 16 * (ts - 1))
                        ins = v.scalar_tensor_tensor(vsb[:, (ts % 2) * D:(ts % 2 + 1) * D],
                                                     vps[ts % 2][:], 1.0, bvt[:], ALU.mult, ALU.add)
                        V2.inc(ins)
                block.vector(_ve)

                def _sc(s, c=c):
                    for ts in range(4):
                        s.wait_ge(V2.h, v2_b + ts + 1)
                        s.dma_start(out=v_store[c * 4 + ts],
                                    in_=vsb[:, (ts % 2) * D:(ts % 2 + 1) * D]).then_inc(SS.h, 16)
                        SS.n += 16
                block.scalar(_sc)

        def _gconv(g):
            g.dma_start(out=wo16_d, in_=w_o).then_inc(WCV.h, 16)
            WCV.n += 16
        block.gpsimd(_gconv)
        with ExitStack() as ph:
            sb3 = lambda name, shape, dtype: ph.enter_context(nc.sbuf_tensor("P3" + name, shape, dtype))
            KCC3 = KC * C3
            qsb = sb3("qsb", [128, 2 * KCC3], BF16)
            ksb = sb3("ksb", [128, 2 * KCC3], BF16)
            vsb = sb3("vsb", [128, 2 * 2 * D], BF16)
            ktok = sb3("ktok", [128, 2 * D], BF16)
            ktg = sb3("ktg", [128, 2 * D], BF16)
            qg = sb3("qg", [128, KCC3], BF16)
            Mct = sb3("Mct", [128, H * 384], F32)
            Gt = sb3("Gt", [128, KCC3], F32)
            Gmt = sb3("Gmt", [128, 2 * D], F32)
            Ssb = sb3("Ssb", [128, KC * DH], F32)
            Sbf = sb3("Sbf", [128, KC * DH], BF16)
            simb = sb3("simb", [128, 2 * 384], BF16)
            retb = sb3("retb", [128, 2 * 2 * D], BF16)
            rtb = sb3("rtb", [128, 2 * 2 * D], BF16)
            scrB = sb3("scrB", [128, 2 * D], F32)
            gsum = sb3("gsum", [128, 16], F32)
            gmneg = sb3("gmneg", [128, 16], F32)
            gmsq = sb3("gmsq", [128, 16], F32)
            gsq = sb3("gsq", [128, 16], F32)
            gvar = sb3("gvar", [128, 16], F32)
            grs = sb3("grs", [128, 16], F32)

            simc = [ph.enter_context(psum(f"simc{i}", [128, 384], F32)) for i in range(2)]
            ret_ps = [[ph.enter_context(psum(f"ret{i}{j}", [128, DH], F32)) for j in range(2)]
                      for i in range(2)]
            upd_ps = [ph.enter_context(psum(f"upd{i}", [128, 2 * DH], F32)) for i in range(2)]

            T3s = sem("T3s")
            T3r = sem("T3r")
            T3u = sem("T3u")
            V3s = sem("V3s")
            V3u = sem("V3u")
            A3r = sem("A3r")
            P3p = sem("P3p")
            V3g = sem("V3g")
            A3g = sem("A3g")
            P3q = sem("P3q")
            P3g = sem("P3g")
            SV3 = sem("SV3")
            SP3 = sem("SP3")
            V3p = sem("V3p")

            def _sy(s):
                guard(s)
                s.dma_start(out=Mct[:], in_=Mcomb).then_inc(L.h, 16)
                s.dma_start(out=Gt[:], in_=Gfull).then_inc(L.h, 16)
                s.dma_start(out=Gmt[:], in_=Gam).then_inc(L.h, 16)
                L.n += 48
            block.sync(_sy)

            q_src = q_d.rearrange("(kc p) t -> p kc t", p=128)
            k_src = k_d.rearrange("(kc p) t -> p kc t", p=128)
            v_src = v_d.rearrange("(r p) d -> r p d", p=128)
            ret_store = ret_d.rearrange("(r p) d -> r p d", p=128)

            t3s_e, t3r_e, t3u_e, v3s_e, v3u_e, a3r_e = [], [], [], [], [], []
            p3p_e, l3_e, k3_e, p3g_e, ss3_e = [], [], [], [], []

            def P3_emit_gn(c):
                pp = c % 2
                rb = pp * 2 * D

                def _sq(s, c=c):
                    s.wait_ge(V3g.h, V3g.n)
                    for i in range(2):
                        ins = s.activation(scrB[:, i * D:(i + 1) * D],
                                           retb[:, rb + i * D: rb + (i + 1) * D], AF.Square)
                        A3g.inc(ins)
                block.scalar(_sq)
                a3g_b = A3g.n - 2

                def _vg(v, c=c):
                    for i in range(2):
                        v.wait_ge(A3r.h, a3r_e[c])
                        if c >= 2:
                            v.wait_ge(P3q.h, P3q.n)
                        SV3.inc(v.tensor_reduce(gsum[:, i * 8:(i + 1) * 8].rearrange("p (h o) -> p h o", o=1),
                                                retb[:, rb + i * D: rb + (i + 1) * D].rearrange("p (h d) -> p h d", h=H),
                                                AX.X, ALU.add))
                        v.wait_ge(A3g.h, a3g_b + i + 1)
                        ins = v.tensor_reduce(gsq[:, i * 8:(i + 1) * 8].rearrange("p (h o) -> p h o", o=1),
                                              scrB[:, i * D:(i + 1) * D].rearrange("p (h d) -> p h d", h=H),
                                              AX.X, ALU.add)
                        V3g.inc(ins)
                block.vector(_vg)
                v3g_b = V3g.n - 2

                def _pm(g, c=c):
                    for i in range(2):
                        g.wait_ge(V3g.h, v3g_b + i + 1)
                        SP3.inc(g.tensor_scalar(gmneg[:, i * 8:(i + 1) * 8], gsum[:, i * 8:(i + 1) * 8],
                                                -1.0 / DH, None, ALU.mult))
                        g.wait_ge(SP3.h, SP3.n)
                        SP3.inc(g.tensor_tensor(gmsq[:, i * 8:(i + 1) * 8], gmneg[:, i * 8:(i + 1) * 8],
                                                gmneg[:, i * 8:(i + 1) * 8], ALU.mult))
                        g.wait_ge(SP3.h, SP3.n)
                        SP3.inc(g.tensor_scalar(gvar[:, i * 8:(i + 1) * 8], gsq[:, i * 8:(i + 1) * 8],
                                                1.0 / DH, None, ALU.mult))
                        g.wait_ge(SP3.h, SP3.n)
                        ins = g.tensor_tensor(gvar[:, i * 8:(i + 1) * 8], gvar[:, i * 8:(i + 1) * 8],
                                              gmsq[:, i * 8:(i + 1) * 8], ALU.subtract)
                        P3q.inc(ins)
                block.gpsimd(_pm)
                p3q_b = P3q.n - 2

                def _sd(s, c=c):
                    for i in range(2):
                        s.wait_ge(P3q.h, p3q_b + i + 1)
                        SA3g = s.activation(gvar[:, i * 8:(i + 1) * 8], gvar[:, i * 8:(i + 1) * 8],
                                            AF.Ln, bias=epst[:], scale=1.0)
                        A3g.inc(SA3g)
                        s.wait_ge(A3g.h, A3g.n)
                        ins = s.activation(grs[:, i * 8:(i + 1) * 8], gvar[:, i * 8:(i + 1) * 8],
                                           AF.Exp, bias=0.0, scale=-0.5)
                        A3g.inc(ins)
                block.scalar(_sd)
                a3g_b2 = A3g.n - 4

                def _pn(g, c=c):
                    for i in range(2):
                        g.wait_ge(A3g.h, a3g_b2 + 2 * (i + 1))
                        if c >= 2:
                            g.wait_ge(SS.h, ss3_e[c - 2])
                        for h in range(H):
                            ins = g.tensor_scalar(rtb[:, rb + i * D + h * DH: rb + i * D + (h + 1) * DH],
                                                  retb[:, rb + i * D + h * DH: rb + i * D + (h + 1) * DH],
                                                  gmneg[:, i * 8 + h: i * 8 + h + 1],
                                                  grs[:, i * 8 + h: i * 8 + h + 1],
                                                  ALU.add, ALU.mult)
                            P3g.inc(ins)
                block.gpsimd(_pn)
                p3g_e.append(P3g.n)

                def _st(s, c=c):
                    for i in range(2):
                        s.wait_ge(P3g.h, p3g_e[-1] - (1 - i) * 8)
                        s.dma_start(out=ret_store[2 * c + i],
                                    in_=rtb[:, rb + i * D: rb + (i + 1) * D]).then_inc(SS.h, 16)
                        SS.n += 16
                    ss3_e.append(SS.n)
                block.scalar(_st)

            for c in range(NCH3):
                pp = c % 2
                qb = pp * KCC3
                vb = pp * 2 * D
                rb = pp * 2 * D

                def _sy(s, c=c):
                    if c == 0:
                        guard(s)
                    elif c >= 2:
                        s.wait_ge(T3s.h, t3s_e[c - 2])
                        s.wait_ge(T3r.h, t3r_e[c - 2])
                        s.wait_ge(T3u.h, t3u_e[c - 2])
                        s.wait_ge(P3p.h, p3p_e[c - 2])
                    s.dma_start(out=qsb[:, qb:qb + KCC3].rearrange("p (kc t) -> p kc t", kc=KC),
                                in_=q_src[:, :, c * C3:(c + 1) * C3]).then_inc(L.h, 16)
                    s.dma_start(out=ksb[:, qb:qb + KCC3].rearrange("p (kc t) -> p kc t", kc=KC),
                                in_=k_src[:, :, c * C3:(c + 1) * C3]).then_inc(L.h, 16)
                    s.dma_start(out=vsb[:, vb:vb + 2 * D].rearrange("p (r d) -> p r d", r=2),
                                in_=v_src[2 * c: 2 * c + 2].rearrange("r p d -> p r d")).then_inc(L.h, 16)
                    L.n += 48
                    l3_e.append(L.n)
                block.sync(_sy)

                def _sck(s, c=c):
                    if c == 0:
                        guard(s)
                    else:
                        s.wait_ge(P3p.h, p3p_e[c - 1])
                    for ts in range(2):
                        s.dma_start(out=ktok[:, ts * D:(ts + 1) * D],
                                    in_=k_d[:, c * C3 + ts * 128: c * C3 + (ts + 1) * 128],
                                    transpose=True).then_inc(SS.h, 16)
                    SS.n += 32
                    k3_e.append(SS.n)
                block.scalar(_sck)

                p3_b = P3p.n

                def _gp(g, c=c):
                    g.wait_ge(SS.h, k3_e[c])
                    if c > 0:
                        g.wait_ge(T3u.h, t3u_e[c - 1])
                    P3p.inc(g.tensor_tensor(ktg[:], ktok[:], Gmt[:], ALU.mult))
                    p3p_e.append(P3p.n)
                block.gpsimd(_gp)

                t3s_b, t3r_b, t3u_b = T3s.n, T3r.n, T3u.n
                v3s_b, v3u_b, a3r_b = V3s.n, V3u.n, A3r.n
                v3p_b = V3p.n

                def _te(t, c=c):
                    t.wait_ge(L.h, l3_e[c])
                    for h in range(H):
                        hp = h % 2
                        if h > 1:
                            t.wait_ge(V3s.h, v3s_b + h - 1)
                        elif c > 0:
                            t.wait_ge(V3s.h, v3s_b)
                        for dk in range(2):
                            blk = 2 * h + dk
                            t.matmul(simc[hp][:, 0:C3],
                                     ksb[:, qb + blk * C3: qb + blk * C3 + 128],
                                     qsb[:, qb + blk * C3: qb + blk * C3 + C3],
                                     start=(dk == 0), stop=(dk == 1))
                        for dk in range(2):
                            blk = 2 * h + dk
                            ins = t.matmul(simc[hp][:, 256:384],
                                           ksb[:, qb + blk * C3 + 128: qb + blk * C3 + 256],
                                           qsb[:, qb + blk * C3 + 128: qb + blk * C3 + 256],
                                           start=(dk == 0), stop=(dk == 1))
                        T3s.inc(ins)
                        if h > 1:
                            t.wait_ge(A3r.h, a3r_b + 2 * (h - 1))
                        elif c > 0:
                            t.wait_ge(A3r.h, a3r_b)
                        if c > 0 and h == 0:
                            t.wait_ge(V3p.h, v3p_b + 2)
                        t.wait_ge(V3s.h, v3s_b + h + 1)
                        for i in range(2):
                            ops = [(simb[:, hp * 384 + i * 128: hp * 384 + (i + 1) * 128],
                                    vsb[:, vb + 0 * D + h * DH: vb + 0 * D + (h + 1) * DH])]
                            if i == 1:
                                ops.append((simb[:, hp * 384 + 2 * 128: hp * 384 + 3 * 128],
                                            vsb[:, vb + 1 * D + h * DH: vb + 1 * D + (h + 1) * DH]))
                            if c > 0:
                                for dk in range(2):
                                    blk = 2 * h + dk
                                    ops.append((qg[:, blk * C3 + i * 128: blk * C3 + (i + 1) * 128],
                                                Sbf[:, blk * DH:(blk + 1) * DH]))
                            for oi, (lh, rh) in enumerate(ops):
                                ins = t.matmul(ret_ps[i][hp][:], lh, rh,
                                               start=(oi == 0), stop=(oi == len(ops) - 1))
                            T3r.inc(ins)
                    t.wait_ge(P3p.h, p3_b + 1)
                    for h in range(H):
                        hp = h % 2
                        if h > 1:
                            t.wait_ge(V3u.h, v3u_b + h - 1)
                        elif c > 0:
                            t.wait_ge(V3u.h, v3u_b)
                        for dk in range(2):
                            for ts in range(2):
                                ins = t.matmul(upd_ps[hp][:, dk * DH:(dk + 1) * DH],
                                               ktg[:, ts * D + h * DH + dk * 128: ts * D + h * DH + (dk + 1) * 128],
                                               vsb[:, vb + ts * D + h * DH: vb + ts * D + (h + 1) * DH],
                                               start=(ts == 0), stop=(ts == 1))
                        T3u.inc(ins)
                    t3s_e.append(T3s.n)
                    t3r_e.append(T3r.n)
                    t3u_e.append(T3u.n)
                block.tensor(_te)

                def _ve(v, c=c):
                    v.wait_ge(L.h, l3_e[c])
                    if c > 0:
                        v.wait_ge(V3u.h, v3u_e[c - 1])
                        v.wait_ge(T3r.h, t3r_e[c - 1])
                        V3p.inc(v.tensor_copy(Sbf[:], Ssb[:]))
                        V3p.inc(v.tensor_tensor(qg[:], qsb[:, qb:qb + KCC3], Gt[:], ALU.mult))
                    for h in range(H):
                        hp = h % 2
                        v.wait_ge(T3s.h, t3s_b + h + 1)
                        if h > 1:
                            v.wait_ge(T3r.h, t3r_b + 2 * (h - 1))
                        elif c > 0:
                            v.wait_ge(T3r.h, t3r_b)
                        ins = v.tensor_tensor(simb[:, hp * 384:(hp + 1) * 384], simc[hp][:],
                                              Mct[:, h * 384:(h + 1) * 384], ALU.mult)
                        V3s.inc(ins)
                    for h in range(H):
                        hp = h % 2
                        v.wait_ge(T3u.h, t3u_b + h + 1)
                        if c == 0:
                            ins = v.tensor_copy(Ssb[:, 2 * h * DH:(2 * h + 2) * DH], upd_ps[hp][:])
                        else:
                            ins = v.scalar_tensor_tensor(Ssb[:, 2 * h * DH:(2 * h + 2) * DH],
                                                         Ssb[:, 2 * h * DH:(2 * h + 2) * DH],
                                                         gC_list[h], upd_ps[hp][:],
                                                         ALU.mult, ALU.add)
                        V3u.inc(ins)
                    v3s_e.append(V3s.n)
                    v3u_e.append(V3u.n)
                block.vector(_ve)

                def _sc(s, c=c):
                    for h in range(H):
                        hp = h % 2
                        for i in range(2):
                            s.wait_ge(T3r.h, t3r_b + 2 * h + i + 1)
                            if c >= 2 and h == 0 and i == 0:
                                s.wait_ge(P3g.h, p3g_e[c - 2])
                            A3r.inc(s.copy(retb[:, rb + i * D + h * DH: rb + i * D + (h + 1) * DH],
                                           ret_ps[i][hp][:]))
                    a3r_e.append(A3r.n)
                block.scalar(_sc)

                if c >= 1:
                    P3_emit_gn(c - 1)

            P3_emit_gn(NCH3 - 1)

        with ExitStack() as ph:
            sb4 = lambda name, shape, dtype: ph.enter_context(nc.sbuf_tensor("P4g" + name, shape, dtype))
            wgb = sb4("wgb", [128, KC * D], BF16)
            bgt = sb4("bgt", [128, KC], F32)
            silb = sb4("silb", [128, 2 * C1], BF16)
            gnwt = sb4("gnwt", [128, KC], F32)
            gnbt = sb4("gnbt", [128, KC], F32)
            xnTs = sb4("xnTs", [128, 2 * KC * C1], BF16)
            rtf = sb4("rtf", [128, 2 * KC * C1], BF16)
            rta = sb4("rta", [128, 2 * C1], BF16)
            gateb = sb4("gateb", [128, 2 * KC * C1], BF16)
            gps_ = [ph.enter_context(psum(f"gps{i}", [128, C1], F32)) for i in range(2)]

            T4 = sem("T4")
            A4 = sem("A4")
            V4 = sem("V4")
            V4s = sem("V4s")
            P4 = sem("P4")
            SV4 = sem("SV4")

            def _g1(g):
                guard(g)
                g.dma_start(out=wgb.rearrange("p (kc m) -> p kc m", kc=KC)[:, 0:8],
                            in_=wg16_d.rearrange("(kc p) m -> p kc m", p=128)[:, 0:8]).then_inc(GQ.h, 16)
                GQ.n += 16
            block.gpsimd(_g1)

            def _sy0(s):
                guard(s)
                s.dma_start(out=wgb.rearrange("p (kc m) -> p kc m", kc=KC)[:, 8:16],
                            in_=wg16_d.rearrange("(kc p) m -> p kc m", p=128)[:, 8:16]).then_inc(L.h, 16)
                L.n += 16
            block.sync(_sy0)

            def _s1(s):
                guard(s)
                s.dma_start(out=bgt[:], in_=bg_t).then_inc(SS.h, 16)
                s.dma_start(out=gnwt[:], in_=gnw_t).then_inc(SS.h, 16)
                s.dma_start(out=gnbt[:], in_=gnb_t).then_inc(SS.h, 16)
                SS.n += 48
            block.scalar(_s1)

            gate_store = gate_d.rearrange("(kc p) t -> p kc t", p=128)
            KCC = KC * C1
            t4_e, l4_e, r4_e, v4_e, p4_e, ss4_e = [], [], [], [], [], []
            for c in range(NCH1):
                pp_ = c % 2
                xsl = slice(pp_ * KCC, (pp_ + 1) * KCC)
                def _sck(s, c=c):
                    if c == 0:
                        guard(s)
                    else:
                        if c >= 2:
                            s.wait_ge(T4.h, t4_e[c - 2])
                            s.wait_ge(V4s.h, v4_e[c - 2])
                    s.dma_start(out=xnTs[:, pp_ * KCC:(pp_ + 1) * KCC].rearrange("p (kc t) -> p kc t", kc=KC),
                                in_=xnT_d.rearrange("(kc p) t -> p kc t", p=128)[:, :, c * C1:(c + 1) * C1]).then_inc(L.h, 16)
                    L.n += 16
                    l4_e.append(L.n)
                    for mm in range(KC):
                        s.dma_start(out=rtf[:, pp_ * KCC + mm * C1: pp_ * KCC + (mm + 1) * C1],
                                    in_=ret_d[c * C1:(c + 1) * C1, mm * 128:(mm + 1) * 128],
                                    transpose=True).then_inc(SS.h, 16)
                    SS.n += 16 * KC
                    r4_e.append(SS.n)
                block.sync(_sck)

                t4_b, a4_b, v4_b, p4_b, v4s_b = T4.n, A4.n, V4.n, P4.n, V4s.n

                def _te(t, c=c):
                    t.wait_ge(L.h, l4_e[c])
                    if c == 0:
                        t.wait_ge(GQ.h, GQ.n)
                        t.wait_ge(SS.h, SS.n)
                    else:
                        t.wait_ge(A4.h, a4_b)
                    for mm in range(KC):
                        if mm >= 2:
                            t.wait_ge(A4.h, a4_b + mm - 1)
                        for kc in range(KC):
                            ins = t.matmul(gps_[mm % 2][:],
                                           wgb[:, kc * D + mm * 128: kc * D + (mm + 1) * 128],
                                           xnTs[:, pp_ * KCC + kc * C1: pp_ * KCC + (kc + 1) * C1],
                                           start=(kc == 0), stop=(kc == KC - 1))
                        T4.inc(ins)
                    t4_e.append(T4.n)
                block.tensor(_te)

                def _sc(s, c=c):
                    if c == 0:
                        s.wait_ge(SS.h, SS.n)
                    else:
                        s.wait_ge(P4.h, p4_b)
                    for mm in range(KC):
                        pm = mm % 2
                        s.wait_ge(T4.h, t4_b + mm + 1)
                        if mm >= 2:
                            s.wait_ge(P4.h, p4_b + mm - 1)
                        ins = s.activation(silb[:, pm * C1:(pm + 1) * C1], gps_[mm % 2][:],
                                           AF.Silu, bias=bgt[:, mm:mm + 1], scale=1.0)
                        A4.inc(ins)
                block.scalar(_sc)

                def _ve(v, c=c):
                    v.wait_ge(SS.h, r4_e[c])
                    if c > 0:
                        v.wait_ge(P4.h, p4_b)
                    for mm in range(KC):
                        pm = mm % 2
                        if mm >= 2:
                            v.wait_ge(P4.h, p4_b + mm - 1)
                        ins = v.tensor_scalar(rta[:, pm * C1:(pm + 1) * C1],
                                              rtf[:, pp_ * KCC + mm * C1: pp_ * KCC + (mm + 1) * C1],
                                              gnwt[:, mm:mm + 1], gnbt[:, mm:mm + 1],
                                              ALU.mult, ALU.add)
                        V4s.inc(ins)
                    v4_e.append(V4s.n)
                block.vector(_ve)

                def _gp(g, c=c):
                    if c >= 2:
                        g.wait_ge(SS.h, ss4_e[c - 2])
                    for mm in range(KC):
                        pm = mm % 2
                        g.wait_ge(V4s.h, v4s_b + mm + 1)
                        g.wait_ge(A4.h, a4_b + mm + 1)
                        ins = g.tensor_tensor(gateb[:, pp_ * KCC + mm * C1: pp_ * KCC + (mm + 1) * C1],
                                              silb[:, pm * C1:(pm + 1) * C1],
                                              rta[:, pm * C1:(pm + 1) * C1], ALU.mult)
                        P4.inc(ins)
                block.gpsimd(_gp)


                def _sc(s, c=c):
                    s.wait_ge(P4.h, p4_b + KC)
                    s.dma_start(out=gate_store[:, :, c * C1:(c + 1) * C1],
                                in_=gateb[:, xsl].rearrange("p (kc t) -> p kc t", kc=KC)).then_inc(SS.h, 16)
                    SS.n += 16
                    ss4_e.append(SS.n)
                block.scalar(_sc)

        with ExitStack() as ph:
            sb5 = lambda name, shape, dtype: ph.enter_context(nc.sbuf_tensor("P4o" + name, shape, dtype))
            wob = sb5("wob", [128, KC * D], BF16)
            bot = sb5("bot", [128, D], F32)
            gateb = sb5("gateb", [128, 2 * KC * C1], BF16)
            xb = sb5("xb", [128, 2 * D], F32)
            fnb = sb5("fnb", [128, 2 * D], F16)
            otmp = sb5("otmp", [128, 2 * D], F32)
            otm2 = sb5("otm2", [128, 2 * D], F32)
            osb = sb5("osb", [128, 2 * D], F32)
            ops_ = [ph.enter_context(psum(f"ops{i}", [128, D], F32)) for i in range(2)]

            T5 = sem("T5")
            V5 = sem("V5")
            V5b = sem("V5b")
            P5 = sem("P5")
            L5 = sem("L5")
            SO = sem("SO")

            def _g1(g):
                guard(g)
                g.dma_start(out=wob.rearrange("p (kc m) -> p kc m", kc=KC)[:, 0:8],
                            in_=wo16_d.rearrange("(kc p) m -> p kc m", p=128)[:, 0:8]).then_inc(GQ.h, 16)
                GQ.n += 16
            block.gpsimd(_g1)

            def _sy0(s):
                guard(s)
                s.dma_start(out=wob.rearrange("p (kc m) -> p kc m", kc=KC)[:, 8:16],
                            in_=wo16_d.rearrange("(kc p) m -> p kc m", p=128)[:, 8:16]).then_inc(L.h, 16)
                L.n += 16
            block.sync(_sy0)

            def _s1(s):
                guard(s)
                s.dma_start(out=bot[:], in_=bo_bc).then_inc(SS.h, 16)
                SS.n += 16
            block.scalar(_s1)

            gate_src = gate_d.rearrange("(kc p) t -> p kc t", p=128)
            x_src = x_d.rearrange("(r p) d -> r p d", p=128)
            f_src = ffn_d.rearrange("(r p) d -> r p d", p=128)
            out_store = out_d.rearrange("(r p) d -> r p d", p=128)
            KCC = KC * C1
            t5_e, v5_e, l5_e = [], [], []
            for c in range(NCH1):
                pp_ = c % 2
                xsl = slice(pp_ * KCC, (pp_ + 1) * KCC)
                def _sy(s, c=c):
                    if c >= 2:
                        s.wait_ge(T5.h, t5_e[c - 2])
                    s.dma_start(out=gateb[:, xsl].rearrange("p (kc t) -> p kc t", kc=KC),
                                in_=gate_src[:, :, c * C1:(c + 1) * C1]).then_inc(L.h, 16)
                    L.n += 16
                    l5_e.append(L.n)
                    for ts in range(4):
                        tsp = ts % 2
                        gi = 4 * c + ts
                        if gi >= 2:
                            s.wait_ge(V5.h, max(0, gi - 1))
                            s.wait_ge(V5b.h, max(0, gi - 1))
                        s.dma_start(out=xb[:, tsp * D:(tsp + 1) * D], in_=x_src[gi]).then_inc(L5.h, 16)
                        s.dma_start(out=fnb[:, tsp * D:(tsp + 1) * D], in_=f_src[gi]).then_inc(L5.h, 16)
                        L5.n += 32
                block.sync(_sy)

                t5_b, v5_b, v5b_b, p5_b, so_b = T5.n, V5.n, V5b.n, P5.n, SO.n

                def _te(t, c=c):
                    t.wait_ge(L.h, l5_e[c])
                    if c == 0:
                        t.wait_ge(GQ.h, GQ.n)
                        t.wait_ge(SS.h, SS.n)
                    else:
                        t.wait_ge(V5.h, v5_b)
                    for ts in range(4):
                        if ts >= 2:
                            t.wait_ge(V5.h, v5_b + ts - 1)
                        for nf in range(4):
                            for kc in range(KC):
                                ins = t.matmul(ops_[ts % 2][:, nf * 512:(nf + 1) * 512],
                                               gateb[:, pp_ * KCC + kc * C1 + ts * 128: pp_ * KCC + kc * C1 + (ts + 1) * 128],
                                               wob[:, kc * D + nf * 512: kc * D + (nf + 1) * 512],
                                               start=(kc == 0), stop=(kc == KC - 1))
                        T5.inc(ins)
                    t5_e.append(T5.n)
                block.tensor(_te)

                def _ve(v, c=c):
                    if c > 0:
                        v.wait_ge(P5.h, p5_b)
                    for ts in range(4):
                        tsp = ts % 2
                        v.wait_ge(T5.h, t5_b + ts + 1)
                        v.wait_ge(L5.h, 32 * (4 * c + ts) + 32)
                        if ts >= 2:
                            v.wait_ge(P5.h, p5_b + ts - 1)
                        ins = v.scalar_tensor_tensor(otmp[:, tsp * D:(tsp + 1) * D],
                                                     ops_[ts % 2][:], 1.0, xb[:, tsp * D:(tsp + 1) * D],
                                                     ALU.mult, ALU.add)
                        V5.inc(ins)
                        ins = v.tensor_tensor(otm2[:, tsp * D:(tsp + 1) * D],
                                              otmp[:, tsp * D:(tsp + 1) * D],
                                              fnb[:, tsp * D:(tsp + 1) * D], ALU.add)
                        V5b.inc(ins)
                block.vector(_ve)

                def _gp(g, c=c):
                    if c == 0:
                        g.wait_ge(SS.h, SS.n)
                    else:
                        g.wait_ge(SO.h, so_b)
                    for ts in range(4):
                        tsp = ts % 2
                        g.wait_ge(V5b.h, v5b_b + ts + 1)
                        if ts >= 2:
                            g.wait_ge(SO.h, so_b + 16 * (ts - 1))
                        ins = g.tensor_tensor(osb[:, tsp * D:(tsp + 1) * D],
                                              otm2[:, tsp * D:(tsp + 1) * D], bot[:], ALU.add)
                        P5.inc(ins)
                block.gpsimd(_gp)

                def _sc(s, c=c):
                    for ts in range(4):
                        s.wait_ge(P5.h, p5_b + ts + 1)
                        s.dma_start(out=out_store[4 * c + ts],
                                    in_=osb[:, (ts % 2) * D:(ts % 2 + 1) * D]).then_inc(SO.h, 16)
                        SO.n += 16
                block.scalar(_sc)

            if debug:
                DBG = sem("DBG")

                def _dbg(s):
                    guard(s)
                    for nm, srcap in [("xn", xn_d), ("xq", xq_d), ("ffn", ffn_d), ("q", q_d),
                                      ("k", k_d), ("v", v_d), ("ret", ret_d), ("gate", gate_d)]:
                        s.dma_start(out=dbg[nm], in_=srcap).then_inc(DBG.h, 16)
                        DBG.n += 16
                    s.wait_ge(DBG.h, DBG.n)
                block.sync(_dbg)

            def _fin(s):
                s.wait_ge(SO.h, SO.n)
            block.scalar(_fin)

    return nc


def kernel(L0_attn_x, L0_attn_w_q, L0_attn_b_q, L0_attn_w_k, L0_attn_b_k,
           L0_attn_w_v, L0_attn_b_v, L0_attn_w_g, L0_attn_b_g,
           L0_attn_w_o, L0_attn_b_o, L0_attn_gn_weight, L0_attn_gn_bias,
           L0_ffn_weight, L0_ffn_w_scale, L0_ffn_bias):
    x = np.ascontiguousarray(np.asarray(L0_attn_x, dtype=np.float32))
    consts = _host_constants()

    def col16(v):
        return np.ascontiguousarray(np.asarray(v, np.float32).reshape(KC, 128).T)

    def bc(v):
        return np.ascontiguousarray(np.broadcast_to(np.asarray(v, np.float32).reshape(1, D), (128, D)))

    shared = {
        "w_q": np.ascontiguousarray(np.asarray(L0_attn_w_q, np.float32)),
        "w_k": np.ascontiguousarray(np.asarray(L0_attn_w_k, np.float32)),
        "w_v": np.ascontiguousarray(np.asarray(L0_attn_w_v, np.float32)),
        "w_g": np.ascontiguousarray(np.asarray(L0_attn_w_g, np.float32)),
        "w_o": np.ascontiguousarray(np.asarray(L0_attn_w_o, np.float32)),
        "ffw": np.ascontiguousarray(np.asarray(L0_ffn_weight, np.int8)),
        "bq_t": col16(L0_attn_b_q), "bk_t": col16(L0_attn_b_k), "bg_t": col16(L0_attn_b_g),
        "gnw_t": col16(L0_attn_gn_weight), "gnb_t": col16(L0_attn_gn_bias),
        "bv_bc": bc(L0_attn_b_v), "bo_bc": bc(L0_attn_b_o),
        "ws_bc": bc(np.asarray(L0_ffn_w_scale, np.float32)[:, 0]),
        "fb_bc": bc(L0_ffn_bias),
        "cosT": consts["cosT"], "sinT": consts["sinT"],
        "cosK": consts["cosK"], "sinK": consts["sinK"],
        "Mcomb": consts["Mcomb"], "Gfull": consts["Gfull"], "Gam": consts["Gam"],
        "RT": consts["RT"], "idf32": consts["idf32"], "idbf": consts["idbf"],
        "epsln": consts["epsln"],
    }
    nc = build_nc()
    in_maps = [dict(shared, x=np.ascontiguousarray(x[b])) for b in range(8)]
    res = run_bass_kernel_spmd(nc, in_maps, core_ids=list(range(8)))
    out = np.stack([np.asarray(res.results[b]["out"]) for b in range(8)], axis=0)
    return out
